# version 04
# baseline (speedup 1.0000x reference)
import numpy as np
from contextlib import ExitStack

import concourse.bacc as bacc
import concourse.bass as bass
import concourse.mybir as mybir
from concourse.bass_utils import run_bass_kernel_spmd

F32 = mybir.dt.float32
BF16 = mybir.dt.bfloat16
I32 = mybir.dt.int32
AF = mybir.ActivationFunctionType
ALU = mybir.AluOpType
AX = mybir.AxisListType

B, T, C = 4, 4096, 1024
H, DH, DHE = 16, 64, 32
NPAIR = 8
E, FF = 8, 2048
TOWN = 2048
NDC = C // 128
NTC = T // 512
NKT = T // 128
GARB = 8 * TOWN

ENGINES = ("tensor", "vector", "scalar", "gpsimd", "sync")


class Sched:

    def __init__(self, sems, cnt):
        self.sems = sems
        self.cnt = cnt
        self.ops = {e: [] for e in ENGINES}

    def emit(self, eng, fn, waits=(), inc=None, inck=1):
        self.ops[eng].append((fn, tuple(waits), inc, inck))
        if inc is not None:
            self.cnt[inc] += inck

    def v(self, si):
        return self.cnt[si]

    def run(self, nc):
        sems = self.sems
        with nc.Block() as blk:
            for eng in ENGINES:
                lst = self.ops[eng]
                if not lst:
                    continue

                def make(lst):
                    def body(h):
                        for (fn, waits, inc, inck) in lst:
                            for (si, val) in waits:
                                if val > 0:
                                    h.wait_ge(sems[si], val)
                            ins = fn(h)
                            if inc is not None:
                                ins.then_inc(sems[inc], inck)
                    return body
                getattr(blk, eng)(make(lst))


def _rope_tables():
    inv = 1.0 / (10000.0 ** (np.arange(0, DHE, 2, dtype=np.float64) / DHE))
    freqs = np.arange(T, dtype=np.float64)[:, None] * inv[None, :]
    emb = np.concatenate([freqs, freqs], axis=1)
    cos = np.cos(emb).astype(np.float32).T
    sin = np.sin(emb).astype(np.float32).T
    cp = np.ones((128, T), np.float32)
    sp = np.zeros((128, T), np.float32)
    cp[32:64] = cos
    cp[96:128] = cos
    sp[32:64] = sin
    sp[96:128] = sin
    return cp, sp


def _perm_matrix():
    P = np.zeros((DHE, DHE), np.float32)
    for i in range(DHE // 2):
        P[2 * i + 1, 2 * i] = -1.0
        P[2 * i, 2 * i + 1] = 1.0
    return P


_CACHE = {}
_DBG_SCRATCH = False


def build_nc():
    if "nc" in _CACHE:
        return _CACHE["nc"]
    nc = bacc.Bacc("TRN2", target_bir_lowering=False, detect_race_conditions=False)

    dt = lambda n, s, d, k: nc.dram_tensor(n, s, d, kind=k).ap()
    d_xT = dt("xT", [128, NDC, T], F32, "ExternalInput")
    d_wqdT = dt("wqdT", [128, NDC, C], F32, "ExternalInput")
    d_wkdT = dt("wkdT", [128, NDC, C], F32, "ExternalInput")
    d_wvdT = dt("wvdT", [128, NDC, C], F32, "ExternalInput")
    d_upqc = dt("upqc", [64, 32], F32, "ExternalInput")
    d_upqe = dt("upqe", [64, 32], F32, "ExternalInput")
    d_upqeT = dt("upqeT", [32, 64], F32, "ExternalInput")
    d_upkc = dt("upkc", [64, 32], F32, "ExternalInput")
    d_upke = dt("upke", [64, 32], F32, "ExternalInput")
    d_upkeT = dt("upkeT", [32, 64], F32, "ExternalInput")
    d_upv = dt("upv", [64, 64], F32, "ExternalInput")
    d_P = dt("Pm", [32, 32], F32, "ExternalInput")
    d_wc = dt("wc", [128, NDC, C], F32, "ExternalInput")
    d_rwT = dt("rwT", [128, NDC, 8], F32, "ExternalInput")
    d_rb = dt("rb", [8, 1], F32, "ExternalInput")
    d_temp = dt("temp", [1, 1], F32, "ExternalInput")
    d_wgp = dt("wgp", [E, 16, 128, 2, NDC, 128], F32, "ExternalInput")
    d_wo = dt("wo", [E, 2, 8, 128, 8, 128], F32, "ExternalInput")
    d_cos = dt("cosp", [128, T], F32, "ExternalInput")
    d_sin = dt("sinp", [128, T], F32, "ExternalInput")
    d_tri = dt("tri2", [128, 256], F32, "ExternalInput")
    d_ident = dt("ident", [128, 128], F32, "ExternalInput")
    d_sel2 = dt("sel2", [2, 128], F32, "ExternalInput")
    d_offs = dt("offs", [NTC, 8], I32, "ExternalInput")
    d_soff = dt("soff", [1, 1], I32, "ExternalInput")

    d_yout = dt("yout", [128, 8, TOWN], F32, "ExternalOutput")

    _knd = "ExternalOutput" if _DBG_SCRATCH else "Internal"
    d_qh = dt("qh", [NPAIR, 128, T], BF16, _knd)
    d_kh = dt("kh", [NPAIR, 128, T], BF16, _knd)
    d_vh = dt("vh", [128, NKT, H, 65], BF16, _knd)
    d_yh = dt("yh", [NPAIR, 128, T], BF16, _knd)
    d_wgpb = dt("wgpb", [E, 16, 128, 2, NDC, 128], BF16, "Internal")
    d_wob = dt("wob", [E, 2, 8, 128, 8, 128], BF16, "Internal")
    if _DBG_SCRATCH:
        d_dbg1 = dt("dbg1", [128, 512], F32, "ExternalOutput")
        d_dbg2 = dt("dbg2", [8, T], F32, "ExternalOutput")
        d_dbg3 = dt("dbg3", [8, TOWN], F32, "ExternalOutput")
        d_dbg4 = dt("dbg4", [128, 8 * TOWN + 512], F32, "ExternalOutput")
        d_dbg5 = dt("dbg5", [128, 256], F32, "ExternalOutput")
        d_dbg6 = dt("dbg6", [8, T], F32, "ExternalOutput")
        d_dbg7 = dt("dbg7", [128, 256], F32, "ExternalOutput")
        d_dbg8 = dt("dbg8", [128, 256], F32, "ExternalOutput")
        d_dbg9 = dt("dbg9", [128, 8 * TOWN + 512], F32, "ExternalOutput")
        d_dbg10 = dt("dbg10", [128, NDC, TOWN], BF16, "ExternalOutput")
        d_dbg11 = dt("dbg11", [128, 8, TOWN], BF16, "ExternalOutput")

    es = ExitStack()
    nc._keepalive = es
    sems = [es.enter_context(nc.semaphore(f"sm{i}")) for i in range(64)]
    cnt = {i: 0 for i in range(64)}
    _ns = [0]

    def S():
        i = _ns[0]
        _ns[0] += 1
        return i

    sb = lambda st, name, shape, d: st.enter_context(nc.sbuf_tensor(name, shape, d))

    ident_f = sb(es, "ident_f", [128, 128], F32)
    ones_b = sb(es, "ones_b", [128, 1], BF16)
    ones1f = sb(es, "ones1f", [1, 128], F32)
    sel2 = sb(es, "sel2s", [2, 128], F32)
    tri2 = sb(es, "tri2s", [128, 256], BF16)
    l_all = sb(es, "l_all", [128, 512], F32)
    rl_all = sb(es, "rl_all", [128, 512], F32)
    rb_sb = sb(es, "rb_sb", [8, 1], F32)
    rtemp = sb(es, "rtemp", [8, 1], F32)
    t_sb = sb(es, "t_sb", [1, 1], F32)
    offs_sb = sb(es, "offs_sb", [NTC, 8], I32)
    soff_sb = sb(es, "soff_sb", [1, 1], I32)

    st_qk = ExitStack()
    xT = sb(st_qk, "xT_sb", [128, NDC, T], BF16)
    cosp = sb(st_qk, "cosp_sb", [128, T], BF16)
    sinp = sb(st_qk, "sinp_sb", [128, T], BF16)
    wq_eff = sb(st_qk, "wq_eff", [128, NDC, C], BF16)
    wq_rot = sb(st_qk, "wq_rot", [128, NDC, C], BF16)
    wk_eff = sb(st_qk, "wk_eff", [128, NDC, C], BF16)
    wk_rot = sb(st_qk, "wk_rot", [128, NDC, C], BF16)
    st_fold = ExitStack()
    wdT2 = sb(st_fold, "wdT2", [128, 2 * NDC, C], BF16)
    upfq = sb(st_fold, "upfq", [128, 128], BF16)
    upfk = sb(st_fold, "upfk", [128, 128], BF16)
    upeT2 = sb(st_fold, "upeT2", [32, 2, 64], BF16)
    P_sb = sb(st_fold, "P_sb", [32, 32], BF16)

    sc = Sched(sems, cnt)
    SL, SLg, SLm = S(), S(), S()
    for dst, src in [(ident_f, d_ident), (sel2, d_sel2), (rb_sb, d_rb), (t_sb, d_temp),
                     (offs_sb, d_offs), (soff_sb, d_soff)]:
        sc.emit("sync", lambda h, a=dst, b=src: h.dma_start(out=a[:, :], in_=b[:, :]),
                inc=SL, inck=16)
    for dst, src in [(tri2, d_tri), (cosp, d_cos), (sinp, d_sin), (P_sb, d_P)]:
        sc.emit("gpsimd", lambda h, a=dst, b=src: h.dma_start(out=a[:, :], in_=b[:, :]),
                inc=SLg, inck=16)
    for (dst, c0, c1, src) in [(upfq, 0, 32, d_upqc), (upfq, 32, 64, d_upqe),
                               (upfk, 0, 32, d_upkc), (upfk, 32, 64, d_upke)]:
        sc.emit("gpsimd", lambda h, a=dst, x0=c0, x1=c1, b=src: h.dma_start(
            out=a[0:64, x0:x1], in_=b[:, :]), inc=SLg, inck=16)
    sc.emit("gpsimd", lambda h: h.dma_start(out=upeT2[:, 0, :], in_=d_upqeT[:, :]),
            inc=SLg, inck=16)
    sc.emit("gpsimd", lambda h: h.dma_start(out=upeT2[:, 1, :], in_=d_upkeT[:, :]),
            inc=SLg, inck=16)
    sc.emit("gpsimd", lambda h: h.dma_start(out=xT[:, :, :], in_=d_xT[:, :, :]),
            inc=SLg, inck=16)
    sc.emit("gpsimd", lambda h: h.dma_start(out=wdT2[:, 0:NDC, :], in_=d_wqdT[:, :, :]),
            inc=SLg, inck=16)
    sc.emit("gpsimd", lambda h: h.dma_start(out=wdT2[:, NDC:2 * NDC, :], in_=d_wkdT[:, :, :]),
            inc=SLg, inck=16)
    sc.emit("gpsimd", lambda h: h.memset(ones_b[:, :], 1.0))
    sc.emit("gpsimd", lambda h: h.memset(ones1f[:, :], 1.0))
    sc.emit("gpsimd", lambda h: h.memset(upfq[:, 64:96], 0.0))
    sc.emit("gpsimd", lambda h: h.memset(upfk[:, 64:96], 0.0), inc=SLm, inck=1)
    sc.run(nc)
    LD_ALL = sc.v(SL)
    LDG_ALL = sc.v(SLg)

    sc = Sched(sems, cnt)
    SM0, SV0, SD0 = S(), S(), S()
    with nc.psum_tensor("psR", [64, 2, 512], F32) as psR:
        for kk in range(2):
            sc.emit("tensor",
                    lambda h, k=kk: h.matmul(psR[:, k, 0:32], upeT2[:, k, :], P_sb[:, :],
                                             start=True, stop=True),
                    waits=[(SL, LD_ALL), (SLg, LDG_ALL), (SLm, 1)], inc=SM0)
        sc.emit("vector", lambda h: h.tensor_copy(upfq[0:64, 96:128], psR[:, 0, 0:32]),
                waits=[(SM0, 1)], inc=SV0)
        sc.emit("vector", lambda h: h.tensor_copy(upfk[0:64, 96:128], psR[:, 1, 0:32]),
                waits=[(SM0, 2)], inc=SV0)
        sc.emit("vector", lambda h: h.tensor_scalar_mul(upfq[0:64, :], upfq[0:64, :], 0.125),
                inc=SV0)
        sc.emit("gpsimd", lambda h: h.dma_start(out=upfq[64:128, :], in_=upfq[0:64, :]),
                waits=[(SV0, sc.v(SV0))], inc=SD0, inck=16)
        sc.emit("gpsimd", lambda h: h.dma_start(out=upfk[64:128, :], in_=upfk[0:64, :]),
                inc=SD0, inck=16)
        sc.run(nc)

    sc = Sched(sems, cnt)
    SM2, SV2 = S(), S()
    with nc.psum_tensor("psF", [128, 2, 512], F32) as psF:
        mmn = 0
        for kind in range(2):
            upf = upfq if kind == 0 else upfk
            eff = wq_eff if kind == 0 else wk_eff
            rot = wq_rot if kind == 0 else wk_rot
            for h0 in range(H):
                for dc0 in range(NDC):
                    w = [(SL, LD_ALL), (SLg, LDG_ALL), (SLm, 1), (SD0, 32)] if mmn == 0 else []
                    if mmn >= 2:
                        w.append((SV2, 2 * (mmn - 1)))
                    lhs = wdT2[64 * (h0 % 2):64 * (h0 % 2) + 64,
                               kind * NDC + h0 // 2, 128 * dc0:128 * dc0 + 128]
                    r0 = 64 * (h0 % 2)
                    sc.emit("tensor",
                            lambda h, l=lhs, u=upf, g=mmn % 2, r=r0: h.matmul(
                                psF[:, g, 0:128], l, u[r:r + 64, :], start=True, stop=True),
                            waits=w, inc=SM2)
                    mm_done = sc.v(SM2)
                    sc.emit("vector",
                            lambda h, e=eff, d=dc0, hh=h0, g=mmn % 2: h.tensor_copy(
                                e[:, d, 64 * hh:64 * hh + 64], psF[:, g, 0:64]),
                            waits=[(SM2, mm_done)], inc=SV2)
                    sc.emit("vector",
                            lambda h, r=rot, d=dc0, hh=h0, g=mmn % 2: h.tensor_copy(
                                r[:, d, 64 * hh:64 * hh + 64], psF[:, g, 64:128]),
                            inc=SV2)
                    mmn += 1
        sc.run(nc)
    st_fold.close()

    st1 = ExitStack()
    t1b = sb(st1, "t1b", [128, 512], F32)
    t2b = sb(st1, "t2b", [128, 512], F32)
    qstage = sb(st1, "qstage", [128, 2, 512], BF16)

    sc = Sched(sems, cnt)
    P1M, P1V, P1D = S(), S(), S()
    with nc.psum_tensor("psP", [128, 2, 512], F32) as psP, \
         nc.psum_tensor("psO", [128, 2, 512], F32) as psO:
        it = 0
        for kind in range(2):
            eff = wq_eff if kind == 0 else wk_eff
            rot = wq_rot if kind == 0 else wk_rot
            dh = d_qh if kind == 0 else d_kh
            for p in range(NPAIR):
                for tc in range(NTC):
                    g = it % 2
                    w = [(SL, LD_ALL), (SLg, LDG_ALL)] if it == 0 else []
                    if it >= 2:
                        w.append((P1V, 3 * (it - 1)))

                    def mms(h, e=eff, r=rot, p0=p, t0=tc, g0=g):
                        for dc0 in range(NDC):
                            h.matmul(psP[:, g0, :], e[:, dc0, 128 * p0:128 * p0 + 128],
                                     xT[:, dc0, 512 * t0:512 * t0 + 512],
                                     start=(dc0 == 0), stop=(dc0 == NDC - 1))
                        for dc0 in range(NDC):
                            ins = h.matmul(psO[:, g0, :], r[:, dc0, 128 * p0:128 * p0 + 128],
                                           xT[:, dc0, 512 * t0:512 * t0 + 512],
                                           start=(dc0 == 0), stop=(dc0 == NDC - 1))
                        return ins
                    sc.emit("tensor", mms, waits=w, inc=P1M)
                    mmv = sc.v(P1M)
                    wv = [(P1M, mmv)]
                    if it >= 2:
                        wv.append((P1D, 16 * (it - 1)))
                    sc.emit("vector",
                            lambda h, t0=tc, g0=g: h.tensor_tensor(
                                t1b[:, :], psP[:, g0, :],
                                cosp[:, 512 * t0:512 * t0 + 512], ALU.mult),
                            waits=wv, inc=P1V)
                    sc.emit("vector",
                            lambda h, t0=tc, g0=g: h.tensor_tensor(
                                t2b[:, :], psO[:, g0, :],
                                sinp[:, 512 * t0:512 * t0 + 512], ALU.mult),
                            inc=P1V)
                    sc.emit("vector",
                            lambda h, g0=g: h.tensor_add(qstage[:, g0, :], t1b[:, :], t2b[:, :]),
                            inc=P1V)
                    vv = sc.v(P1V)
                    sc.emit("sync",
                            lambda h, d=dh, p0=p, t0=tc, g0=g: h.dma_start(
                                out=d[p0, :, 512 * t0:512 * t0 + 512], in_=qstage[:, g0, :]),
                            waits=[(P1V, vv)], inc=P1D, inck=16)
                    it += 1
        sc.run(nc)
    P1D_ALL = cnt[P1D]
    st1.close()

    st2 = ExitStack()
    wvdT = sb(st2, "wvdT_sb", [128, NDC, C], BF16)
    wv_eff = sb(st2, "wv_eff", [128, NDC, C], BF16)
    upv_sb = sb(st2, "upv_sb", [128, 64], BF16)
    vstage = sb(st2, "vstage", [128, 2, H, 65], BF16)

    sc = Sched(sems, cnt)
    P2L, P2M, P2V, P2D, P2Lm = S(), S(), S(), S(), S()
    with nc.psum_tensor("psVF", [128, 2, 512], F32) as psVF, \
         nc.psum_tensor("psV", [128, 2, 1024], F32) as psV:
        sc.emit("gpsimd", lambda h: h.dma_start(out=wvdT[:, :, :], in_=d_wvdT[:, :, :]),
                inc=P2L, inck=16)
        sc.emit("gpsimd", lambda h: h.dma_start(out=upv_sb[0:64, :], in_=d_upv[:, :]),
                inc=P2L, inck=16)
        sc.emit("gpsimd", lambda h: h.dma_start(out=upv_sb[64:128, :], in_=upv_sb[0:64, :]),
                waits=[(P2L, 32)], inc=P2L, inck=16)
        sc.emit("gpsimd", lambda h: h.memset(vstage[:, :, :, :], 1.0), inc=P2Lm)
        ld = sc.v(P2L)
        mmn = 0
        for h0 in range(H):
            for dc0 in range(NDC):
                w = [(P2L, ld), (P2Lm, 1)] if mmn == 0 else []
                if mmn >= 2:
                    w.append((P2V, mmn - 1))
                lhs = wvdT[64 * (h0 % 2):64 * (h0 % 2) + 64, h0 // 2, 128 * dc0:128 * dc0 + 128]
                r0 = 64 * (h0 % 2)
                sc.emit("tensor",
                        lambda h, l=lhs, g=mmn % 2, r=r0: h.matmul(
                            psVF[:, g, 0:64], l, upv_sb[r:r + 64, :], start=True, stop=True),
                        waits=w, inc=P2M)
                sc.emit("vector",
                        lambda h, d=dc0, hh=h0, g=mmn % 2: h.tensor_copy(
                            wv_eff[:, d, 64 * hh:64 * hh + 64], psVF[:, g, 0:64]),
                        waits=[(P2M, sc.v(P2M))], inc=P2V)
                mmn += 1
        vf_done = sc.v(P2V)
        for kt in range(NKT):
            g = kt % 2
            w = [(P2V, vf_done)] if kt < 2 else [(P2V, vf_done + kt - 1)]

            def vmms(h, k0=kt, g0=g):
                for dc0 in range(NDC):
                    for hh in range(2):
                        ins = h.matmul(psV[:, g0, 512 * hh:512 * hh + 512],
                                       xT[:, dc0, 128 * k0:128 * k0 + 128],
                                       wv_eff[:, dc0, 512 * hh:512 * hh + 512],
                                       start=(dc0 == 0), stop=(dc0 == NDC - 1))
                return ins
            sc.emit("tensor", vmms, waits=w, inc=P2M)
            wv = [(P2M, sc.v(P2M))]
            if kt >= 2:
                wv.append((P2D, 16 * (kt - 1)))
            sc.emit("vector",
                    lambda h, g0=g: h.tensor_copy(
                        vstage[:, g0, :, 0:64],
                        psV[:, g0, :].rearrange("p (h d) -> p h d", h=H)),
                    waits=wv, inc=P2V)
            sc.emit("sync",
                    lambda h, k0=kt, g0=g: h.dma_start(out=d_vh[:, k0, :, :],
                                                       in_=vstage[:, g0, :, :]),
                    waits=[(P2V, sc.v(P2V))], inc=P2D, inck=16)
        sc.run(nc)
    P2D_ALL = cnt[P2D]
    st2.close()
    st_qk.close()

    st3 = ExitStack()
    qbuf = sb(st3, "qbuf", [128, 2, T], BF16)
    cvt = sb(st3, "cvt", [128, 3, 2048], BF16)
    kbuf = sb(st3, "kbuf", [128, 2, T], BF16)
    vbuf = sb(st3, "vbuf", [128, 2, NKT, 2, 65], BF16)
    p2 = sb(st3, "p2", [128, 2, 1024], BF16)
    ystg = sb(st3, "ystg", [128, 2, 1024], BF16)
    lstg = sb(st3, "lstg", [128, 2, 1024], F32)

    sc = Sched(sems, cnt)
    A_LD, A_ST, A_EXP, A_MSK, A_AV, A_EV, A_YD = S(), S(), S(), S(), S(), S(), S()
    CV_L, CV_S = S(), S()
    cjobs = []
    for e0 in range(E):
        for fc in range(16):
            cjobs.append(("wgp", e0, fc))
        for hf in range(2):
            for d0 in range(NDC):
                cjobs.append(("wo", e0, hf, d0))
    for ci, job in enumerate(cjobs):
        g = ci % 3
        w = [(CV_S, 16 * (ci - 2))] if ci >= 3 else []
        if job[0] == "wgp":
            _, e0, fc = job
            sc.emit("gpsimd", lambda h, e=e0, f=fc, g0=g: h.dma_start(
                out=cvt[:, g0, :].rearrange("p (a b c) -> p a b c", a=2, b=NDC),
                in_=d_wgp[e, f, :, :, :, :]), waits=w, inc=CV_L, inck=16)
            sc.emit("sync", lambda h, e=e0, f=fc, g0=g: h.dma_start(
                out=d_wgpb[e, f, :, :, :, :],
                in_=cvt[:, g0, :].rearrange("p (a b c) -> p a b c", a=2, b=NDC)),
                waits=[(CV_L, sc.v(CV_L))], inc=CV_S, inck=16)
        else:
            _, e0, hf, d0 = job
            sc.emit("gpsimd", lambda h, e=e0, f=hf, d=d0, g0=g: h.dma_start(
                out=cvt[:, g0, 0:1024].rearrange("p (a b) -> p a b", a=NDC),
                in_=d_wo[e, f, d, :, :, :]), waits=w, inc=CV_L, inck=16)
            sc.emit("sync", lambda h, e=e0, f=hf, d=d0, g0=g: h.dma_start(
                out=d_wob[e, f, d, :, :, :],
                in_=cvt[:, g0, 0:1024].rearrange("p (a b) -> p a b", a=NDC)),
                waits=[(CV_L, sc.v(CV_L))], inc=CV_S, inck=16)
    CV_ALL = sc.v(CV_S)

    sched3 = []
    for p in range(NPAIR):
        for qc in range(NTC):
            for j in range(4 * qc + 4):
                c0 = max(0, 128 * j - 512 * qc)
                sched3.append((p, qc, j, c0, j == 4 * qc + 3, j >= 4 * qc))
    NJP = len(sched3) // NPAIR

    for p in range(NPAIR):
        w = []
        if p == 0:
            w = [(P1D, P1D_ALL), (P2D, P2D_ALL)]
        if p >= 2:
            w = [(A_AV, NJP * (p - 1))]
        g = p % 2
        sc.emit("sync", lambda h, p0=p, g0=g: h.dma_start(out=qbuf[:, g0, :], in_=d_qh[p0, :, :]),
                waits=w, inc=A_LD, inck=16)
        sc.emit("sync", lambda h, p0=p, g0=g: h.dma_start(out=kbuf[:, g0, :], in_=d_kh[p0, :, :]),
                inc=A_LD, inck=16)
        sc.emit("sync", lambda h, p0=p, g0=g: h.dma_start(
            out=vbuf[:, g0, :, :, :],
            in_=d_vh[:, :, 2 * p0:2 * p0 + 2, :]),
            inc=A_LD, inck=16)

    with nc.psum_tensor("psST", [128, 2, 1024], F32) as psST, \
         nc.psum_tensor("psOT", [128, 2, 1024], F32) as psOT:
        nj = 0
        kidx = 0
        pending_av = None
        for (p, qc, j, c0, last, diag) in sched3:
            g = p % 2
            jg = nj % 2
            w = [(A_LD, 48 * (p + 1))]
            if nj >= 2:
                w.append((A_EXP, nj - 1))

            def stm(h, p0=p, q0=qc, j0=j, cc=c0, g0=g, jg0=jg):
                for hh in range(2):
                    ins = h.matmul(
                        psST[:, jg0, 512 * hh + cc:512 * hh + 512],
                        kbuf[64 * hh:64 * hh + 64, g0, 128 * j0:128 * j0 + 128],
                        qbuf[64 * hh:64 * hh + 64, g0, 512 * q0 + cc:512 * q0 + 512],
                        start=True, stop=True)
                return ins
            sc.emit("tensor", stm, waits=w, inc=A_ST)
            st_done = sc.v(A_ST)

            if pending_av is not None:
                sc.emit("tensor", pending_av[0], waits=pending_av[1], inc=A_AV)
                pending_av = None

            w = [(A_ST, st_done)]
            if nj >= 2:
                w.append((A_AV, nj - 1))

            def expf(h, cc=c0, jg0=jg):
                iv = psST[:, jg0, :].rearrange("p (h q) -> p h q", h=2)[:, :, cc:512]
                ov = p2[:, jg0, :].rearrange("p (h q) -> p h q", h=2)[:, :, cc:512]
                return h.activation(ov, iv, AF.Exp)
            sc.emit("scalar", expf, waits=w, inc=A_EXP)
            exp_done = sc.v(A_EXP)

            if diag:
                def mskf(h, cc=c0, jg0=jg):
                    ov = p2[:, jg0, :].rearrange("p (h q) -> p h q", h=2)[:, :, cc:cc + 128]
                    tv = tri2[:, :].rearrange("p (h q) -> p h q", h=2)
                    return h.tensor_tensor(ov, ov, tv, ALU.mult)
                sc.emit("vector", mskf, waits=[(A_EXP, exp_done)], inc=A_MSK)
            else:
                sc.emit("vector", lambda h: h.engine_nop(),
                        waits=[(A_EXP, exp_done)], inc=A_MSK)
            msk_done = sc.v(A_MSK)

            def avm(h, p0=p, j0=j, cc=c0, last0=last, first=(j == 0), g0=g, jg0=jg,
                    ot=kidx % 2):
                for hh in range(2):
                    ins = h.matmul(
                        psOT[0:65, ot, 512 * hh + cc:512 * hh + 512],
                        vbuf[:, g0, j0, hh, :],
                        p2[:, jg0, 512 * hh + cc:512 * hh + 512],
                        start=first, stop=last0)
                return ins
            avw = [(A_MSK, msk_done)]
            if j == 0 and kidx >= 2:
                avw.append((A_EV, 3 * (kidx - 1)))
            if last:
                sc.emit("tensor", avm, waits=avw, inc=A_AV)
                av_done = sc.v(A_AV)
                row = 2 * kidx
                eg = kidx % 2
                w = [(A_AV, av_done)]
                if kidx >= 2:
                    w.append((A_YD, 48 * (kidx - 1)))
                sc.emit("vector",
                        lambda h, e=eg, ot=kidx % 2: h.tensor_copy(
                            lstg[64:65, e, :], psOT[64:65, ot, :]),
                        waits=w, inc=A_EV)
                sc.emit("vector",
                        lambda h, e=eg, ot=kidx % 2: h.tensor_copy(
                            ystg[0:64, e, 0:512], psOT[0:64, ot, 0:512]),
                        inc=A_EV)
                sc.emit("vector",
                        lambda h, e=eg, ot=kidx % 2: h.tensor_copy(
                            ystg[0:64, e, 512:1024], psOT[0:64, ot, 512:1024]),
                        inc=A_EV)
                ev_here = sc.v(A_EV)
                sc.emit("gpsimd",
                        lambda h, p0=p, q0=qc, e=eg: h.dma_start(
                            out=d_yh[p0, 0:64, 512 * q0:512 * q0 + 512], in_=ystg[0:64, e, 0:512]),
                        waits=[(A_EV, ev_here)], inc=A_YD, inck=16)
                sc.emit("gpsimd",
                        lambda h, p0=p, q0=qc, e=eg: h.dma_start(
                            out=d_yh[p0, 64:128, 512 * q0:512 * q0 + 512], in_=ystg[0:64, e, 512:1024]),
                        inc=A_YD, inck=16)
                sc.emit("gpsimd",
                        lambda h, r=row, e=eg: h.dma_start(
                            out=l_all[r:r + 2, :],
                            in_=lstg[64:65, e, :].rearrange("p (a b) -> p a b", a=2)),
                        inc=A_YD, inck=16)
                kidx += 1
            else:
                pending_av = (avm, avw)
            nj += 1
        sc.emit("vector", lambda h: h.reciprocal(rl_all[:, :], l_all[:, :]),
                waits=[(A_YD, cnt[A_YD])], inc=A_EV)
        sc.run(nc)
    A_YD_ALL = cnt[A_YD]
    G_ALL3 = cnt[A_EV]
    st3.close()

    st_acc = ExitStack()
    acc2 = sb(st_acc, "acc2", [128, 8 * TOWN + 512], F32)
    w_own = sb(st_acc, "w_own", [8, TOWN], F32)
    st_sc = ExitStack()
    scoresT = sb(st_sc, "scoresT", [8, T], F32)
    st4 = ExitStack()
    wc_sb = sb(st4, "wc_sb", [128, NDC, C], BF16)
    rw_sb = sb(st4, "rw_sb", [128, NDC, 8], F32)
    ybuf = sb(st4, "ybuf", [128, 2, 8, 512], BF16)
    ystr = sb(st4, "ystr", [128, 2, 8, 512], BF16)
    xfb = sb(st4, "xfb", [128, 2, 512], F32)
    stag2 = sb(st4, "stag2", [2, 2, 512], F32)
    rbuf = sb(st4, "rbuf", [128, NDC, 512], F32)
    smls = sb(st4, "smls", [1, 3072], F32)
    xpbf = sb(st4, "xpbf", [128, NDC, 512], BF16)
    xpf = sb(st4, "xpf", [128, NDC, 512], F32)

    sc = Sched(sems, cnt)
    L4, M4, V4, S4, D4, G4, L4b, D4s = S(), S(), S(), S(), S(), S(), S(), S()
    with nc.psum_tensor("psRB", [128, 2, 512], F32) as psRB, \
         nc.psum_tensor("psAT", [128, 2, 512], F32) as psAT, \
         nc.psum_tensor("psMS", [1, 1024], F32) as psMS, \
         nc.psum_tensor("psSC", [8, 512], F32) as psSC:
        sc.emit("gpsimd", lambda h: h.dma_start(out=wc_sb[:, :, :], in_=d_wc[:, :, :]),
                inc=L4, inck=16)
        sc.emit("sync", lambda h: h.dma_start(out=rw_sb[:, :, :], in_=d_rwT[:, :, :]),
                inc=D4, inck=16)
        sc.emit("vector", lambda h: h.tensor_scalar_max(t_sb[:, :], t_sb[:, :], 0.1),
                waits=[(SL, LD_ALL)], inc=V4)
        sc.emit("vector", lambda h: h.reciprocal(t_sb[:, :], t_sb[:, :]), inc=V4)
        sc.emit("gpsimd", lambda h: h.partition_broadcast(rtemp[:, :], t_sb[:, :]),
                waits=[(V4, 2)], inc=L4b)
        ld4 = sc.v(L4)

        vnorm_done, resid_done, tc_end_v4, resid_v4, rlb_mm, statm_done = {}, {}, {}, {}, {}, {}
        dbl_prev = [0]
        g4_prev = [0]
        for tc in range(NTC):
            g = tc % 2
            w = [(A_YD, A_YD_ALL)] if tc == 0 else []
            if tc >= 2:
                w.append((V4, vnorm_done[tc - 2]))
            sc.emit("sync", lambda h, t0=tc, g0=g: h.dma_start(
                out=ystr[:, g0, :, :],
                in_=d_yh[:, :, 512 * t0:512 * t0 + 512].transpose([1, 0, 2])),
                waits=w, inc=D4, inck=16)
            ld_here = sc.v(D4)

            for p in range(NPAIR):
                k0 = 2 * (NPAIR * p + tc)
                wg_ = []
                if NPAIR * tc + p >= 2:
                    wg_ = [(M4, rlb_mm.get(NPAIR * tc + p - 2, 0))]
                sc.emit("scalar", lambda h, k=k0, g0=p % 2: h.dma_start(
                    out=stag2[:, g0, :], in_=rl_all[k:k + 2, :]),
                    waits=wg_, inc=D4s, inck=16)
                gat_here = sc.v(D4s)
                w = [(D4s, gat_here), (L4, ld4)]
                if p >= 2:
                    w.append((V4, sc.v(V4) - 1))
                elif tc >= 1:
                    w.append((V4, vnorm_done[tc - 1]))
                sc.emit("tensor", lambda h, g0=p % 2: h.matmul(
                    psRB[:, g0, :], sel2[:, :], stag2[:, g0, :],
                    start=True, stop=True),
                    waits=w, inc=M4)
                rlb_mm[NPAIR * tc + p] = sc.v(M4)
                sc.emit("vector", lambda h, p0=p, g0=p % 2, gg=g: h.tensor_tensor(
                    ybuf[:, gg, p0, :], ystr[:, gg, p0, :], psRB[:, g0, :], ALU.mult),
                    waits=[(M4, sc.v(M4)), (D4, ld_here)], inc=V4)
            vnorm_done[tc] = sc.v(V4)

            for dout in range(NDC):
                gi = 8 * tc + dout
                w = [(V4, resid_v4[gi - 2])] if gi >= 2 else []
                sc.emit("sync", lambda h, t0=tc, d0=dout, g2=gi % 2: h.dma_start(
                    out=xfb[:, g2, :], in_=d_xT[:, d0, 512 * t0:512 * t0 + 512]),
                    waits=w, inc=D4, inck=16)
                xf_here = sc.v(D4)
                w = [(V4, vnorm_done[tc])]
                if dout >= 2:
                    w.append((V4, sc.v(V4)))

                def wcm(h, d0=dout, g0=dout % 2, gg=g):
                    for p in range(NPAIR):
                        ins = h.matmul(psAT[:, g0, :],
                                       wc_sb[:, p, 128 * d0:128 * d0 + 128],
                                       ybuf[:, gg, p, :], start=(p == 0), stop=(p == NPAIR - 1))
                    return ins
                if dout < 2 and tc >= 1:
                    w.append((V4, dbl_prev[0]))
                sc.emit("tensor", wcm, waits=w, inc=M4)
                wrs = [(M4, sc.v(M4)), (D4, xf_here)]
                if dout == 0 and tc >= 1:
                    wrs.append((G4, g4_prev[0]))
                sc.emit("vector", lambda h, d0=dout, g0=dout % 2, g2=gi % 2: h.tensor_tensor(
                    rbuf[:, d0, :], psAT[:, g0, :], xfb[:, g2, :], ALU.add),
                    waits=wrs, inc=V4)
                resid_v4[gi] = sc.v(V4)
            resid_done[tc] = sc.v(V4)

            sc.emit("scalar", lambda h: h.activation(xpbf[:, :, :], rbuf[:, :, :], AF.Square),
                    waits=[(V4, resid_done[tc])], inc=S4)
            sc.emit("vector", lambda h, gg=g: h.tensor_copy(ybuf[:, gg, :, :], rbuf[:, :, :]),
                    waits=[(V4, vnorm_done[tc])], inc=V4)
            rdy = (sc.v(S4), sc.v(V4))

            def statm(h, gg=g):
                for d0 in range(NDC):
                    h.matmul(psMS[0:1, 0:512], ones_b[:, :], ybuf[:, gg, d0, :],
                             start=(d0 == 0), stop=(d0 == NDC - 1))
                for d0 in range(NDC):
                    ins = h.matmul(psMS[0:1, 512:1024], ones_b[:, :], xpbf[:, d0, :],
                                   start=(d0 == 0), stop=(d0 == NDC - 1))
                return ins
            sc.emit("tensor", statm, waits=[(S4, rdy[0]), (V4, rdy[1])], inc=M4)
            stat_done = sc.v(M4)
            statm_done[tc] = stat_done

            def small1(h):
                h.tensor_scalar_mul(smls[0:1, 0:512], psMS[0:1, 0:512], 1.0 / C)
                h.tensor_scalar_mul(smls[0:1, 512:1024], psMS[0:1, 512:1024], 1.0 / C)
                h.tensor_tensor(smls[0:1, 1024:1536], smls[0:1, 0:512], smls[0:1, 0:512], ALU.mult)
                h.tensor_tensor(smls[0:1, 1536:2048], smls[0:1, 512:1024], smls[0:1, 1024:1536], ALU.subtract)
                return h.tensor_scalar_add(smls[0:1, 1536:2048], smls[0:1, 1536:2048], 1e-5)
            sc.emit("vector", small1, waits=[(M4, stat_done)], inc=V4)
            sc.emit("scalar", lambda h: h.activation(smls[0:1, 2048:2560], smls[0:1, 1536:2048],
                                                     AF.Ln),
                    waits=[(V4, sc.v(V4))], inc=S4)
            sc.emit("scalar", lambda h: h.activation(smls[0:1, 2560:3072], smls[0:1, 2048:2560],
                                                     AF.Exp, scale=-0.5),
                    inc=S4)
            sc.emit("vector", lambda h: h.tensor_tensor(
                smls[0:1, 1024:1536], smls[0:1, 0:512], smls[0:1, 2560:3072], ALU.mult),
                waits=[(S4, sc.v(S4))], inc=V4)
            ms_done = sc.v(V4)

            def bcm(h):
                h.matmul(psAT[:, 0, :], ones1f[:, :], smls[0:1, 2560:3072], start=True, stop=True)
                return h.matmul(psAT[:, 1, :], ones1f[:, :], smls[0:1, 1024:1536], start=True, stop=True)
            sc.emit("tensor", bcm, waits=[(V4, ms_done)], inc=M4)
            bc_done = sc.v(M4)

            for dout in range(NDC):
                sc.emit("vector", lambda h, d0=dout: h.tensor_tensor(
                    xpf[:, d0, :], rbuf[:, d0, :], psAT[:, 0, :], ALU.mult),
                    waits=[(M4, bc_done)], inc=V4)
                sc.emit("vector", lambda h, d0=dout: h.tensor_tensor(
                    xpf[:, d0, :], xpf[:, d0, :], psAT[:, 1, :], ALU.subtract),
                    inc=V4)
            xpf_done = sc.v(V4)
            sc.emit("vector", lambda h: h.tensor_scalar_mul(
                rbuf[:, :, :], xpf[:, :, :], 2.0), waits=[], inc=V4)
            dbl_done = sc.v(V4)
            dbl_prev[0] = dbl_done

            def rtm(h):
                for d0 in range(NDC):
                    ins = h.matmul(psSC[:, :], rw_sb[:, d0, :], xpf[:, d0, :],
                                   start=(d0 == 0), stop=(d0 == NDC - 1))
                return ins
            sc.emit("tensor", rtm, waits=[(V4, xpf_done)], inc=M4)
            sc.emit("vector", lambda h, t0=tc: h.tensor_scalar(
                scoresT[:, 512 * t0:512 * t0 + 512], psSC[:, :],
                rb_sb[:, 0:1], rtemp[:, 0:1], ALU.add, ALU.mult),
                waits=[(M4, sc.v(M4)), (L4b, 1)], inc=V4)

            for dout in range(NDC):
                def dyn(h, t0=tc, d0=dout):
                    off = h.value_load(offs_sb[t0:t0 + 1, d0:d0 + 1])
                    return h.dma_start(out=acc2[:, bass.ds(off, 512)], in_=rbuf[:, d0, :])
                sc.emit("gpsimd", dyn, waits=[(V4, dbl_done)], inc=G4, inck=16)
            g4_prev[0] = sc.v(G4)
            tc_end_v4[tc] = sc.v(V4)
        sc.run(nc)
    G4_ALL = cnt[G4]
    st4.close()

    st5 = ExitStack()
    sc_tm = sb(st5, "sc_tm", [128, 256], F32)
    mx32 = sb(st5, "mx32", [128, 32], F32)
    ssum = sb(st5, "ssum", [128, 32], F32)
    ebuf = sb(st5, "ebuf", [128, 256], F32)
    ebf = sb(st5, "ebf", [128, 256], BF16)
    tsb5 = sb(st5, "tsb5", [1, 256], F32)
    lnt8 = sb(st5, "lnt8", [1, 8], F32)
    lnt_b = sb(st5, "lnt_b", [128, 8], F32)
    gbuf = sb(st5, "gbuf", [128, 256], F32)
    eqb = sb(st5, "eqb", [128, 256], F32)
    g2b = sb(st5, "g2b", [128, 256], F32)
    v1b = sb(st5, "v1b", [128, 32], F32)
    v2b = sb(st5, "v2b", [128, 32], F32)
    w_tm = sb(st5, "w_tm", [128, 256], F32)
    w_em = sb(st5, "w_em", [8, T], F32)

    sc = Sched(sems, cnt)
    M5, V5, S5, G5, G5d = S(), S(), S(), S(), S()
    v3 = lambda x: x[:, :].rearrange("p (j e) -> p j e", e=8)

    with nc.psum_tensor("psT5", [128, 256], F32) as psT5, \
         nc.psum_tensor("psS5", [1, 256], F32) as psS5, \
         nc.psum_tensor("psW5", [8, 512], F32) as psW5:
        def trs(h):
            for j in range(NKT):
                ins = h.transpose(psT5[:, 8 * j:8 * j + 8],
                                  scoresT[0:8, 128 * j:128 * j + 128],
                                  ident_f[0:8, 0:8])
            return ins
        sc.emit("tensor", trs, inc=M5)
        sc.emit("vector", lambda h: h.tensor_copy(sc_tm[:, :], psT5[:, :]),
                waits=[(M5, sc.v(M5))], inc=V5)

        tsb_done = {}
        for it in range(3):
            sc.emit("vector", lambda h: h.reduce_max(mx32[:, :], v3(sc_tm), axis=AX.X),
                    inc=V5)
            for e0 in range(8):
                sc.emit("vector", lambda h, e=e0: h.tensor_tensor(
                    v3(sc_tm)[:, :, e], v3(sc_tm)[:, :, e], mx32[:, :], ALU.subtract),
                    inc=V5)
            sc.emit("scalar", lambda h: h.activation(ebuf[:, :], sc_tm[:, :], AF.Exp),
                    waits=[(V5, sc.v(V5))], inc=S5)
            sc.emit("vector", lambda h: h.reduce_sum(ssum[:, :], v3(ebuf), axis=AX.X),
                    waits=[(S5, sc.v(S5))], inc=V5)
            sc.emit("scalar", lambda h: h.activation(ssum[:, :], ssum[:, :], AF.Ln),
                    waits=[(V5, sc.v(V5))], inc=S5)
            for e0 in range(8):
                sc.emit("vector", lambda h, e=e0: h.tensor_tensor(
                    v3(sc_tm)[:, :, e], v3(sc_tm)[:, :, e], ssum[:, :], ALU.subtract),
                    waits=[(S5, sc.v(S5))] if e0 == 0 else [], inc=V5)
            sc.emit("scalar", lambda h: h.activation(ebf[:, :], sc_tm[:, :], AF.Exp),
                    waits=[(V5, sc.v(V5))], inc=S5)
            sc.emit("tensor", lambda h: h.matmul(psS5[:, :], ones_b[:, :], ebf[:, :],
                                                 start=True, stop=True),
                    waits=[(S5, sc.v(S5))] + ([(V5, tsb_done[it - 1])] if it > 0 else []),
                    inc=M5)
            sc.emit("vector", lambda h: h.tensor_copy(tsb5[:, :], psS5[:, :]),
                    waits=[(M5, sc.v(M5))], inc=V5)
            tsb_done[it] = sc.v(V5)
            sc.emit("vector", lambda h: h.reduce_sum(
                lnt8[:, :], tsb5[:, :].rearrange("p (j e) -> p e j", e=8), axis=AX.X),
                inc=V5)
            sc.emit("scalar", lambda h: h.activation(lnt8[:, :], lnt8[:, :], AF.Ln),
                    waits=[(V5, sc.v(V5))], inc=S5)
            sc.emit("gpsimd", lambda h: h.partition_broadcast(lnt_b[:, :], lnt8[:, :]),
                    waits=[(S5, sc.v(S5))], inc=G5)
            for e0 in range(8):
                sc.emit("vector", lambda h, e=e0: h.tensor_scalar_sub(
                    v3(sc_tm)[:, :, e], v3(sc_tm)[:, :, e], lnt_b[:, e:e + 1]),
                    waits=[(G5, sc.v(G5))] if e0 == 0 else [], inc=V5)

        sc.emit("scalar", lambda h: h.activation(gbuf[:, :], sc_tm[:, :], AF.Exp),
                waits=[(V5, sc.v(V5))], inc=S5)
        sc.emit("vector", lambda h: h.reduce_sum(ssum[:, :], v3(gbuf), axis=AX.X),
                waits=[(S5, sc.v(S5))], inc=V5)
        sc.emit("vector", lambda h: h.tensor_scalar_add(ssum[:, :], ssum[:, :], 1e-8), inc=V5)
        sc.emit("vector", lambda h: h.reciprocal(ssum[:, :], ssum[:, :]), inc=V5)
        rec_done = sc.v(V5)
        for e0 in range(8):
            sc.emit("vector", lambda h, e=e0: h.tensor_tensor(
                v3(gbuf)[:, :, e], v3(gbuf)[:, :, e], ssum[:, :], ALU.mult),
                waits=[(V5, rec_done)] if e0 == 0 else [], inc=V5)
        sc.emit("vector", lambda h: h.reduce_max(v1b[:, :], v3(gbuf), axis=AX.X), inc=V5)
        for e0 in range(8):
            sc.emit("vector", lambda h, e=e0: h.tensor_tensor(
                v3(eqb)[:, :, e], v3(gbuf)[:, :, e], v1b[:, :], ALU.is_equal), inc=V5)
        for e0 in range(8):
            sc.emit("vector", lambda h, e=e0: h.scalar_tensor_tensor(
                v3(g2b)[:, :, e], v3(eqb)[:, :, e], -1e30, v3(gbuf)[:, :, e],
                ALU.mult, ALU.add), inc=V5)
        sc.emit("vector", lambda h: h.reduce_max(v2b[:, :], v3(g2b), axis=AX.X), inc=V5)
        sc.emit("vector", lambda h: h.tensor_tensor(v1b[:, :], v1b[:, :], v2b[:, :], ALU.add),
                inc=V5)
        sc.emit("vector", lambda h: h.tensor_scalar_add(v1b[:, :], v1b[:, :], 1e-8), inc=V5)
        sc.emit("vector", lambda h: h.reciprocal(v1b[:, :], v1b[:, :]), inc=V5)
        rec2_done = sc.v(V5)
        for e0 in range(8):
            sc.emit("vector", lambda h, e=e0: h.tensor_tensor(
                v3(eqb)[:, :, e], v3(gbuf)[:, :, e], v2b[:, :], ALU.is_ge),
                waits=[(V5, rec2_done)] if e0 == 0 else [], inc=V5)
        for e0 in range(8):
            sc.emit("vector", lambda h, e=e0: h.tensor_tensor(
                v3(w_tm)[:, :, e], v3(gbuf)[:, :, e], v3(eqb)[:, :, e], ALU.mult), inc=V5)
        for e0 in range(8):
            sc.emit("vector", lambda h, e=e0: h.tensor_tensor(
                v3(w_tm)[:, :, e], v3(w_tm)[:, :, e], v1b[:, :], ALU.mult), inc=V5)
        wtm_done = sc.v(V5)

        for c in range(8):
            w = [(V5, wtm_done if c == 0 else sc.v(V5))]

            def trb(h, c0=c):
                for jj in range(4):
                    j = 4 * c0 + jj
                    ins = h.transpose(psW5[:, 128 * jj:128 * jj + 128],
                                      w_tm[:, 8 * j:8 * j + 8], ident_f[:, :])
                return ins
            sc.emit("tensor", trb, waits=w, inc=M5)
            sc.emit("vector", lambda h, c0=c: h.tensor_copy(
                w_em[:, 512 * c0:512 * c0 + 512], psW5[:, :]),
                waits=[(M5, sc.v(M5))], inc=V5)

        def wslice(h):
            off = h.value_load(soff_sb[0:1, 0:1])
            return h.dma_start(out=w_own[:, :], in_=w_em[:, bass.ds(off, TOWN)])
        sc.emit("gpsimd", wslice, waits=[(V5, sc.v(V5))], inc=G5d, inck=16)
        if _DBG_SCRATCH:
            sc.emit("sync", lambda h: h.dma_start(out=d_dbg5[:, :], in_=w_tm[:, :]),
                    waits=[(G5d, sc.v(G5d))], inc=G5d, inck=16)
            sc.emit("sync", lambda h: h.dma_start(out=d_dbg6[:, :], in_=w_em[:, :]),
                    inc=G5d, inck=16)
            sc.emit("sync", lambda h: h.dma_start(out=d_dbg7[:, :], in_=sc_tm[:, :]),
                    inc=G5d, inck=16)
            sc.emit("sync", lambda h: h.dma_start(out=d_dbg8[:, :], in_=gbuf[:, :]),
                    inc=G5d, inck=16)
            sc.emit("sync", lambda h: h.dma_start(out=d_dbg1[:, :], in_=l_all[:, :]),
                    inc=G5d, inck=16)
            sc.emit("sync", lambda h: h.dma_start(out=d_dbg2[:, :], in_=scoresT[:, :]),
                    inc=G5d, inck=16)
            sc.emit("sync", lambda h: h.dma_start(out=d_dbg3[:, :], in_=w_own[:, :]),
                    inc=G5d, inck=16)
            sc.emit("sync", lambda h: None, waits=[(G5d, sc.v(G5d))])
        sc.run(nc)
    G5_ALL = cnt[G5d]
    st5.close()
    st_sc.close()

    st6 = ExitStack()
    xpo = sb(st6, "xpo", [128, NDC, TOWN], BF16)
    gpw = sb(st6, "gpw", [128, 8, TOWN], BF16)
    grelu = sb(st6, "grelu", [128, TOWN], BF16)
    gptmp = sb(st6, "gptmp", [128, TOWN], BF16)
    wgp_r = sb(st6, "wgp_r", [128, 2, 2, NDC, 128], BF16)
    wo_r = sb(st6, "wo_r", [128, 2, 8, 128], BF16)
    wbc = sb(st6, "wbc", [128, 2, TOWN], F32)
    wstag = sb(st6, "wstag", [1, 2, TOWN], F32)
    wbcb = sb(st6, "wbcb", [128, 2, TOWN], BF16)

    sc = Sched(sems, cnt)
    L6, M6, V6, S6, G6, G6d = S(), S(), S(), S(), S(), S()

    with nc.psum_tensor("psG", [128, 1024], F32) as psG, \
         nc.psum_tensor("psP6", [128, 1024], F32) as psP6, \
         nc.psum_tensor("psO6", [128, 2, 1024], F32) as psO6:
        if _DBG_SCRATCH:
            DBG9 = S()
            sc.emit("sync", lambda h: h.dma_start(out=d_dbg9[:, :], in_=acc2[:, :]),
                    waits=[(G4, G4_ALL)], inc=DBG9, inck=16)
        sc.emit("vector", lambda h: h.tensor_scalar_mul(
            xpo[:, :, :], acc2[:, 0:8 * TOWN].rearrange("p (d t) -> p d t", d=NDC), 0.5),
            waits=[(G4, G4_ALL)], inc=V6)
        xpo_done = sc.v(V6)
        if _DBG_SCRATCH:
            sc.emit("sync", lambda h: h.dma_start(out=d_dbg10[:, :, :], in_=xpo[:, :, :]),
                    waits=[(V6, xpo_done)], inc=DBG9, inck=16)
            sc.emit("sync", lambda h: None, waits=[(DBG9, 32)])

        om_last = [0]
        acc_stt = {}
        nfc = 0
        nth = 0
        ndo = 0
        wgld, wold, wbcd, gpend, grelu_rd = {}, {}, {}, {}, {}
        relu_cnt = {}
        stt_cnt = {}
        for e0 in range(E):
            w = [(G5d, G5_ALL)]
            if e0 >= 2:
                w.append((V6, gpend[e0 - 2]))
            sc.emit("gpsimd", lambda h, e=e0: h.dma_start(
                out=wstag[:, e % 2, :], in_=w_own[e:e + 1, :]), waits=w, inc=G6d, inck=16)
            sc.emit("gpsimd", lambda h, e=e0: h.partition_broadcast(
                wbc[:, e % 2, :], wstag[:, e % 2, :]), waits=[(G6d, sc.v(G6d))], inc=G6)
            sc.emit("vector", lambda h, e=e0: h.tensor_copy(wbcb[:, e % 2, :], wbc[:, e % 2, :]),
                    waits=[(G6, sc.v(G6))], inc=V6)
            wbcd[e0] = sc.v(V6)

            for half in range(2):
                for f8 in range(8):
                    fc = 8 * half + f8
                    w = [(M6, wgld[nfc - 2])] if nfc >= 2 else []
                    if nfc == 0:
                        w.append((CV_S, CV_ALL))
                    sc.emit("sync", lambda h, e=e0, f=fc, g=nfc % 2: h.dma_start(
                        out=wgp_r[:, g, :, :, :], in_=d_wgpb[e, f, :, :, :, :]),
                        waits=w, inc=L6, inck=16)
                    ldv = sc.v(L6)

                    for th in range(2):
                        w = [(L6, ldv), (V6, xpo_done)]
                        if nth >= 1:
                            w.append((S6, relu_cnt[nth - 1]))

                        def gm(h, g=nfc % 2, t1=th):
                            for d0 in range(NDC):
                                for t0 in range(2):
                                    ins = h.matmul(
                                        psG[:, 512 * t0:512 * t0 + 512],
                                        wgp_r[:, g, 0, d0, :],
                                        xpo[:, d0, 1024 * t1 + 512 * t0:1024 * t1 + 512 * t0 + 512],
                                        start=(d0 == 0), stop=(d0 == NDC - 1))
                            return ins
                        sc.emit("tensor", gm, waits=w, inc=M6)
                        wr = [(M6, sc.v(M6))]
                        if nth >= 1:
                            wr.append((V6, grelu_rd[nth - 1]))
                        sc.emit("scalar", lambda h, t1=th: h.activation(
                            grelu[:, 1024 * t1:1024 * t1 + 1024], psG[:, :], AF.Relu),
                            waits=wr, inc=S6)
                        relu_cnt[nth] = sc.v(S6)

                        w = []
                        if nth >= 1:
                            w.append((V6, stt_cnt[nth - 1]))

                        def pm(h, g=nfc % 2, t1=th):
                            for d0 in range(NDC):
                                for t0 in range(2):
                                    ins = h.matmul(
                                        psP6[:, 512 * t0:512 * t0 + 512],
                                        wgp_r[:, g, 1, d0, :],
                                        xpo[:, d0, 1024 * t1 + 512 * t0:1024 * t1 + 512 * t0 + 512],
                                        start=(d0 == 0), stop=(d0 == NDC - 1))
                            return ins
                        sc.emit("tensor", pm, waits=w, inc=M6)
                        pm_done = sc.v(M6)
                        if th == 1:
                            wgld[nfc] = pm_done

                        w = [(M6, pm_done), (S6, relu_cnt[nth]), (V6, wbcd[e0])]
                        if half == 1 or e0 > 0:
                            w.append((M6, om_last[0]))
                        sc.emit("vector", lambda h, t1=th: h.scalar_tensor_tensor(
                            gptmp[:, 1024 * t1:1024 * t1 + 1024], psP6[:, :], 1.0,
                            grelu[:, 1024 * t1:1024 * t1 + 1024], ALU.mult, ALU.mult),
                            waits=w, inc=V6)
                        sc.emit("vector", lambda h, f=f8, e=e0, t1=th: h.tensor_tensor(
                            gpw[:, f, 1024 * t1:1024 * t1 + 1024],
                            gptmp[:, 1024 * t1:1024 * t1 + 1024],
                            wbcb[:, e % 2, 1024 * t1:1024 * t1 + 1024], ALU.mult),
                            inc=V6)
                        stt_cnt[nth] = sc.v(V6)
                        grelu_rd[nth] = sc.v(V6)
                        nth += 1
                    nfc += 1
                gpw_done = sc.v(V6)

                for dout in range(NDC):
                    w = [(M6, wold[ndo // 2 - 1])] if ndo >= 2 else []
                    sc.emit("sync", lambda h, e=e0, hf=half, d0=dout, g=(ndo // 2) % 2: h.dma_start(
                        out=wo_r[:, g, :, :], in_=d_wob[e, hf, d0, :, :, :]),
                        waits=w, inc=L6, inck=16)
                    ldv = sc.v(L6)
                    for th in range(2):
                        w = [(L6, ldv), (V6, gpw_done)]
                        if ndo >= 2:
                            w.append((V6, acc_stt[ndo - 2]))

                        def om(h, g=(ndo // 2) % 2, og=ndo % 2, t1=th):
                            for f in range(8):
                                for t0 in range(2):
                                    ins = h.matmul(
                                        psO6[:, og, 512 * t0:512 * t0 + 512],
                                        wo_r[:, g, f, :],
                                        gpw[:, f, 1024 * t1 + 512 * t0:1024 * t1 + 512 * t0 + 512],
                                        start=(f == 0), stop=(f == 7))
                            return ins
                        sc.emit("tensor", om, waits=w, inc=M6)
                        om_last[0] = sc.v(M6)
                        if th == 1:
                            wold[ndo // 2] = sc.v(M6)
                        sc.emit("vector", lambda h, d0=dout, og=ndo % 2, t1=th: h.scalar_tensor_tensor(
                            acc2[:, TOWN * d0 + 1024 * t1:TOWN * d0 + 1024 * t1 + 1024],
                            psO6[:, og, :], 1.0,
                            acc2[:, TOWN * d0 + 1024 * t1:TOWN * d0 + 1024 * t1 + 1024],
                            ALU.mult, ALU.add),
                            waits=[(M6, sc.v(M6))], inc=V6)
                        acc_stt[ndo] = sc.v(V6)
                        ndo += 1
            gpend[e0] = sc.v(V6)
        sc.run(nc)
    st6.close()

    st7 = ExitStack()
    rbf7 = sb(st7, "rbf7", [128, NDC, 512], BF16)
    sqb7 = sb(st7, "sqb7", [128, NDC, 512], BF16)
    sml7 = sb(st7, "sml7", [1, 3072], F32)
    ost = sb(st7, "ost", [128, 2, NDC, 512], F32)

    sc = Sched(sems, cnt)
    M7, V7, S7, D7 = S(), S(), S(), S()
    statm7_done = {}
    if _DBG_SCRATCH:
        scd = Sched(sems, cnt)
        DD7 = S()
        scd.emit("sync", lambda h: h.dma_start(out=d_dbg4[:, :], in_=acc2[:, :]),
                 inc=DD7, inck=16)
        scd.emit("sync", lambda h: None, waits=[(DD7, 16)])
        scd.run(nc)
    with nc.psum_tensor("psM7", [1, 1024], F32) as psM7, \
         nc.psum_tensor("psB7", [128, 2, 512], F32) as psB7:
        accv = acc2[:, 0:8 * TOWN].rearrange("p (d t) -> p d t", d=NDC)
        for tc in range(4):
            av = accv[:, :, 512 * tc:512 * tc + 512]
            sc.emit("scalar", lambda h, a=av: h.activation(sqb7[:, :, :], a, AF.Square),
                    inc=S7)
            sc.emit("vector", lambda h, a=av: h.tensor_copy(rbf7[:, :, :], a),
                    waits=[(M7, statm7_done.get(tc - 1, 0))], inc=V7)
            rdy = (sc.v(S7), sc.v(V7))

            def statm7(h):
                for d0 in range(NDC):
                    h.matmul(psM7[0:1, 0:512], ones_b[:, :], rbf7[:, d0, :],
                             start=(d0 == 0), stop=(d0 == NDC - 1))
                for d0 in range(NDC):
                    ins = h.matmul(psM7[0:1, 512:1024], ones_b[:, :], sqb7[:, d0, :],
                                   start=(d0 == 0), stop=(d0 == NDC - 1))
                return ins
            sc.emit("tensor", statm7, waits=[(S7, rdy[0]), (V7, rdy[1])], inc=M7)
            statm7_done[tc] = sc.v(M7)

            def small7(h):
                h.tensor_scalar_mul(sml7[0:1, 0:512], psM7[0:1, 0:512], 1.0 / C)
                h.tensor_scalar_mul(sml7[0:1, 512:1024], psM7[0:1, 512:1024], 1.0 / C)
                h.tensor_tensor(sml7[0:1, 1024:1536], sml7[0:1, 0:512], sml7[0:1, 0:512], ALU.mult)
                h.tensor_tensor(sml7[0:1, 1536:2048], sml7[0:1, 512:1024], sml7[0:1, 1024:1536], ALU.subtract)
                h.tensor_scalar_add(sml7[0:1, 1536:2048], sml7[0:1, 1536:2048], 1e-5)
                return h.reciprocal(sml7[0:1, 2048:2560], sml7[0:1, 1536:2048])
            sc.emit("vector", small7, waits=[(M7, sc.v(M7))], inc=V7)
            sc.emit("scalar", lambda h: h.activation(sml7[0:1, 2560:3072], sml7[0:1, 2048:2560],
                                                     AF.Sqrt),
                    waits=[(V7, sc.v(V7))], inc=S7)
            sc.emit("vector", lambda h: h.tensor_tensor(
                sml7[0:1, 1024:1536], sml7[0:1, 0:512], sml7[0:1, 2560:3072], ALU.mult),
                waits=[(S7, sc.v(S7))], inc=V7)

            def bcm7(h):
                h.matmul(psB7[:, 0, :], ones1f[:, :], sml7[0:1, 2560:3072], start=True, stop=True)
                return h.matmul(psB7[:, 1, :], ones1f[:, :], sml7[0:1, 1024:1536], start=True, stop=True)
            sc.emit("tensor", bcm7, waits=[(V7, sc.v(V7))], inc=M7)
            bc7 = sc.v(M7)

            g = tc % 2
            for dout in range(NDC):
                w = [(M7, bc7)]
                if tc >= 2 and dout == 0:
                    w.append((D7, 16 * (tc - 1)))
                sc.emit("vector", lambda h, a=av, d0=dout, g0=g: h.tensor_tensor(
                    ost[:, g0, d0, :], a[:, d0, :], psB7[:, 0, :], ALU.mult),
                    waits=w, inc=V7)
                sc.emit("vector", lambda h, d0=dout, g0=g: h.tensor_tensor(
                    ost[:, g0, d0, :], ost[:, g0, d0, :], psB7[:, 1, :], ALU.subtract),
                    inc=V7)
            sc.emit("sync", lambda h, t0=tc, g0=g: h.dma_start(
                out=d_yout[:, :, 512 * t0:512 * t0 + 512], in_=ost[:, g0, :, :]),
                waits=[(V7, sc.v(V7))], inc=D7, inck=16)
        sc.emit("sync", lambda h: None, waits=[(D7, sc.v(D7))])
        sc.run(nc)
    st7.close()
    st_acc.close()

    nc.finalize()
    _CACHE["nc"] = nc
    return nc


def _host_prep(inputs):
    x = np.asarray(inputs["L0_attn_x"], np.float32)
    ac = np.ascontiguousarray
    r3 = lambda a: ac(a.reshape(NDC, 128, -1).transpose(1, 0, 2))

    cp, sp = _rope_tables()
    tri = np.triu(np.ones((128, 128), np.float32))
    sel = np.zeros((2, 128), np.float32)
    sel[0, 0:64] = 1.0
    sel[1, 64:128] = 1.0

    wg = np.asarray(inputs["L0_ffn_wg"], np.float32)
    wp = np.asarray(inputs["L0_ffn_wp"], np.float32)
    wo = np.asarray(inputs["L0_ffn_wo"], np.float32)
    wgs = wg.reshape(E, NDC, 128, 16, 128)
    wps = wp.reshape(E, NDC, 128, 16, 128)
    wgp = ac(np.stack([wgs, wps], axis=1).transpose(0, 4, 3, 1, 2, 5))
    wo_h = ac(wo.reshape(E, 2, 8, 128, 8, 128).transpose(0, 1, 4, 3, 2, 5))

    common = {
        "wqdT": r3(np.asarray(inputs["L0_attn_Wq_down"], np.float32).T),
        "wkdT": r3(np.asarray(inputs["L0_attn_Wk_down"], np.float32).T),
        "wvdT": r3(np.asarray(inputs["L0_attn_Wv_down"], np.float32).T),
        "upqc": np.asarray(inputs["L0_attn_Wq_up_c"], np.float32),
        "upqe": np.asarray(inputs["L0_attn_Wq_up_e"], np.float32),
        "upqeT": ac(np.asarray(inputs["L0_attn_Wq_up_e"], np.float32).T),
        "upkc": np.asarray(inputs["L0_attn_Wk_up_c"], np.float32),
        "upke": np.asarray(inputs["L0_attn_Wk_up_e"], np.float32),
        "upkeT": ac(np.asarray(inputs["L0_attn_Wk_up_e"], np.float32).T),
        "upv": np.asarray(inputs["L0_attn_Wv_up"], np.float32),
        "Pm": _perm_matrix(),
        "wc": r3(np.asarray(inputs["L0_attn_Wc"], np.float32)),
        "rwT": r3(np.asarray(inputs["L0_ffn_router_w"], np.float32).T),
        "rb": ac(np.asarray(inputs["L0_ffn_router_b"], np.float32).reshape(8, 1)),
        "temp": ac(np.asarray(inputs["L0_ffn_temperature"], np.float32).reshape(1, 1)),
        "wgp": wgp,
        "wo": wo_h,
        "cosp": cp,
        "sinp": sp,
        "tri2": ac(np.concatenate([tri, tri], axis=1)),
        "ident": np.eye(128, dtype=np.float32),
        "sel2": sel,
    }
    xts = [r3(ac(x[b].T)) for b in range(B)]
    offs, soffs = [], []
    for s in range(2):
        o = np.full((NTC, 8), GARB, np.int32)
        for tc in range(NTC):
            if 4 * s <= tc < 4 * (s + 1):
                for dout in range(NDC):
                    o[tc, dout] = TOWN * dout + 512 * (tc - 4 * s)
        offs.append(o)
        soffs.append(np.array([[TOWN * s]], np.int32))
    return (common, xts, offs, soffs)


def kernel(**inputs):
    common, xts, offs, soffs = _host_prep(inputs)
    nc = build_nc()
    in_maps = []
    for c in range(8):
        b, s = c // 2, c % 2
        m = dict(common)
        m["xT"] = xts[b]
        m["offs"] = offs[s]
        m["soff"] = soffs[s]
        in_maps.append(m)
    res = run_bass_kernel_spmd(nc, in_maps, core_ids=list(range(8)))
    out = np.empty((B, T, C), np.float32)
    for c in range(8):
        b, s = c // 2, c % 2
        y = np.asarray(res.results[c]["yout"])
        out[b, TOWN * s:TOWN * (s + 1), :] = y.transpose(1, 0, 2).reshape(C, TOWN).T
    return out


# version 05
# speedup vs baseline: 1.2056x; 1.2056x over previous
import numpy as np
from contextlib import ExitStack

import concourse.bacc as bacc
import concourse.bass as bass
import concourse.mybir as mybir
from concourse.bass_utils import run_bass_kernel_spmd

F32 = mybir.dt.float32
BF16 = mybir.dt.bfloat16
I32 = mybir.dt.int32
AF = mybir.ActivationFunctionType
ALU = mybir.AluOpType
AX = mybir.AxisListType

B, T, C = 4, 4096, 1024
H, DH, DHE = 16, 64, 32
NPAIR = 8
E, FF = 8, 2048
TOWN = 2048
NDC = C // 128
NTC = T // 512
NKT = T // 128
GARB = 8 * TOWN

ENGINES = ("tensor", "vector", "scalar", "gpsimd", "sync")


class Sched:

    def __init__(self, sems, cnt):
        self.sems = sems
        self.cnt = cnt
        self.ops = {e: [] for e in ENGINES}

    def emit(self, eng, fn, waits=(), inc=None, inck=1):
        self.ops[eng].append((fn, tuple(waits), inc, inck))
        if inc is not None:
            self.cnt[inc] += inck

    def v(self, si):
        return self.cnt[si]

    def run(self, nc):
        sems = self.sems
        with nc.Block() as blk:
            for eng in ENGINES:
                lst = self.ops[eng]
                if not lst:
                    continue

                def make(lst):
                    def body(h):
                        for (fn, waits, inc, inck) in lst:
                            for (si, val) in waits:
                                if val > 0:
                                    h.wait_ge(sems[si], val)
                            ins = fn(h)
                            if inc is not None:
                                ins.then_inc(sems[inc], inck)
                    return body
                getattr(blk, eng)(make(lst))


def _rope_tables():
    inv = 1.0 / (10000.0 ** (np.arange(0, DHE, 2, dtype=np.float64) / DHE))
    freqs = np.arange(T, dtype=np.float64)[:, None] * inv[None, :]
    emb = np.concatenate([freqs, freqs], axis=1)
    cos = np.cos(emb).astype(np.float32).T
    sin = np.sin(emb).astype(np.float32).T
    cp = np.ones((128, T), np.float32)
    sp = np.zeros((128, T), np.float32)
    cp[32:64] = cos
    cp[96:128] = cos
    sp[32:64] = sin
    sp[96:128] = sin
    return cp, sp


def _perm_matrix():
    P = np.zeros((DHE, DHE), np.float32)
    for i in range(DHE // 2):
        P[2 * i + 1, 2 * i] = -1.0
        P[2 * i, 2 * i + 1] = 1.0
    return P


_CACHE = {}
_DBG_SCRATCH = False


def build_nc():
    if "nc" in _CACHE:
        return _CACHE["nc"]
    nc = bacc.Bacc("TRN2", target_bir_lowering=False, detect_race_conditions=False)

    dt = lambda n, s, d, k: nc.dram_tensor(n, s, d, kind=k).ap()
    d_xT = dt("xT", [128, NDC, T], F32, "ExternalInput")
    d_wqdT = dt("wqdT", [128, NDC, C], F32, "ExternalInput")
    d_wkdT = dt("wkdT", [128, NDC, C], F32, "ExternalInput")
    d_wvdT = dt("wvdT", [128, NDC, C], F32, "ExternalInput")
    d_upqc = dt("upqc", [64, 32], F32, "ExternalInput")
    d_upqe = dt("upqe", [64, 32], F32, "ExternalInput")
    d_upqeT = dt("upqeT", [32, 64], F32, "ExternalInput")
    d_upkc = dt("upkc", [64, 32], F32, "ExternalInput")
    d_upke = dt("upke", [64, 32], F32, "ExternalInput")
    d_upkeT = dt("upkeT", [32, 64], F32, "ExternalInput")
    d_upv = dt("upv", [64, 64], F32, "ExternalInput")
    d_P = dt("Pm", [32, 32], F32, "ExternalInput")
    d_wc = dt("wc", [128, NDC, C], F32, "ExternalInput")
    d_rwT = dt("rwT", [128, NDC, 8], F32, "ExternalInput")
    d_rb = dt("rb", [8, 1], F32, "ExternalInput")
    d_temp = dt("temp", [1, 1], F32, "ExternalInput")
    d_wgp = dt("wgp", [E, 16, 128, 2, NDC, 128], F32, "ExternalInput")
    d_wo = dt("wo", [E, 2, 8, 128, 8, 128], F32, "ExternalInput")
    d_cos = dt("cosp", [128, T], F32, "ExternalInput")
    d_sin = dt("sinp", [128, T], F32, "ExternalInput")
    d_tri = dt("tri2", [128, 256], F32, "ExternalInput")
    d_ident = dt("ident", [128, 128], F32, "ExternalInput")
    d_sel2 = dt("sel2", [2, 128], F32, "ExternalInput")
    d_offs = dt("offs", [NTC, 8], I32, "ExternalInput")
    d_soff = dt("soff", [1, 1], I32, "ExternalInput")

    d_yout = dt("yout", [128, 8, TOWN], F32, "ExternalOutput")

    _knd = "ExternalOutput" if _DBG_SCRATCH else "Internal"
    d_qh = dt("qh", [NPAIR, 128, T], BF16, _knd)
    d_kh = dt("kh", [NPAIR, 128, T], BF16, _knd)
    d_vh = dt("vh", [128, NKT, H, 65], BF16, _knd)
    d_yh = dt("yh", [NPAIR, 128, T], BF16, _knd)
    if _DBG_SCRATCH:
        d_dbg1 = dt("dbg1", [128, 512], F32, "ExternalOutput")
        d_dbg2 = dt("dbg2", [8, T], F32, "ExternalOutput")
        d_dbg3 = dt("dbg3", [8, TOWN], F32, "ExternalOutput")
        d_dbg4 = dt("dbg4", [128, 8 * TOWN + 512], F32, "ExternalOutput")
        d_dbg5 = dt("dbg5", [128, 256], F32, "ExternalOutput")
        d_dbg6 = dt("dbg6", [8, T], F32, "ExternalOutput")
        d_dbg7 = dt("dbg7", [128, 256], F32, "ExternalOutput")
        d_dbg8 = dt("dbg8", [128, 256], F32, "ExternalOutput")
        d_dbg9 = dt("dbg9", [128, 8 * TOWN + 512], F32, "ExternalOutput")
        d_dbg10 = dt("dbg10", [128, NDC, TOWN], BF16, "ExternalOutput")
        d_dbg11 = dt("dbg11", [128, 8, TOWN], BF16, "ExternalOutput")

    es = ExitStack()
    nc._keepalive = es
    sems = [es.enter_context(nc.semaphore(f"sm{i}")) for i in range(64)]
    cnt = {i: 0 for i in range(64)}
    _ns = [0]

    def S():
        i = _ns[0]
        _ns[0] += 1
        return i

    sb = lambda st, name, shape, d: st.enter_context(nc.sbuf_tensor(name, shape, d))

    ident_f = sb(es, "ident_f", [128, 128], F32)
    ones_b = sb(es, "ones_b", [128, 1], BF16)
    ones1f = sb(es, "ones1f", [1, 128], F32)
    sel2 = sb(es, "sel2s", [2, 128], F32)
    tri2 = sb(es, "tri2s", [128, 256], BF16)
    l_all = sb(es, "l_all", [128, 512], F32)
    rl_all = sb(es, "rl_all", [128, 512], F32)
    rb_sb = sb(es, "rb_sb", [8, 1], F32)
    rtemp = sb(es, "rtemp", [8, 1], F32)
    t_sb = sb(es, "t_sb", [1, 1], F32)
    offs_sb = sb(es, "offs_sb", [NTC, 8], I32)
    soff_sb = sb(es, "soff_sb", [1, 1], I32)

    st_qk = ExitStack()
    xT = sb(st_qk, "xT_sb", [128, NDC, T], BF16)
    cosp = sb(st_qk, "cosp_sb", [128, T], BF16)
    sinp = sb(st_qk, "sinp_sb", [128, T], BF16)
    wq_eff = sb(st_qk, "wq_eff", [128, NDC, C], BF16)
    wq_rot = sb(st_qk, "wq_rot", [128, NDC, C], BF16)
    wk_eff = sb(st_qk, "wk_eff", [128, NDC, C], BF16)
    wk_rot = sb(st_qk, "wk_rot", [128, NDC, C], BF16)
    st_fold = ExitStack()
    wdT2 = sb(st_fold, "wdT2", [128, 2 * NDC, C], BF16)
    upfq = sb(st_fold, "upfq", [128, 128], BF16)
    upfk = sb(st_fold, "upfk", [128, 128], BF16)
    upeT2 = sb(st_fold, "upeT2", [32, 2, 64], BF16)
    P_sb = sb(st_fold, "P_sb", [32, 32], BF16)

    sc = Sched(sems, cnt)
    SL, SLg, SLm = S(), S(), S()
    for dst, src in [(ident_f, d_ident), (sel2, d_sel2), (rb_sb, d_rb), (t_sb, d_temp),
                     (offs_sb, d_offs), (soff_sb, d_soff)]:
        sc.emit("sync", lambda h, a=dst, b=src: h.dma_start(out=a[:, :], in_=b[:, :]),
                inc=SL, inck=16)
    for dst, src in [(tri2, d_tri), (cosp, d_cos), (sinp, d_sin), (P_sb, d_P)]:
        sc.emit("gpsimd", lambda h, a=dst, b=src: h.dma_start(out=a[:, :], in_=b[:, :]),
                inc=SLg, inck=16)
    for (dst, c0, c1, src) in [(upfq, 0, 32, d_upqc), (upfq, 32, 64, d_upqe),
                               (upfk, 0, 32, d_upkc), (upfk, 32, 64, d_upke)]:
        sc.emit("gpsimd", lambda h, a=dst, x0=c0, x1=c1, b=src: h.dma_start(
            out=a[0:64, x0:x1], in_=b[:, :]), inc=SLg, inck=16)
    sc.emit("gpsimd", lambda h: h.dma_start(out=upeT2[:, 0, :], in_=d_upqeT[:, :]),
            inc=SLg, inck=16)
    sc.emit("gpsimd", lambda h: h.dma_start(out=upeT2[:, 1, :], in_=d_upkeT[:, :]),
            inc=SLg, inck=16)
    sc.emit("gpsimd", lambda h: h.dma_start(out=xT[:, :, :], in_=d_xT[:, :, :]),
            inc=SLg, inck=16)
    sc.emit("gpsimd", lambda h: h.dma_start(out=wdT2[:, 0:NDC, :], in_=d_wqdT[:, :, :]),
            inc=SLg, inck=16)
    sc.emit("gpsimd", lambda h: h.dma_start(out=wdT2[:, NDC:2 * NDC, :], in_=d_wkdT[:, :, :]),
            inc=SLg, inck=16)
    sc.emit("gpsimd", lambda h: h.memset(ones_b[:, :], 1.0))
    sc.emit("gpsimd", lambda h: h.memset(ones1f[:, :], 1.0))
    sc.emit("gpsimd", lambda h: h.memset(upfq[:, 64:96], 0.0))
    sc.emit("gpsimd", lambda h: h.memset(upfk[:, 64:96], 0.0), inc=SLm, inck=1)
    sc.run(nc)
    LD_ALL = sc.v(SL)
    LDG_ALL = sc.v(SLg)

    sc = Sched(sems, cnt)
    SM0, SV0, SD0 = S(), S(), S()
    with nc.psum_tensor("psR", [64, 2, 512], F32) as psR:
        for kk in range(2):
            sc.emit("tensor",
                    lambda h, k=kk: h.matmul(psR[:, k, 0:32], upeT2[:, k, :], P_sb[:, :],
                                             start=True, stop=True),
                    waits=[(SL, LD_ALL), (SLg, LDG_ALL), (SLm, 1)], inc=SM0)
        sc.emit("vector", lambda h: h.tensor_copy(upfq[0:64, 96:128], psR[:, 0, 0:32]),
                waits=[(SM0, 1)], inc=SV0)
        sc.emit("vector", lambda h: h.tensor_copy(upfk[0:64, 96:128], psR[:, 1, 0:32]),
                waits=[(SM0, 2)], inc=SV0)
        sc.emit("vector", lambda h: h.tensor_scalar_mul(upfq[0:64, :], upfq[0:64, :], 0.125),
                inc=SV0)
        sc.emit("gpsimd", lambda h: h.dma_start(out=upfq[64:128, :], in_=upfq[0:64, :]),
                waits=[(SV0, sc.v(SV0))], inc=SD0, inck=16)
        sc.emit("gpsimd", lambda h: h.dma_start(out=upfk[64:128, :], in_=upfk[0:64, :]),
                inc=SD0, inck=16)
        sc.run(nc)

    sc = Sched(sems, cnt)
    SM2, SV2 = S(), S()
    with nc.psum_tensor("psF", [128, 2, 512], F32) as psF:
        mmn = 0
        for kind in range(2):
            upf = upfq if kind == 0 else upfk
            eff = wq_eff if kind == 0 else wk_eff
            rot = wq_rot if kind == 0 else wk_rot
            for h0 in range(H):
                for dc0 in range(NDC):
                    w = [(SL, LD_ALL), (SLg, LDG_ALL), (SLm, 1), (SD0, 32)] if mmn == 0 else []
                    if mmn >= 2:
                        w.append((SV2, 2 * (mmn - 1)))
                    lhs = wdT2[64 * (h0 % 2):64 * (h0 % 2) + 64,
                               kind * NDC + h0 // 2, 128 * dc0:128 * dc0 + 128]
                    r0 = 64 * (h0 % 2)
                    sc.emit("tensor",
                            lambda h, l=lhs, u=upf, g=mmn % 2, r=r0: h.matmul(
                                psF[:, g, 0:128], l, u[r:r + 64, :], start=True, stop=True),
                            waits=w, inc=SM2)
                    mm_done = sc.v(SM2)
                    sc.emit("vector",
                            lambda h, e=eff, d=dc0, hh=h0, g=mmn % 2: h.tensor_copy(
                                e[:, d, 64 * hh:64 * hh + 64], psF[:, g, 0:64]),
                            waits=[(SM2, mm_done)], inc=SV2)
                    sc.emit("vector",
                            lambda h, r=rot, d=dc0, hh=h0, g=mmn % 2: h.tensor_copy(
                                r[:, d, 64 * hh:64 * hh + 64], psF[:, g, 64:128]),
                            inc=SV2)
                    mmn += 1
        sc.run(nc)
    st_fold.close()

    st1 = ExitStack()
    t1b = sb(st1, "t1b", [128, 512], F32)
    t2b = sb(st1, "t2b", [128, 512], F32)
    qstage = sb(st1, "qstage", [128, 2, 512], BF16)

    sc = Sched(sems, cnt)
    P1M, P1V, P1D = S(), S(), S()
    with nc.psum_tensor("psP", [128, 2, 512], F32) as psP, \
         nc.psum_tensor("psO", [128, 2, 512], F32) as psO:
        it = 0
        for kind in range(2):
            eff = wq_eff if kind == 0 else wk_eff
            rot = wq_rot if kind == 0 else wk_rot
            dh = d_qh if kind == 0 else d_kh
            for p in range(NPAIR):
                for tc in range(NTC):
                    g = it % 2
                    w = [(SL, LD_ALL), (SLg, LDG_ALL)] if it == 0 else []
                    if it >= 2:
                        w.append((P1V, 3 * (it - 1)))

                    def mms(h, e=eff, r=rot, p0=p, t0=tc, g0=g):
                        for dc0 in range(NDC):
                            h.matmul(psP[:, g0, :], e[:, dc0, 128 * p0:128 * p0 + 128],
                                     xT[:, dc0, 512 * t0:512 * t0 + 512],
                                     start=(dc0 == 0), stop=(dc0 == NDC - 1))
                        for dc0 in range(NDC):
                            ins = h.matmul(psO[:, g0, :], r[:, dc0, 128 * p0:128 * p0 + 128],
                                           xT[:, dc0, 512 * t0:512 * t0 + 512],
                                           start=(dc0 == 0), stop=(dc0 == NDC - 1))
                        return ins
                    sc.emit("tensor", mms, waits=w, inc=P1M)
                    mmv = sc.v(P1M)
                    wv = [(P1M, mmv)]
                    if it >= 2:
                        wv.append((P1D, 16 * (it - 1)))
                    sc.emit("vector",
                            lambda h, t0=tc, g0=g: h.tensor_tensor(
                                t1b[:, :], psP[:, g0, :],
                                cosp[:, 512 * t0:512 * t0 + 512], ALU.mult),
                            waits=wv, inc=P1V)
                    sc.emit("vector",
                            lambda h, t0=tc, g0=g: h.tensor_tensor(
                                t2b[:, :], psO[:, g0, :],
                                sinp[:, 512 * t0:512 * t0 + 512], ALU.mult),
                            inc=P1V)
                    sc.emit("vector",
                            lambda h, g0=g: h.tensor_add(qstage[:, g0, :], t1b[:, :], t2b[:, :]),
                            inc=P1V)
                    vv = sc.v(P1V)
                    sc.emit("sync",
                            lambda h, d=dh, p0=p, t0=tc, g0=g: h.dma_start(
                                out=d[p0, :, 512 * t0:512 * t0 + 512], in_=qstage[:, g0, :]),
                            waits=[(P1V, vv)], inc=P1D, inck=16)
                    it += 1
        sc.run(nc)
    P1D_ALL = cnt[P1D]
    st1.close()

    st2 = ExitStack()
    wvdT = sb(st2, "wvdT_sb", [128, NDC, C], BF16)
    wv_eff = sb(st2, "wv_eff", [128, NDC, C], BF16)
    upv_sb = sb(st2, "upv_sb", [128, 64], BF16)
    vstage = sb(st2, "vstage", [128, 2, H, 65], BF16)

    sc = Sched(sems, cnt)
    P2L, P2M, P2V, P2D, P2Lm = S(), S(), S(), S(), S()
    with nc.psum_tensor("psVF", [128, 2, 512], F32) as psVF, \
         nc.psum_tensor("psV", [128, 2, 1024], F32) as psV:
        sc.emit("gpsimd", lambda h: h.dma_start(out=wvdT[:, :, :], in_=d_wvdT[:, :, :]),
                inc=P2L, inck=16)
        sc.emit("gpsimd", lambda h: h.dma_start(out=upv_sb[0:64, :], in_=d_upv[:, :]),
                inc=P2L, inck=16)
        sc.emit("gpsimd", lambda h: h.dma_start(out=upv_sb[64:128, :], in_=upv_sb[0:64, :]),
                waits=[(P2L, 32)], inc=P2L, inck=16)
        sc.emit("gpsimd", lambda h: h.memset(vstage[:, :, :, :], 1.0), inc=P2Lm)
        ld = sc.v(P2L)
        mmn = 0
        for h0 in range(H):
            for dc0 in range(NDC):
                w = [(P2L, ld), (P2Lm, 1)] if mmn == 0 else []
                if mmn >= 2:
                    w.append((P2V, mmn - 1))
                lhs = wvdT[64 * (h0 % 2):64 * (h0 % 2) + 64, h0 // 2, 128 * dc0:128 * dc0 + 128]
                r0 = 64 * (h0 % 2)
                sc.emit("tensor",
                        lambda h, l=lhs, g=mmn % 2, r=r0: h.matmul(
                            psVF[:, g, 0:64], l, upv_sb[r:r + 64, :], start=True, stop=True),
                        waits=w, inc=P2M)
                sc.emit("vector",
                        lambda h, d=dc0, hh=h0, g=mmn % 2: h.tensor_copy(
                            wv_eff[:, d, 64 * hh:64 * hh + 64], psVF[:, g, 0:64]),
                        waits=[(P2M, sc.v(P2M))], inc=P2V)
                mmn += 1
        vf_done = sc.v(P2V)
        for kt in range(NKT):
            g = kt % 2
            w = [(P2V, vf_done)] if kt < 2 else [(P2V, vf_done + kt - 1)]

            def vmms(h, k0=kt, g0=g):
                for dc0 in range(NDC):
                    for hh in range(2):
                        ins = h.matmul(psV[:, g0, 512 * hh:512 * hh + 512],
                                       xT[:, dc0, 128 * k0:128 * k0 + 128],
                                       wv_eff[:, dc0, 512 * hh:512 * hh + 512],
                                       start=(dc0 == 0), stop=(dc0 == NDC - 1))
                return ins
            sc.emit("tensor", vmms, waits=w, inc=P2M)
            wv = [(P2M, sc.v(P2M))]
            if kt >= 2:
                wv.append((P2D, 16 * (kt - 1)))
            sc.emit("vector",
                    lambda h, g0=g: h.tensor_copy(
                        vstage[:, g0, :, 0:64],
                        psV[:, g0, :].rearrange("p (h d) -> p h d", h=H)),
                    waits=wv, inc=P2V)
            sc.emit("sync",
                    lambda h, k0=kt, g0=g: h.dma_start(out=d_vh[:, k0, :, :],
                                                       in_=vstage[:, g0, :, :]),
                    waits=[(P2V, sc.v(P2V))], inc=P2D, inck=16)
        sc.run(nc)
    P2D_ALL = cnt[P2D]
    st2.close()
    st_qk.close()

    st3 = ExitStack()
    qbuf = sb(st3, "qbuf", [128, 2, T], BF16)
    kbuf = sb(st3, "kbuf", [128, 2, T], BF16)
    vbuf = sb(st3, "vbuf", [128, 2, NKT, 2, 65], BF16)
    p2 = sb(st3, "p2", [128, 2, 1024], BF16)
    ystg = sb(st3, "ystg", [128, 2, 1024], BF16)
    lstg = sb(st3, "lstg", [128, 2, 1024], F32)

    sc = Sched(sems, cnt)
    A_LD, A_ST, A_EXP, A_MSK, A_AV, A_EV, A_YD = S(), S(), S(), S(), S(), S(), S()

    sched3 = []
    for p in range(NPAIR):
        for qc in range(NTC):
            for j in range(4 * qc + 4):
                c0 = max(0, 128 * j - 512 * qc)
                sched3.append((p, qc, j, c0, j == 4 * qc + 3, j >= 4 * qc))
    NJP = len(sched3) // NPAIR

    for p in range(NPAIR):
        w = []
        if p == 0:
            w = [(P1D, P1D_ALL), (P2D, P2D_ALL)]
        if p >= 2:
            w = [(A_AV, NJP * (p - 1))]
        g = p % 2
        sc.emit("sync", lambda h, p0=p, g0=g: h.dma_start(out=qbuf[:, g0, :], in_=d_qh[p0, :, :]),
                waits=w, inc=A_LD, inck=16)
        sc.emit("sync", lambda h, p0=p, g0=g: h.dma_start(out=kbuf[:, g0, :], in_=d_kh[p0, :, :]),
                inc=A_LD, inck=16)
        sc.emit("sync", lambda h, p0=p, g0=g: h.dma_start(
            out=vbuf[:, g0, :, :, :],
            in_=d_vh[:, :, 2 * p0:2 * p0 + 2, :]),
            inc=A_LD, inck=16)

    with nc.psum_tensor("psST", [128, 2, 1024], F32) as psST, \
         nc.psum_tensor("psOT", [128, 2, 1024], F32) as psOT:
        nj = 0
        kidx = 0
        pending_av = None
        for (p, qc, j, c0, last, diag) in sched3:
            g = p % 2
            jg = nj % 2
            w = [(A_LD, 48 * (p + 1))]
            if nj >= 2:
                w.append((A_EXP, nj - 1))

            def stm(h, p0=p, q0=qc, j0=j, cc=c0, g0=g, jg0=jg):
                for hh in range(2):
                    ins = h.matmul(
                        psST[:, jg0, 512 * hh + cc:512 * hh + 512],
                        kbuf[64 * hh:64 * hh + 64, g0, 128 * j0:128 * j0 + 128],
                        qbuf[64 * hh:64 * hh + 64, g0, 512 * q0 + cc:512 * q0 + 512],
                        start=True, stop=True)
                return ins
            sc.emit("tensor", stm, waits=w, inc=A_ST)
            st_done = sc.v(A_ST)

            if pending_av is not None:
                sc.emit("tensor", pending_av[0], waits=pending_av[1], inc=A_AV)
                pending_av = None

            w = [(A_ST, st_done)]
            if nj >= 2:
                w.append((A_AV, nj - 1))

            def expf(h, cc=c0, jg0=jg):
                iv = psST[:, jg0, :].rearrange("p (h q) -> p h q", h=2)[:, :, cc:512]
                ov = p2[:, jg0, :].rearrange("p (h q) -> p h q", h=2)[:, :, cc:512]
                return h.activation(ov, iv, AF.Exp)
            sc.emit("scalar", expf, waits=w, inc=A_EXP)
            exp_done = sc.v(A_EXP)

            if diag:
                def mskf(h, cc=c0, jg0=jg):
                    ov = p2[:, jg0, :].rearrange("p (h q) -> p h q", h=2)[:, :, cc:cc + 128]
                    tv = tri2[:, :].rearrange("p (h q) -> p h q", h=2)
                    return h.tensor_tensor(ov, ov, tv, ALU.mult)
                sc.emit("vector", mskf, waits=[(A_EXP, exp_done)], inc=A_MSK)
            else:
                sc.emit("vector", lambda h: h.engine_nop(),
                        waits=[(A_EXP, exp_done)], inc=A_MSK)
            msk_done = sc.v(A_MSK)

            def avm(h, p0=p, j0=j, cc=c0, last0=last, first=(j == 0), g0=g, jg0=jg,
                    ot=kidx % 2):
                for hh in range(2):
                    ins = h.matmul(
                        psOT[0:65, ot, 512 * hh + cc:512 * hh + 512],
                        vbuf[:, g0, j0, hh, :],
                        p2[:, jg0, 512 * hh + cc:512 * hh + 512],
                        start=first, stop=last0)
                return ins
            avw = [(A_MSK, msk_done)]
            if j == 0 and kidx >= 2:
                avw.append((A_EV, 3 * (kidx - 1)))
            if last:
                sc.emit("tensor", avm, waits=avw, inc=A_AV)
                av_done = sc.v(A_AV)
                row = 2 * kidx
                eg = kidx % 2
                w = [(A_AV, av_done)]
                if kidx >= 2:
                    w.append((A_YD, 48 * (kidx - 1)))
                sc.emit("vector",
                        lambda h, e=eg, ot=kidx % 2: h.tensor_copy(
                            lstg[64:65, e, :], psOT[64:65, ot, :]),
                        waits=w, inc=A_EV)
                sc.emit("vector",
                        lambda h, e=eg, ot=kidx % 2: h.tensor_copy(
                            ystg[0:64, e, 0:512], psOT[0:64, ot, 0:512]),
                        inc=A_EV)
                sc.emit("vector",
                        lambda h, e=eg, ot=kidx % 2: h.tensor_copy(
                            ystg[0:64, e, 512:1024], psOT[0:64, ot, 512:1024]),
                        inc=A_EV)
                ev_here = sc.v(A_EV)
                sc.emit("gpsimd",
                        lambda h, p0=p, q0=qc, e=eg: h.dma_start(
                            out=d_yh[p0, 0:64, 512 * q0:512 * q0 + 512], in_=ystg[0:64, e, 0:512]),
                        waits=[(A_EV, ev_here)], inc=A_YD, inck=16)
                sc.emit("gpsimd",
                        lambda h, p0=p, q0=qc, e=eg: h.dma_start(
                            out=d_yh[p0, 64:128, 512 * q0:512 * q0 + 512], in_=ystg[0:64, e, 512:1024]),
                        inc=A_YD, inck=16)
                sc.emit("gpsimd",
                        lambda h, r=row, e=eg: h.dma_start(
                            out=l_all[r:r + 2, :],
                            in_=lstg[64:65, e, :].rearrange("p (a b) -> p a b", a=2)),
                        inc=A_YD, inck=16)
                kidx += 1
            else:
                pending_av = (avm, avw)
            nj += 1
        sc.emit("vector", lambda h: h.reciprocal(rl_all[:, :], l_all[:, :]),
                waits=[(A_YD, cnt[A_YD])], inc=A_EV)
        sc.run(nc)
    A_YD_ALL = cnt[A_YD]
    G_ALL3 = cnt[A_EV]
    st3.close()

    st_acc = ExitStack()
    acc2 = sb(st_acc, "acc2", [128, 8 * TOWN + 512], F32)
    w_own = sb(st_acc, "w_own", [8, TOWN], F32)
    st_sc = ExitStack()
    scoresT = sb(st_sc, "scoresT", [8, T], F32)
    st4 = ExitStack()
    wc_sb = sb(st4, "wc_sb", [128, NDC, C], BF16)
    rw_sb = sb(st4, "rw_sb", [128, NDC, 8], F32)
    ybuf = sb(st4, "ybuf", [128, 2, 8, 512], BF16)
    ystr = sb(st4, "ystr", [128, 2, 8, 512], BF16)
    xfb = sb(st4, "xfb", [128, 2, 512], F32)
    stag2 = sb(st4, "stag2", [2, 2, 512], F32)
    rbuf = sb(st4, "rbuf", [128, NDC, 512], F32)
    smls = sb(st4, "smls", [1, 3072], F32)
    xpbf = sb(st4, "xpbf", [128, NDC, 512], BF16)
    xpf = sb(st4, "xpf", [128, NDC, 512], F32)

    sc = Sched(sems, cnt)
    L4, M4, V4, S4, D4, G4, L4b, D4s = S(), S(), S(), S(), S(), S(), S(), S()
    with nc.psum_tensor("psRB", [128, 2, 512], F32) as psRB, \
         nc.psum_tensor("psAT", [128, 2, 512], F32) as psAT, \
         nc.psum_tensor("psMS", [1, 1024], F32) as psMS, \
         nc.psum_tensor("psSC", [8, 512], F32) as psSC:
        sc.emit("gpsimd", lambda h: h.dma_start(out=wc_sb[:, :, :], in_=d_wc[:, :, :]),
                inc=L4, inck=16)
        sc.emit("sync", lambda h: h.dma_start(out=rw_sb[:, :, :], in_=d_rwT[:, :, :]),
                inc=D4, inck=16)
        sc.emit("vector", lambda h: h.tensor_scalar_max(t_sb[:, :], t_sb[:, :], 0.1),
                waits=[(SL, LD_ALL)], inc=V4)
        sc.emit("vector", lambda h: h.reciprocal(t_sb[:, :], t_sb[:, :]), inc=V4)
        sc.emit("gpsimd", lambda h: h.partition_broadcast(rtemp[:, :], t_sb[:, :]),
                waits=[(V4, 2)], inc=L4b)
        ld4 = sc.v(L4)

        vnorm_done, resid_done, tc_end_v4, resid_v4, rlb_mm, statm_done = {}, {}, {}, {}, {}, {}
        dbl_prev = [0]
        g4_prev = [0]
        for tc in range(NTC):
            g = tc % 2
            w = [(A_YD, A_YD_ALL)] if tc == 0 else []
            if tc >= 2:
                w.append((V4, vnorm_done[tc - 2]))
            sc.emit("sync", lambda h, t0=tc, g0=g: h.dma_start(
                out=ystr[:, g0, :, :],
                in_=d_yh[:, :, 512 * t0:512 * t0 + 512].transpose([1, 0, 2])),
                waits=w, inc=D4, inck=16)
            ld_here = sc.v(D4)

            for p in range(NPAIR):
                k0 = 2 * (NPAIR * p + tc)
                wg_ = []
                if NPAIR * tc + p >= 2:
                    wg_ = [(M4, rlb_mm.get(NPAIR * tc + p - 2, 0))]
                sc.emit("scalar", lambda h, k=k0, g0=p % 2: h.dma_start(
                    out=stag2[:, g0, :], in_=rl_all[k:k + 2, :]),
                    waits=wg_, inc=D4s, inck=16)
                gat_here = sc.v(D4s)
                w = [(D4s, gat_here), (L4, ld4)]
                if p >= 2:
                    w.append((V4, sc.v(V4) - 1))
                elif tc >= 1:
                    w.append((V4, vnorm_done[tc - 1]))
                sc.emit("tensor", lambda h, g0=p % 2: h.matmul(
                    psRB[:, g0, :], sel2[:, :], stag2[:, g0, :],
                    start=True, stop=True),
                    waits=w, inc=M4)
                rlb_mm[NPAIR * tc + p] = sc.v(M4)
                sc.emit("vector", lambda h, p0=p, g0=p % 2, gg=g: h.tensor_tensor(
                    ybuf[:, gg, p0, :], ystr[:, gg, p0, :], psRB[:, g0, :], ALU.mult),
                    waits=[(M4, sc.v(M4)), (D4, ld_here)], inc=V4)
            vnorm_done[tc] = sc.v(V4)

            for dout in range(NDC):
                gi = 8 * tc + dout
                w = [(V4, resid_v4[gi - 2])] if gi >= 2 else []
                sc.emit("sync", lambda h, t0=tc, d0=dout, g2=gi % 2: h.dma_start(
                    out=xfb[:, g2, :], in_=d_xT[:, d0, 512 * t0:512 * t0 + 512]),
                    waits=w, inc=D4, inck=16)
                xf_here = sc.v(D4)
                w = [(V4, vnorm_done[tc])]
                if dout >= 2:
                    w.append((V4, sc.v(V4)))

                def wcm(h, d0=dout, g0=dout % 2, gg=g):
                    for p in range(NPAIR):
                        ins = h.matmul(psAT[:, g0, :],
                                       wc_sb[:, p, 128 * d0:128 * d0 + 128],
                                       ybuf[:, gg, p, :], start=(p == 0), stop=(p == NPAIR - 1))
                    return ins
                if dout < 2 and tc >= 1:
                    w.append((V4, dbl_prev[0]))
                sc.emit("tensor", wcm, waits=w, inc=M4)
                wrs = [(M4, sc.v(M4)), (D4, xf_here)]
                if dout == 0 and tc >= 1:
                    wrs.append((G4, g4_prev[0]))
                sc.emit("vector", lambda h, d0=dout, g0=dout % 2, g2=gi % 2: h.tensor_tensor(
                    rbuf[:, d0, :], psAT[:, g0, :], xfb[:, g2, :], ALU.add),
                    waits=wrs, inc=V4)
                resid_v4[gi] = sc.v(V4)
            resid_done[tc] = sc.v(V4)

            sc.emit("scalar", lambda h: h.activation(xpbf[:, :, :], rbuf[:, :, :], AF.Square),
                    waits=[(V4, resid_done[tc])], inc=S4)
            sc.emit("vector", lambda h, gg=g: h.tensor_copy(ybuf[:, gg, :, :], rbuf[:, :, :]),
                    waits=[(V4, vnorm_done[tc])], inc=V4)
            rdy = (sc.v(S4), sc.v(V4))

            def statm(h, gg=g):
                for d0 in range(NDC):
                    h.matmul(psMS[0:1, 0:512], ones_b[:, :], ybuf[:, gg, d0, :],
                             start=(d0 == 0), stop=(d0 == NDC - 1))
                for d0 in range(NDC):
                    ins = h.matmul(psMS[0:1, 512:1024], ones_b[:, :], xpbf[:, d0, :],
                                   start=(d0 == 0), stop=(d0 == NDC - 1))
                return ins
            sc.emit("tensor", statm, waits=[(S4, rdy[0]), (V4, rdy[1])], inc=M4)
            stat_done = sc.v(M4)
            statm_done[tc] = stat_done

            def small1(h):
                h.tensor_scalar_mul(smls[0:1, 0:512], psMS[0:1, 0:512], 1.0 / C)
                h.tensor_scalar_mul(smls[0:1, 512:1024], psMS[0:1, 512:1024], 1.0 / C)
                h.tensor_tensor(smls[0:1, 1024:1536], smls[0:1, 0:512], smls[0:1, 0:512], ALU.mult)
                h.tensor_tensor(smls[0:1, 1536:2048], smls[0:1, 512:1024], smls[0:1, 1024:1536], ALU.subtract)
                return h.tensor_scalar_add(smls[0:1, 1536:2048], smls[0:1, 1536:2048], 1e-5)
            sc.emit("vector", small1, waits=[(M4, stat_done)], inc=V4)
            sc.emit("scalar", lambda h: h.activation(smls[0:1, 2048:2560], smls[0:1, 1536:2048],
                                                     AF.Ln),
                    waits=[(V4, sc.v(V4))], inc=S4)
            sc.emit("scalar", lambda h: h.activation(smls[0:1, 2560:3072], smls[0:1, 2048:2560],
                                                     AF.Exp, scale=-0.5),
                    inc=S4)
            sc.emit("vector", lambda h: h.tensor_tensor(
                smls[0:1, 1024:1536], smls[0:1, 0:512], smls[0:1, 2560:3072], ALU.mult),
                waits=[(S4, sc.v(S4))], inc=V4)
            ms_done = sc.v(V4)

            def bcm(h):
                h.matmul(psAT[:, 0, :], ones1f[:, :], smls[0:1, 2560:3072], start=True, stop=True)
                return h.matmul(psAT[:, 1, :], ones1f[:, :], smls[0:1, 1024:1536], start=True, stop=True)
            sc.emit("tensor", bcm, waits=[(V4, ms_done)], inc=M4)
            bc_done = sc.v(M4)

            for dout in range(NDC):
                sc.emit("vector", lambda h, d0=dout: h.tensor_tensor(
                    xpf[:, d0, :], rbuf[:, d0, :], psAT[:, 0, :], ALU.mult),
                    waits=[(M4, bc_done)], inc=V4)
                sc.emit("vector", lambda h, d0=dout: h.tensor_tensor(
                    xpf[:, d0, :], xpf[:, d0, :], psAT[:, 1, :], ALU.subtract),
                    inc=V4)
            xpf_done = sc.v(V4)
            sc.emit("vector", lambda h: h.tensor_scalar_mul(
                rbuf[:, :, :], xpf[:, :, :], 2.0), waits=[], inc=V4)
            dbl_done = sc.v(V4)
            dbl_prev[0] = dbl_done

            def rtm(h):
                for d0 in range(NDC):
                    ins = h.matmul(psSC[:, :], rw_sb[:, d0, :], xpf[:, d0, :],
                                   start=(d0 == 0), stop=(d0 == NDC - 1))
                return ins
            sc.emit("tensor", rtm, waits=[(V4, xpf_done)], inc=M4)
            sc.emit("vector", lambda h, t0=tc: h.tensor_scalar(
                scoresT[:, 512 * t0:512 * t0 + 512], psSC[:, :],
                rb_sb[:, 0:1], rtemp[:, 0:1], ALU.add, ALU.mult),
                waits=[(M4, sc.v(M4)), (L4b, 1)], inc=V4)

            for dout in range(NDC):
                def dyn(h, t0=tc, d0=dout):
                    off = h.value_load(offs_sb[t0:t0 + 1, d0:d0 + 1])
                    return h.dma_start(out=acc2[:, bass.ds(off, 512)], in_=rbuf[:, d0, :])
                sc.emit("gpsimd", dyn, waits=[(V4, dbl_done)], inc=G4, inck=16)
            g4_prev[0] = sc.v(G4)
            tc_end_v4[tc] = sc.v(V4)
        sc.run(nc)
    G4_ALL = cnt[G4]
    st4.close()

    st5 = ExitStack()
    sc_tm = sb(st5, "sc_tm", [128, 256], F32)
    mx32 = sb(st5, "mx32", [128, 32], F32)
    ssum = sb(st5, "ssum", [128, 32], F32)
    ebuf = sb(st5, "ebuf", [128, 256], F32)
    ebf = sb(st5, "ebf", [128, 256], BF16)
    tsb5 = sb(st5, "tsb5", [1, 256], F32)
    lnt8 = sb(st5, "lnt8", [1, 8], F32)
    lnt_b = sb(st5, "lnt_b", [128, 8], F32)
    gbuf = sb(st5, "gbuf", [128, 256], F32)
    eqb = sb(st5, "eqb", [128, 256], F32)
    g2b = sb(st5, "g2b", [128, 256], F32)
    v1b = sb(st5, "v1b", [128, 32], F32)
    v2b = sb(st5, "v2b", [128, 32], F32)
    w_tm = sb(st5, "w_tm", [128, 256], F32)
    w_em = sb(st5, "w_em", [8, T], F32)

    sc = Sched(sems, cnt)
    M5, V5, S5, G5, G5d = S(), S(), S(), S(), S()
    v3 = lambda x: x[:, :].rearrange("p (j e) -> p j e", e=8)

    with nc.psum_tensor("psT5", [128, 256], F32) as psT5, \
         nc.psum_tensor("psS5", [1, 256], F32) as psS5, \
         nc.psum_tensor("psW5", [8, 512], F32) as psW5:
        def trs(h):
            for j in range(NKT):
                ins = h.transpose(psT5[:, 8 * j:8 * j + 8],
                                  scoresT[0:8, 128 * j:128 * j + 128],
                                  ident_f[0:8, 0:8])
            return ins
        sc.emit("tensor", trs, inc=M5)
        sc.emit("vector", lambda h: h.tensor_copy(sc_tm[:, :], psT5[:, :]),
                waits=[(M5, sc.v(M5))], inc=V5)

        tsb_done = {}
        for it in range(3):
            sc.emit("vector", lambda h: h.reduce_max(mx32[:, :], v3(sc_tm), axis=AX.X),
                    inc=V5)
            for e0 in range(8):
                sc.emit("vector", lambda h, e=e0: h.tensor_tensor(
                    v3(sc_tm)[:, :, e], v3(sc_tm)[:, :, e], mx32[:, :], ALU.subtract),
                    inc=V5)
            sc.emit("scalar", lambda h: h.activation(ebuf[:, :], sc_tm[:, :], AF.Exp),
                    waits=[(V5, sc.v(V5))], inc=S5)
            sc.emit("vector", lambda h: h.reduce_sum(ssum[:, :], v3(ebuf), axis=AX.X),
                    waits=[(S5, sc.v(S5))], inc=V5)
            sc.emit("scalar", lambda h: h.activation(ssum[:, :], ssum[:, :], AF.Ln),
                    waits=[(V5, sc.v(V5))], inc=S5)
            for e0 in range(8):
                sc.emit("vector", lambda h, e=e0: h.tensor_tensor(
                    v3(sc_tm)[:, :, e], v3(sc_tm)[:, :, e], ssum[:, :], ALU.subtract),
                    waits=[(S5, sc.v(S5))] if e0 == 0 else [], inc=V5)
            sc.emit("scalar", lambda h: h.activation(ebf[:, :], sc_tm[:, :], AF.Exp),
                    waits=[(V5, sc.v(V5))], inc=S5)
            sc.emit("tensor", lambda h: h.matmul(psS5[:, :], ones_b[:, :], ebf[:, :],
                                                 start=True, stop=True),
                    waits=[(S5, sc.v(S5))] + ([(V5, tsb_done[it - 1])] if it > 0 else []),
                    inc=M5)
            sc.emit("vector", lambda h: h.tensor_copy(tsb5[:, :], psS5[:, :]),
                    waits=[(M5, sc.v(M5))], inc=V5)
            tsb_done[it] = sc.v(V5)
            sc.emit("vector", lambda h: h.reduce_sum(
                lnt8[:, :], tsb5[:, :].rearrange("p (j e) -> p e j", e=8), axis=AX.X),
                inc=V5)
            sc.emit("scalar", lambda h: h.activation(lnt8[:, :], lnt8[:, :], AF.Ln),
                    waits=[(V5, sc.v(V5))], inc=S5)
            sc.emit("gpsimd", lambda h: h.partition_broadcast(lnt_b[:, :], lnt8[:, :]),
                    waits=[(S5, sc.v(S5))], inc=G5)
            for e0 in range(8):
                sc.emit("vector", lambda h, e=e0: h.tensor_scalar_sub(
                    v3(sc_tm)[:, :, e], v3(sc_tm)[:, :, e], lnt_b[:, e:e + 1]),
                    waits=[(G5, sc.v(G5))] if e0 == 0 else [], inc=V5)

        sc.emit("scalar", lambda h: h.activation(gbuf[:, :], sc_tm[:, :], AF.Exp),
                waits=[(V5, sc.v(V5))], inc=S5)
        sc.emit("vector", lambda h: h.reduce_sum(ssum[:, :], v3(gbuf), axis=AX.X),
                waits=[(S5, sc.v(S5))], inc=V5)
        sc.emit("vector", lambda h: h.tensor_scalar_add(ssum[:, :], ssum[:, :], 1e-8), inc=V5)
        sc.emit("vector", lambda h: h.reciprocal(ssum[:, :], ssum[:, :]), inc=V5)
        rec_done = sc.v(V5)
        for e0 in range(8):
            sc.emit("vector", lambda h, e=e0: h.tensor_tensor(
                v3(gbuf)[:, :, e], v3(gbuf)[:, :, e], ssum[:, :], ALU.mult),
                waits=[(V5, rec_done)] if e0 == 0 else [], inc=V5)
        sc.emit("vector", lambda h: h.reduce_max(v1b[:, :], v3(gbuf), axis=AX.X), inc=V5)
        for e0 in range(8):
            sc.emit("vector", lambda h, e=e0: h.tensor_tensor(
                v3(eqb)[:, :, e], v3(gbuf)[:, :, e], v1b[:, :], ALU.is_equal), inc=V5)
        for e0 in range(8):
            sc.emit("vector", lambda h, e=e0: h.scalar_tensor_tensor(
                v3(g2b)[:, :, e], v3(eqb)[:, :, e], -1e30, v3(gbuf)[:, :, e],
                ALU.mult, ALU.add), inc=V5)
        sc.emit("vector", lambda h: h.reduce_max(v2b[:, :], v3(g2b), axis=AX.X), inc=V5)
        sc.emit("vector", lambda h: h.tensor_tensor(v1b[:, :], v1b[:, :], v2b[:, :], ALU.add),
                inc=V5)
        sc.emit("vector", lambda h: h.tensor_scalar_add(v1b[:, :], v1b[:, :], 1e-8), inc=V5)
        sc.emit("vector", lambda h: h.reciprocal(v1b[:, :], v1b[:, :]), inc=V5)
        rec2_done = sc.v(V5)
        for e0 in range(8):
            sc.emit("vector", lambda h, e=e0: h.tensor_tensor(
                v3(eqb)[:, :, e], v3(gbuf)[:, :, e], v2b[:, :], ALU.is_ge),
                waits=[(V5, rec2_done)] if e0 == 0 else [], inc=V5)
        for e0 in range(8):
            sc.emit("vector", lambda h, e=e0: h.tensor_tensor(
                v3(w_tm)[:, :, e], v3(gbuf)[:, :, e], v3(eqb)[:, :, e], ALU.mult), inc=V5)
        for e0 in range(8):
            sc.emit("vector", lambda h, e=e0: h.tensor_tensor(
                v3(w_tm)[:, :, e], v3(w_tm)[:, :, e], v1b[:, :], ALU.mult), inc=V5)
        wtm_done = sc.v(V5)

        for c in range(8):
            w = [(V5, wtm_done if c == 0 else sc.v(V5))]

            def trb(h, c0=c):
                for jj in range(4):
                    j = 4 * c0 + jj
                    ins = h.transpose(psW5[:, 128 * jj:128 * jj + 128],
                                      w_tm[:, 8 * j:8 * j + 8], ident_f[:, :])
                return ins
            sc.emit("tensor", trb, waits=w, inc=M5)
            sc.emit("vector", lambda h, c0=c: h.tensor_copy(
                w_em[:, 512 * c0:512 * c0 + 512], psW5[:, :]),
                waits=[(M5, sc.v(M5))], inc=V5)

        def wslice(h):
            off = h.value_load(soff_sb[0:1, 0:1])
            return h.dma_start(out=w_own[:, :], in_=w_em[:, bass.ds(off, TOWN)])
        sc.emit("gpsimd", wslice, waits=[(V5, sc.v(V5))], inc=G5d, inck=16)
        if _DBG_SCRATCH:
            sc.emit("sync", lambda h: h.dma_start(out=d_dbg5[:, :], in_=w_tm[:, :]),
                    waits=[(G5d, sc.v(G5d))], inc=G5d, inck=16)
            sc.emit("sync", lambda h: h.dma_start(out=d_dbg6[:, :], in_=w_em[:, :]),
                    inc=G5d, inck=16)
            sc.emit("sync", lambda h: h.dma_start(out=d_dbg7[:, :], in_=sc_tm[:, :]),
                    inc=G5d, inck=16)
            sc.emit("sync", lambda h: h.dma_start(out=d_dbg8[:, :], in_=gbuf[:, :]),
                    inc=G5d, inck=16)
            sc.emit("sync", lambda h: h.dma_start(out=d_dbg1[:, :], in_=l_all[:, :]),
                    inc=G5d, inck=16)
            sc.emit("sync", lambda h: h.dma_start(out=d_dbg2[:, :], in_=scoresT[:, :]),
                    inc=G5d, inck=16)
            sc.emit("sync", lambda h: h.dma_start(out=d_dbg3[:, :], in_=w_own[:, :]),
                    inc=G5d, inck=16)
            sc.emit("sync", lambda h: None, waits=[(G5d, sc.v(G5d))])
        sc.run(nc)
    G5_ALL = cnt[G5d]
    st5.close()
    st_sc.close()

    st6 = ExitStack()
    xpo = sb(st6, "xpo", [128, NDC, TOWN], BF16)
    gpw = sb(st6, "gpw", [128, 8, TOWN], BF16)
    grelu = sb(st6, "grelu", [128, TOWN], BF16)
    wgp_f = sb(st6, "wgp_f", [128, 2, NDC, 128], F32)
    wgp_r = sb(st6, "wgp_r", [128, 2, 2, NDC, 128], BF16)
    wo_f = sb(st6, "wo_f", [128, 2, 8, 128], F32)
    wo_r = sb(st6, "wo_r", [128, 2, 8, 128], BF16)
    wbc = sb(st6, "wbc", [128, TOWN], F32)
    wstag = sb(st6, "wstag", [1, TOWN], F32)
    wbcb = sb(st6, "wbcb", [128, 2, TOWN], BF16)

    sc = Sched(sems, cnt)
    L6, M6, V6, S6, G6, G6d, C6 = S(), S(), S(), S(), S(), S(), S()

    with nc.psum_tensor("psG", [128, 1024], F32) as psG, \
         nc.psum_tensor("psP6", [128, 1024], F32) as psP6, \
         nc.psum_tensor("psO6", [128, 2, 1024], F32) as psO6:
        if _DBG_SCRATCH:
            DBG9 = S()
            sc.emit("sync", lambda h: h.dma_start(out=d_dbg9[:, :], in_=acc2[:, :]),
                    waits=[(G4, G4_ALL)], inc=DBG9, inck=16)
        sc.emit("vector", lambda h: h.tensor_scalar_mul(
            xpo[:, :, :], acc2[:, 0:8 * TOWN].rearrange("p (d t) -> p d t", d=NDC), 0.5),
            waits=[(G4, G4_ALL)], inc=V6)
        xpo_done = sc.v(V6)
        if _DBG_SCRATCH:
            sc.emit("sync", lambda h: h.dma_start(out=d_dbg10[:, :, :], in_=xpo[:, :, :]),
                    waits=[(V6, xpo_done)], inc=DBG9, inck=16)
            sc.emit("sync", lambda h: None, waits=[(DBG9, 32)])

        om_last = [0]
        acc_stt = {}
        cast_cnt = {}
        kcast = {}
        wocast = {}
        nfc = 0
        nth = 0
        ndo = 0
        wgld, wold, wbcd, gpend, grelu_rd = {}, {}, {}, {}, {}
        relu_cnt = {}
        stt_cnt = {}
        for e0 in range(E):
            w = [(G5d, G5_ALL)]
            if e0 >= 2:
                w.append((V6, gpend[e0 - 2]))
            sc.emit("gpsimd", lambda h, e=e0: h.dma_start(
                out=wstag[:, :], in_=w_own[e:e + 1, :]), waits=w, inc=G6d, inck=16)
            sc.emit("gpsimd", lambda h: h.partition_broadcast(
                wbc[:, :], wstag[:, :]), waits=[(G6d, sc.v(G6d))], inc=G6)
            sc.emit("vector", lambda h, e=e0: h.tensor_copy(wbcb[:, e % 2, :], wbc[:, :]),
                    waits=[(G6, sc.v(G6))], inc=V6)
            wbcd[e0] = sc.v(V6)

            for half in range(2):
                for f8 in range(8):
                    fc = 8 * half + f8
                    for kd in range(2):
                        nk = 2 * nfc + kd
                        w = [(C6, kcast.get(nk - 2, 0))] if nk >= 2 else []
                        sc.emit("sync", lambda h, e=e0, f=fc, k=kd, g=nk % 2: h.dma_start(
                            out=wgp_f[:, g, :, :], in_=d_wgp[e, f, :, k, :, :]),
                            waits=w, inc=L6, inck=16)
                        wc_ = [(L6, sc.v(L6))]
                        if nfc >= 2:
                            wc_.append((M6, wgld[nfc - 2]))
                        sc.emit("scalar", lambda h, k=kd, g=nk % 2, gr=nfc % 2: h.copy(
                            wgp_r[:, gr, k, :, :], wgp_f[:, g, :, :]), waits=wc_, inc=C6)
                        kcast[nk] = sc.v(C6)
                    cast_cnt[nfc] = sc.v(C6)
                    ldv = None

                    for th in range(2):
                        w = [(C6, cast_cnt[nfc]), (V6, xpo_done)]
                        if nth >= 1:
                            w.append((S6, relu_cnt[nth - 1]))

                        def gm(h, g=nfc % 2, t1=th):
                            for d0 in range(NDC):
                                for t0 in range(2):
                                    ins = h.matmul(
                                        psG[:, 512 * t0:512 * t0 + 512],
                                        wgp_r[:, g, 0, d0, :],
                                        xpo[:, d0, 1024 * t1 + 512 * t0:1024 * t1 + 512 * t0 + 512],
                                        start=(d0 == 0), stop=(d0 == NDC - 1))
                            return ins
                        sc.emit("tensor", gm, waits=w, inc=M6)
                        wr = [(M6, sc.v(M6))]
                        if nth >= 1:
                            wr.append((V6, grelu_rd[nth - 1]))
                        sc.emit("scalar", lambda h, t1=th: h.activation(
                            grelu[:, 1024 * t1:1024 * t1 + 1024], psG[:, :], AF.Relu),
                            waits=wr, inc=S6)
                        relu_cnt[nth] = sc.v(S6)

                        w = []
                        if nth >= 1:
                            w.append((V6, stt_cnt[nth - 1]))

                        def pm(h, g=nfc % 2, t1=th):
                            for d0 in range(NDC):
                                for t0 in range(2):
                                    ins = h.matmul(
                                        psP6[:, 512 * t0:512 * t0 + 512],
                                        wgp_r[:, g, 1, d0, :],
                                        xpo[:, d0, 1024 * t1 + 512 * t0:1024 * t1 + 512 * t0 + 512],
                                        start=(d0 == 0), stop=(d0 == NDC - 1))
                            return ins
                        sc.emit("tensor", pm, waits=w, inc=M6)
                        pm_done = sc.v(M6)
                        if th == 1:
                            wgld[nfc] = pm_done

                        w = [(M6, pm_done), (S6, relu_cnt[nth]), (V6, wbcd[e0])]
                        if half == 1 or e0 > 0:
                            w.append((M6, om_last[0]))
                        sc.emit("vector", lambda h, f=f8, t1=th: h.scalar_tensor_tensor(
                            gpw[:, f, 1024 * t1:1024 * t1 + 1024], psP6[:, :], 1.0,
                            grelu[:, 1024 * t1:1024 * t1 + 1024], ALU.mult, ALU.mult),
                            waits=w, inc=V6)
                        sc.emit("vector", lambda h, f=f8, e=e0, t1=th: h.tensor_tensor(
                            gpw[:, f, 1024 * t1:1024 * t1 + 1024],
                            gpw[:, f, 1024 * t1:1024 * t1 + 1024],
                            wbcb[:, e % 2, 1024 * t1:1024 * t1 + 1024], ALU.mult),
                            inc=V6)
                        stt_cnt[nth] = sc.v(V6)
                        grelu_rd[nth] = sc.v(V6)
                        nth += 1
                    nfc += 1
                gpw_done = sc.v(V6)

                for dout in range(NDC):
                    w = [(C6, wocast.get(ndo // 2 - 2, 0))] if ndo >= 4 else []
                    sc.emit("sync", lambda h, e=e0, hf=half, d0=dout, g=(ndo // 2) % 2: h.dma_start(
                        out=wo_f[:, g, :, :], in_=d_wo[e, hf, d0, :, :, :]),
                        waits=w, inc=L6, inck=16)
                    wc_ = [(L6, sc.v(L6))]
                    if ndo >= 4:
                        wc_.append((M6, wold[ndo // 2 - 2]))
                    sc.emit("scalar", lambda h, g=(ndo // 2) % 2: h.copy(
                        wo_r[:, g, :, :], wo_f[:, g, :, :]), waits=wc_, inc=C6)
                    wocast[ndo // 2] = sc.v(C6)
                    for th in range(2):
                        w = [(C6, wocast[ndo // 2]), (V6, gpw_done)]
                        if ndo >= 2:
                            w.append((V6, acc_stt[ndo - 2]))

                        def om(h, g=(ndo // 2) % 2, og=ndo % 2, t1=th):
                            for f in range(8):
                                for t0 in range(2):
                                    ins = h.matmul(
                                        psO6[:, og, 512 * t0:512 * t0 + 512],
                                        wo_r[:, g, f, :],
                                        gpw[:, f, 1024 * t1 + 512 * t0:1024 * t1 + 512 * t0 + 512],
                                        start=(f == 0), stop=(f == 7))
                            return ins
                        sc.emit("tensor", om, waits=w, inc=M6)
                        om_last[0] = sc.v(M6)
                        if th == 1:
                            wold[ndo // 2] = sc.v(M6)
                        sc.emit("vector", lambda h, d0=dout, og=ndo % 2, t1=th: h.scalar_tensor_tensor(
                            acc2[:, TOWN * d0 + 1024 * t1:TOWN * d0 + 1024 * t1 + 1024],
                            psO6[:, og, :], 1.0,
                            acc2[:, TOWN * d0 + 1024 * t1:TOWN * d0 + 1024 * t1 + 1024],
                            ALU.mult, ALU.add),
                            waits=[(M6, sc.v(M6))], inc=V6)
                        acc_stt[ndo] = sc.v(V6)
                        ndo += 1
            gpend[e0] = sc.v(V6)
        sc.run(nc)
    st6.close()

    st7 = ExitStack()
    rbf7 = sb(st7, "rbf7", [128, NDC, 512], BF16)
    sqb7 = sb(st7, "sqb7", [128, NDC, 512], BF16)
    sml7 = sb(st7, "sml7", [1, 3072], F32)
    ost = sb(st7, "ost", [128, 2, NDC, 512], F32)

    sc = Sched(sems, cnt)
    M7, V7, S7, D7 = S(), S(), S(), S()
    statm7_done = {}
    if _DBG_SCRATCH:
        scd = Sched(sems, cnt)
        DD7 = S()
        scd.emit("sync", lambda h: h.dma_start(out=d_dbg4[:, :], in_=acc2[:, :]),
                 inc=DD7, inck=16)
        scd.emit("sync", lambda h: None, waits=[(DD7, 16)])
        scd.run(nc)
    with nc.psum_tensor("psM7", [1, 1024], F32) as psM7, \
         nc.psum_tensor("psB7", [128, 2, 512], F32) as psB7:
        accv = acc2[:, 0:8 * TOWN].rearrange("p (d t) -> p d t", d=NDC)
        for tc in range(4):
            av = accv[:, :, 512 * tc:512 * tc + 512]
            sc.emit("scalar", lambda h, a=av: h.activation(sqb7[:, :, :], a, AF.Square),
                    inc=S7)
            sc.emit("vector", lambda h, a=av: h.tensor_copy(rbf7[:, :, :], a),
                    waits=[(M7, statm7_done.get(tc - 1, 0))], inc=V7)
            rdy = (sc.v(S7), sc.v(V7))

            def statm7(h):
                for d0 in range(NDC):
                    h.matmul(psM7[0:1, 0:512], ones_b[:, :], rbf7[:, d0, :],
                             start=(d0 == 0), stop=(d0 == NDC - 1))
                for d0 in range(NDC):
                    ins = h.matmul(psM7[0:1, 512:1024], ones_b[:, :], sqb7[:, d0, :],
                                   start=(d0 == 0), stop=(d0 == NDC - 1))
                return ins
            sc.emit("tensor", statm7, waits=[(S7, rdy[0]), (V7, rdy[1])], inc=M7)
            statm7_done[tc] = sc.v(M7)

            def small7(h):
                h.tensor_scalar_mul(sml7[0:1, 0:512], psM7[0:1, 0:512], 1.0 / C)
                h.tensor_scalar_mul(sml7[0:1, 512:1024], psM7[0:1, 512:1024], 1.0 / C)
                h.tensor_tensor(sml7[0:1, 1024:1536], sml7[0:1, 0:512], sml7[0:1, 0:512], ALU.mult)
                h.tensor_tensor(sml7[0:1, 1536:2048], sml7[0:1, 512:1024], sml7[0:1, 1024:1536], ALU.subtract)
                h.tensor_scalar_add(sml7[0:1, 1536:2048], sml7[0:1, 1536:2048], 1e-5)
                return h.reciprocal(sml7[0:1, 2048:2560], sml7[0:1, 1536:2048])
            sc.emit("vector", small7, waits=[(M7, sc.v(M7))], inc=V7)
            sc.emit("scalar", lambda h: h.activation(sml7[0:1, 2560:3072], sml7[0:1, 2048:2560],
                                                     AF.Sqrt),
                    waits=[(V7, sc.v(V7))], inc=S7)
            sc.emit("vector", lambda h: h.tensor_tensor(
                sml7[0:1, 1024:1536], sml7[0:1, 0:512], sml7[0:1, 2560:3072], ALU.mult),
                waits=[(S7, sc.v(S7))], inc=V7)

            def bcm7(h):
                h.matmul(psB7[:, 0, :], ones1f[:, :], sml7[0:1, 2560:3072], start=True, stop=True)
                return h.matmul(psB7[:, 1, :], ones1f[:, :], sml7[0:1, 1024:1536], start=True, stop=True)
            sc.emit("tensor", bcm7, waits=[(V7, sc.v(V7))], inc=M7)
            bc7 = sc.v(M7)

            g = tc % 2
            for dout in range(NDC):
                w = [(M7, bc7)]
                if tc >= 2 and dout == 0:
                    w.append((D7, 16 * (tc - 1)))
                sc.emit("vector", lambda h, a=av, d0=dout, g0=g: h.tensor_tensor(
                    ost[:, g0, d0, :], a[:, d0, :], psB7[:, 0, :], ALU.mult),
                    waits=w, inc=V7)
                sc.emit("vector", lambda h, d0=dout, g0=g: h.tensor_tensor(
                    ost[:, g0, d0, :], ost[:, g0, d0, :], psB7[:, 1, :], ALU.subtract),
                    inc=V7)
            sc.emit("sync", lambda h, t0=tc, g0=g: h.dma_start(
                out=d_yout[:, :, 512 * t0:512 * t0 + 512], in_=ost[:, g0, :, :]),
                waits=[(V7, sc.v(V7))], inc=D7, inck=16)
        sc.emit("sync", lambda h: None, waits=[(D7, sc.v(D7))])
        sc.run(nc)
    st7.close()
    st_acc.close()

    nc.finalize()
    _CACHE["nc"] = nc
    return nc


def _host_prep(inputs):
    x = np.asarray(inputs["L0_attn_x"], np.float32)
    ac = np.ascontiguousarray
    r3 = lambda a: ac(a.reshape(NDC, 128, -1).transpose(1, 0, 2))

    cp, sp = _rope_tables()
    tri = np.triu(np.ones((128, 128), np.float32))
    sel = np.zeros((2, 128), np.float32)
    sel[0, 0:64] = 1.0
    sel[1, 64:128] = 1.0

    wg = np.asarray(inputs["L0_ffn_wg"], np.float32)
    wp = np.asarray(inputs["L0_ffn_wp"], np.float32)
    wo = np.asarray(inputs["L0_ffn_wo"], np.float32)
    wgs = wg.reshape(E, NDC, 128, 16, 128)
    wps = wp.reshape(E, NDC, 128, 16, 128)
    wgp = ac(np.stack([wgs, wps], axis=1).transpose(0, 4, 3, 1, 2, 5))
    wo_h = ac(wo.reshape(E, 2, 8, 128, 8, 128).transpose(0, 1, 4, 3, 2, 5))

    common = {
        "wqdT": r3(np.asarray(inputs["L0_attn_Wq_down"], np.float32).T),
        "wkdT": r3(np.asarray(inputs["L0_attn_Wk_down"], np.float32).T),
        "wvdT": r3(np.asarray(inputs["L0_attn_Wv_down"], np.float32).T),
        "upqc": np.asarray(inputs["L0_attn_Wq_up_c"], np.float32),
        "upqe": np.asarray(inputs["L0_attn_Wq_up_e"], np.float32),
        "upqeT": ac(np.asarray(inputs["L0_attn_Wq_up_e"], np.float32).T),
        "upkc": np.asarray(inputs["L0_attn_Wk_up_c"], np.float32),
        "upke": np.asarray(inputs["L0_attn_Wk_up_e"], np.float32),
        "upkeT": ac(np.asarray(inputs["L0_attn_Wk_up_e"], np.float32).T),
        "upv": np.asarray(inputs["L0_attn_Wv_up"], np.float32),
        "Pm": _perm_matrix(),
        "wc": r3(np.asarray(inputs["L0_attn_Wc"], np.float32)),
        "rwT": r3(np.asarray(inputs["L0_ffn_router_w"], np.float32).T),
        "rb": ac(np.asarray(inputs["L0_ffn_router_b"], np.float32).reshape(8, 1)),
        "temp": ac(np.asarray(inputs["L0_ffn_temperature"], np.float32).reshape(1, 1)),
        "wgp": wgp,
        "wo": wo_h,
        "cosp": cp,
        "sinp": sp,
        "tri2": ac(np.concatenate([tri, tri], axis=1)),
        "ident": np.eye(128, dtype=np.float32),
        "sel2": sel,
    }
    xts = [r3(ac(x[b].T)) for b in range(B)]
    offs, soffs = [], []
    for s in range(2):
        o = np.full((NTC, 8), GARB, np.int32)
        for tc in range(NTC):
            if 4 * s <= tc < 4 * (s + 1):
                for dout in range(NDC):
                    o[tc, dout] = TOWN * dout + 512 * (tc - 4 * s)
        offs.append(o)
        soffs.append(np.array([[TOWN * s]], np.int32))
    return (common, xts, offs, soffs)


def kernel(**inputs):
    common, xts, offs, soffs = _host_prep(inputs)
    nc = build_nc()
    in_maps = []
    for c in range(8):
        b, s = c // 2, c % 2
        m = dict(common)
        m["xT"] = xts[b]
        m["offs"] = offs[s]
        m["soff"] = soffs[s]
        in_maps.append(m)
    res = run_bass_kernel_spmd(nc, in_maps, core_ids=list(range(8)))
    out = np.empty((B, T, C), np.float32)
    for c in range(8):
        b, s = c // 2, c % 2
        y = np.asarray(res.results[c]["yout"])
        out[b, TOWN * s:TOWN * (s + 1), :] = y.transpose(1, 0, 2).reshape(C, TOWN).T
    return out


# version 06
# speedup vs baseline: 1.2954x; 1.0745x over previous
import numpy as np
from contextlib import ExitStack

import concourse.bacc as bacc
import concourse.bass as bass
import concourse.mybir as mybir
from concourse.bass_utils import run_bass_kernel_spmd

F32 = mybir.dt.float32
BF16 = mybir.dt.bfloat16
I32 = mybir.dt.int32
AF = mybir.ActivationFunctionType
ALU = mybir.AluOpType
AX = mybir.AxisListType

B, T, C = 4, 4096, 1024
H, DH, DHE = 16, 64, 32
NPAIR = 8
E, FF = 8, 2048
TOWN = 2048
NDC = C // 128
NTC = T // 512
NKT = T // 128
GARB = 8 * TOWN

ENGINES = ("tensor", "vector", "scalar", "gpsimd", "sync")


class Sched:

    def __init__(self, sems, cnt):
        self.sems = sems
        self.cnt = cnt
        self.ops = {e: [] for e in ENGINES}

    def emit(self, eng, fn, waits=(), inc=None, inck=1):
        self.ops[eng].append((fn, tuple(waits), inc, inck))
        if inc is not None:
            self.cnt[inc] += inck

    def v(self, si):
        return self.cnt[si]

    def run(self, nc):
        sems = self.sems
        with nc.Block() as blk:
            for eng in ENGINES:
                lst = self.ops[eng]
                if not lst:
                    continue

                def make(lst):
                    def body(h):
                        for (fn, waits, inc, inck) in lst:
                            for (si, val) in waits:
                                if val > 0:
                                    h.wait_ge(sems[si], val)
                            ins = fn(h)
                            if inc is not None:
                                ins.then_inc(sems[inc], inck)
                    return body
                getattr(blk, eng)(make(lst))


def _rope_tables():
    inv = 1.0 / (10000.0 ** (np.arange(0, DHE, 2, dtype=np.float64) / DHE))
    freqs = np.arange(T, dtype=np.float64)[:, None] * inv[None, :]
    emb = np.concatenate([freqs, freqs], axis=1)
    cos = np.cos(emb).astype(np.float32).T
    sin = np.sin(emb).astype(np.float32).T
    cp = np.ones((128, T), np.float32)
    sp = np.zeros((128, T), np.float32)
    cp[32:64] = cos
    cp[96:128] = cos
    sp[32:64] = sin
    sp[96:128] = sin
    return cp, sp


def _perm_matrix():
    P = np.zeros((DHE, DHE), np.float32)
    for i in range(DHE // 2):
        P[2 * i + 1, 2 * i] = -1.0
        P[2 * i, 2 * i + 1] = 1.0
    return P


_CACHE = {}
_DBG_SCRATCH = False


def build_nc():
    if "nc" in _CACHE:
        return _CACHE["nc"]
    nc = bacc.Bacc("TRN2", target_bir_lowering=False, detect_race_conditions=False)

    dt = lambda n, s, d, k: nc.dram_tensor(n, s, d, kind=k).ap()
    d_xT = dt("xT", [128, NDC, T], F32, "ExternalInput")
    d_wqdT = dt("wqdT", [128, NDC, C], F32, "ExternalInput")
    d_wkdT = dt("wkdT", [128, NDC, C], F32, "ExternalInput")
    d_wvdT = dt("wvdT", [128, NDC, C], F32, "ExternalInput")
    d_upqc = dt("upqc", [64, 32], F32, "ExternalInput")
    d_upqe = dt("upqe", [64, 32], F32, "ExternalInput")
    d_upqeT = dt("upqeT", [32, 64], F32, "ExternalInput")
    d_upkc = dt("upkc", [64, 32], F32, "ExternalInput")
    d_upke = dt("upke", [64, 32], F32, "ExternalInput")
    d_upkeT = dt("upkeT", [32, 64], F32, "ExternalInput")
    d_upv = dt("upv", [64, 64], F32, "ExternalInput")
    d_P = dt("Pm", [32, 32], F32, "ExternalInput")
    d_wc = dt("wc", [128, NDC, C], F32, "ExternalInput")
    d_rwT = dt("rwT", [128, NDC, 8], F32, "ExternalInput")
    d_rb = dt("rb", [8, 1], F32, "ExternalInput")
    d_temp = dt("temp", [1, 1], F32, "ExternalInput")
    d_wgp = dt("wgp", [E, 16, 128, 2, NDC, 128], F32, "ExternalInput")
    d_wo = dt("wo", [E, 2, 8, 128, 8, 128], F32, "ExternalInput")
    d_cos = dt("cosp", [128, T], F32, "ExternalInput")
    d_sin = dt("sinp", [128, T], F32, "ExternalInput")
    d_tri = dt("tri2", [128, 256], F32, "ExternalInput")
    d_ident = dt("ident", [128, 128], F32, "ExternalInput")
    d_sel2 = dt("sel2", [2, 128], F32, "ExternalInput")
    d_offs = dt("offs", [NTC, 8], I32, "ExternalInput")
    d_soff = dt("soff", [1, 1], I32, "ExternalInput")

    d_yout = dt("yout", [128, 8, TOWN], F32, "ExternalOutput")

    _knd = "ExternalOutput" if _DBG_SCRATCH else "Internal"
    d_qh = dt("qh", [NPAIR, 128, T], BF16, _knd)
    d_kh = dt("kh", [NPAIR, 128, T], BF16, _knd)
    d_vh = dt("vh", [128, NKT, H, 65], BF16, _knd)
    d_yh = dt("yh", [NPAIR, 128, T], BF16, _knd)
    if _DBG_SCRATCH:
        d_dbg1 = dt("dbg1", [128, 512], F32, "ExternalOutput")
        d_dbg2 = dt("dbg2", [8, T], F32, "ExternalOutput")
        d_dbg3 = dt("dbg3", [8, TOWN], F32, "ExternalOutput")
        d_dbg4 = dt("dbg4", [128, 8 * TOWN + 512], F32, "ExternalOutput")
        d_dbg5 = dt("dbg5", [128, 256], F32, "ExternalOutput")
        d_dbg6 = dt("dbg6", [8, T], F32, "ExternalOutput")
        d_dbg7 = dt("dbg7", [128, 256], F32, "ExternalOutput")
        d_dbg8 = dt("dbg8", [128, 256], F32, "ExternalOutput")
        d_dbg9 = dt("dbg9", [128, 8 * TOWN + 512], F32, "ExternalOutput")
        d_dbg10 = dt("dbg10", [128, NDC, TOWN], BF16, "ExternalOutput")
        d_dbg11 = dt("dbg11", [128, 8, TOWN], BF16, "ExternalOutput")

    es = ExitStack()
    nc._keepalive = es
    sems = [es.enter_context(nc.semaphore(f"sm{i}")) for i in range(64)]
    cnt = {i: 0 for i in range(64)}
    _ns = [0]

    def S():
        i = _ns[0]
        _ns[0] += 1
        return i

    sb = lambda st, name, shape, d: st.enter_context(nc.sbuf_tensor(name, shape, d))

    ident_f = sb(es, "ident_f", [128, 128], F32)
    ones_b = sb(es, "ones_b", [128, 1], BF16)
    ones1f = sb(es, "ones1f", [1, 128], F32)
    sel2 = sb(es, "sel2s", [2, 128], F32)
    tri2 = sb(es, "tri2s", [128, 256], BF16)
    l_all = sb(es, "l_all", [128, 512], F32)
    rl_all = sb(es, "rl_all", [128, 512], F32)
    rb_sb = sb(es, "rb_sb", [8, 1], F32)
    rtemp = sb(es, "rtemp", [8, 1], F32)
    t_sb = sb(es, "t_sb", [1, 1], F32)
    offs_sb = sb(es, "offs_sb", [NTC, 8], I32)
    soff_sb = sb(es, "soff_sb", [1, 1], I32)

    st_qk = ExitStack()
    xT = sb(st_qk, "xT_sb", [128, NDC, T], BF16)
    cosp = sb(st_qk, "cosp_sb", [128, T], BF16)
    sinp = sb(st_qk, "sinp_sb", [128, T], BF16)
    wq_eff = sb(st_qk, "wq_eff", [128, NDC, C], BF16)
    wq_rot = sb(st_qk, "wq_rot", [128, NDC, C], BF16)
    wk_eff = sb(st_qk, "wk_eff", [128, NDC, C], BF16)
    wk_rot = sb(st_qk, "wk_rot", [128, NDC, C], BF16)
    st_fold = ExitStack()
    wdT2 = sb(st_fold, "wdT2", [128, 2 * NDC, C], BF16)
    upfq = sb(st_fold, "upfq", [128, 128], BF16)
    upfk = sb(st_fold, "upfk", [128, 128], BF16)
    upeT2 = sb(st_fold, "upeT2", [32, 2, 64], BF16)
    P_sb = sb(st_fold, "P_sb", [32, 32], BF16)

    sc = Sched(sems, cnt)
    SL, SLg, SLm = S(), S(), S()
    for dst, src in [(ident_f, d_ident), (sel2, d_sel2), (rb_sb, d_rb), (t_sb, d_temp),
                     (offs_sb, d_offs), (soff_sb, d_soff)]:
        sc.emit("sync", lambda h, a=dst, b=src: h.dma_start(out=a[:, :], in_=b[:, :]),
                inc=SL, inck=16)
    for dst, src in [(tri2, d_tri), (cosp, d_cos), (sinp, d_sin), (P_sb, d_P)]:
        sc.emit("gpsimd", lambda h, a=dst, b=src: h.dma_start(out=a[:, :], in_=b[:, :]),
                inc=SLg, inck=16)
    for (dst, c0, c1, src) in [(upfq, 0, 32, d_upqc), (upfq, 32, 64, d_upqe),
                               (upfk, 0, 32, d_upkc), (upfk, 32, 64, d_upke)]:
        sc.emit("gpsimd", lambda h, a=dst, x0=c0, x1=c1, b=src: h.dma_start(
            out=a[0:64, x0:x1], in_=b[:, :]), inc=SLg, inck=16)
    sc.emit("gpsimd", lambda h: h.dma_start(out=upeT2[:, 0, :], in_=d_upqeT[:, :]),
            inc=SLg, inck=16)
    sc.emit("gpsimd", lambda h: h.dma_start(out=upeT2[:, 1, :], in_=d_upkeT[:, :]),
            inc=SLg, inck=16)
    sc.emit("gpsimd", lambda h: h.dma_start(out=xT[:, :, :], in_=d_xT[:, :, :]),
            inc=SLg, inck=16)
    sc.emit("gpsimd", lambda h: h.dma_start(out=wdT2[:, 0:NDC, :], in_=d_wqdT[:, :, :]),
            inc=SLg, inck=16)
    sc.emit("gpsimd", lambda h: h.dma_start(out=wdT2[:, NDC:2 * NDC, :], in_=d_wkdT[:, :, :]),
            inc=SLg, inck=16)
    sc.emit("gpsimd", lambda h: h.memset(ones_b[:, :], 1.0))
    sc.emit("gpsimd", lambda h: h.memset(ones1f[:, :], 1.0))
    sc.emit("gpsimd", lambda h: h.memset(upfq[:, 64:96], 0.0))
    sc.emit("gpsimd", lambda h: h.memset(upfk[:, 64:96], 0.0), inc=SLm, inck=1)
    sc.run(nc)
    LD_ALL = sc.v(SL)
    LDG_ALL = sc.v(SLg)

    sc = Sched(sems, cnt)
    SM0, SV0, SD0 = S(), S(), S()
    with nc.psum_tensor("psR", [64, 2, 512], F32) as psR:
        for kk in range(2):
            sc.emit("tensor",
                    lambda h, k=kk: h.matmul(psR[:, k, 0:32], upeT2[:, k, :], P_sb[:, :],
                                             start=True, stop=True),
                    waits=[(SL, LD_ALL), (SLg, LDG_ALL), (SLm, 1)], inc=SM0)
        sc.emit("vector", lambda h: h.tensor_copy(upfq[0:64, 96:128], psR[:, 0, 0:32]),
                waits=[(SM0, 1)], inc=SV0)
        sc.emit("vector", lambda h: h.tensor_copy(upfk[0:64, 96:128], psR[:, 1, 0:32]),
                waits=[(SM0, 2)], inc=SV0)
        sc.emit("vector", lambda h: h.tensor_scalar_mul(upfq[0:64, :], upfq[0:64, :], 0.125),
                inc=SV0)
        sc.emit("gpsimd", lambda h: h.dma_start(out=upfq[64:128, :], in_=upfq[0:64, :]),
                waits=[(SV0, sc.v(SV0))], inc=SD0, inck=16)
        sc.emit("gpsimd", lambda h: h.dma_start(out=upfk[64:128, :], in_=upfk[0:64, :]),
                inc=SD0, inck=16)
        sc.run(nc)

    sc = Sched(sems, cnt)
    SM2, SV2 = S(), S()
    with nc.psum_tensor("psF", [128, 2, 512], F32) as psF:
        mmn = 0
        for kind in range(2):
            upf = upfq if kind == 0 else upfk
            eff = wq_eff if kind == 0 else wk_eff
            rot = wq_rot if kind == 0 else wk_rot
            for h0 in range(H):
                for dc0 in range(NDC):
                    w = [(SL, LD_ALL), (SLg, LDG_ALL), (SLm, 1), (SD0, 32)] if mmn == 0 else []
                    if mmn >= 2:
                        w.append((SV2, 2 * (mmn - 1)))
                    lhs = wdT2[64 * (h0 % 2):64 * (h0 % 2) + 64,
                               kind * NDC + h0 // 2, 128 * dc0:128 * dc0 + 128]
                    r0 = 64 * (h0 % 2)
                    sc.emit("tensor",
                            lambda h, l=lhs, u=upf, g=mmn % 2, r=r0: h.matmul(
                                psF[:, g, 0:128], l, u[r:r + 64, :], start=True, stop=True),
                            waits=w, inc=SM2)
                    mm_done = sc.v(SM2)
                    sc.emit("vector",
                            lambda h, e=eff, d=dc0, hh=h0, g=mmn % 2: h.tensor_copy(
                                e[:, d, 64 * hh:64 * hh + 64], psF[:, g, 0:64]),
                            waits=[(SM2, mm_done)], inc=SV2)
                    sc.emit("vector",
                            lambda h, r=rot, d=dc0, hh=h0, g=mmn % 2: h.tensor_copy(
                                r[:, d, 64 * hh:64 * hh + 64], psF[:, g, 64:128]),
                            inc=SV2)
                    mmn += 1
        sc.run(nc)
    st_fold.close()

    st1 = ExitStack()
    t1b = sb(st1, "t1b", [128, 512], F32)
    t2b = sb(st1, "t2b", [128, 512], F32)
    qstage = sb(st1, "qstage", [128, 2, 512], BF16)

    sc = Sched(sems, cnt)
    P1M, P1V, P1D = S(), S(), S()
    with nc.psum_tensor("psP", [128, 2, 512], F32) as psP, \
         nc.psum_tensor("psO", [128, 2, 512], F32) as psO:
        it = 0
        for kind in range(2):
            eff = wq_eff if kind == 0 else wk_eff
            rot = wq_rot if kind == 0 else wk_rot
            dh = d_qh if kind == 0 else d_kh
            for p in range(NPAIR):
                for tc in range(NTC):
                    g = it % 2
                    w = [(SL, LD_ALL), (SLg, LDG_ALL)] if it == 0 else []
                    if it >= 2:
                        w.append((P1V, 3 * (it - 1)))

                    def mms(h, e=eff, r=rot, p0=p, t0=tc, g0=g):
                        for dc0 in range(NDC):
                            h.matmul(psP[:, g0, :], e[:, dc0, 128 * p0:128 * p0 + 128],
                                     xT[:, dc0, 512 * t0:512 * t0 + 512],
                                     start=(dc0 == 0), stop=(dc0 == NDC - 1))
                        for dc0 in range(NDC):
                            ins = h.matmul(psO[:, g0, :], r[:, dc0, 128 * p0:128 * p0 + 128],
                                           xT[:, dc0, 512 * t0:512 * t0 + 512],
                                           start=(dc0 == 0), stop=(dc0 == NDC - 1))
                        return ins
                    sc.emit("tensor", mms, waits=w, inc=P1M)
                    mmv = sc.v(P1M)
                    wv = [(P1M, mmv)]
                    if it >= 2:
                        wv.append((P1D, 16 * (it - 1)))
                    sc.emit("vector",
                            lambda h, t0=tc, g0=g: h.tensor_tensor(
                                t1b[:, :], psP[:, g0, :],
                                cosp[:, 512 * t0:512 * t0 + 512], ALU.mult),
                            waits=wv, inc=P1V)
                    sc.emit("vector",
                            lambda h, t0=tc, g0=g: h.tensor_tensor(
                                t2b[:, :], psO[:, g0, :],
                                sinp[:, 512 * t0:512 * t0 + 512], ALU.mult),
                            inc=P1V)
                    sc.emit("vector",
                            lambda h, g0=g: h.tensor_add(qstage[:, g0, :], t1b[:, :], t2b[:, :]),
                            inc=P1V)
                    vv = sc.v(P1V)
                    sc.emit("sync",
                            lambda h, d=dh, p0=p, t0=tc, g0=g: h.dma_start(
                                out=d[p0, :, 512 * t0:512 * t0 + 512], in_=qstage[:, g0, :]),
                            waits=[(P1V, vv)], inc=P1D, inck=16)
                    it += 1
        sc.run(nc)
    P1D_ALL = cnt[P1D]
    st1.close()

    st2 = ExitStack()
    wvdT = sb(st2, "wvdT_sb", [128, NDC, C], BF16)
    wv_eff = sb(st2, "wv_eff", [128, NDC, C], BF16)
    upv_sb = sb(st2, "upv_sb", [128, 64], BF16)
    vstage = sb(st2, "vstage", [128, 2, H, 65], BF16)

    sc = Sched(sems, cnt)
    P2L, P2M, P2V, P2D, P2Lm = S(), S(), S(), S(), S()
    with nc.psum_tensor("psVF", [128, 2, 512], F32) as psVF, \
         nc.psum_tensor("psV", [128, 2, 1024], F32) as psV:
        sc.emit("gpsimd", lambda h: h.dma_start(out=wvdT[:, :, :], in_=d_wvdT[:, :, :]),
                inc=P2L, inck=16)
        sc.emit("gpsimd", lambda h: h.dma_start(out=upv_sb[0:64, :], in_=d_upv[:, :]),
                inc=P2L, inck=16)
        sc.emit("gpsimd", lambda h: h.dma_start(out=upv_sb[64:128, :], in_=upv_sb[0:64, :]),
                waits=[(P2L, 32)], inc=P2L, inck=16)
        sc.emit("gpsimd", lambda h: h.memset(vstage[:, :, :, :], 1.0), inc=P2Lm)
        ld = sc.v(P2L)
        mmn = 0
        for h0 in range(H):
            for dc0 in range(NDC):
                w = [(P2L, ld), (P2Lm, 1)] if mmn == 0 else []
                if mmn >= 2:
                    w.append((P2V, mmn - 1))
                lhs = wvdT[64 * (h0 % 2):64 * (h0 % 2) + 64, h0 // 2, 128 * dc0:128 * dc0 + 128]
                r0 = 64 * (h0 % 2)
                sc.emit("tensor",
                        lambda h, l=lhs, g=mmn % 2, r=r0: h.matmul(
                            psVF[:, g, 0:64], l, upv_sb[r:r + 64, :], start=True, stop=True),
                        waits=w, inc=P2M)
                sc.emit("vector",
                        lambda h, d=dc0, hh=h0, g=mmn % 2: h.tensor_copy(
                            wv_eff[:, d, 64 * hh:64 * hh + 64], psVF[:, g, 0:64]),
                        waits=[(P2M, sc.v(P2M))], inc=P2V)
                mmn += 1
        vf_done = sc.v(P2V)
        for kt in range(NKT):
            g = kt % 2
            w = [(P2V, vf_done)] if kt < 2 else [(P2V, vf_done + kt - 1)]

            def vmms(h, k0=kt, g0=g):
                for dc0 in range(NDC):
                    for hh in range(2):
                        ins = h.matmul(psV[:, g0, 512 * hh:512 * hh + 512],
                                       xT[:, dc0, 128 * k0:128 * k0 + 128],
                                       wv_eff[:, dc0, 512 * hh:512 * hh + 512],
                                       start=(dc0 == 0), stop=(dc0 == NDC - 1))
                return ins
            sc.emit("tensor", vmms, waits=w, inc=P2M)
            wv = [(P2M, sc.v(P2M))]
            if kt >= 2:
                wv.append((P2D, 16 * (kt - 1)))
            sc.emit("vector",
                    lambda h, g0=g: h.tensor_copy(
                        vstage[:, g0, :, 0:64],
                        psV[:, g0, :].rearrange("p (h d) -> p h d", h=H)),
                    waits=wv, inc=P2V)
            sc.emit("sync",
                    lambda h, k0=kt, g0=g: h.dma_start(out=d_vh[:, k0, :, :],
                                                       in_=vstage[:, g0, :, :]),
                    waits=[(P2V, sc.v(P2V))], inc=P2D, inck=16)
        sc.run(nc)
    P2D_ALL = cnt[P2D]
    st2.close()
    st_qk.close()

    st3 = ExitStack()
    qbuf = sb(st3, "qbuf", [128, 2, T], BF16)
    kbuf = sb(st3, "kbuf", [128, 2, T], BF16)
    vbuf = sb(st3, "vbuf", [128, 2, NKT, 2, 65], BF16)
    p2 = sb(st3, "p2", [128, 2, 1024], BF16)
    ystg = sb(st3, "ystg", [128, 2, 1024], BF16)
    lstg = sb(st3, "lstg", [128, 2, 1024], F32)

    sc = Sched(sems, cnt)
    A_LD, A_ST, A_EXP, A_MSK, A_AV, A_EV, A_YD = S(), S(), S(), S(), S(), S(), S()

    sched3 = []
    for p in range(NPAIR):
        for qc in range(NTC):
            for j in range(4 * qc + 4):
                c0 = max(0, 128 * j - 512 * qc)
                sched3.append((p, qc, j, c0, j == 4 * qc + 3, j >= 4 * qc))
    NJP = len(sched3) // NPAIR

    for p in range(NPAIR):
        w = []
        if p == 0:
            w = [(P1D, P1D_ALL), (P2D, P2D_ALL)]
        if p >= 2:
            w = [(A_AV, NJP * (p - 1))]
        g = p % 2
        sc.emit("sync", lambda h, p0=p, g0=g: h.dma_start(out=qbuf[:, g0, :], in_=d_qh[p0, :, :]),
                waits=w, inc=A_LD, inck=16)
        sc.emit("sync", lambda h, p0=p, g0=g: h.dma_start(out=kbuf[:, g0, :], in_=d_kh[p0, :, :]),
                inc=A_LD, inck=16)
        sc.emit("sync", lambda h, p0=p, g0=g: h.dma_start(
            out=vbuf[:, g0, :, :, :],
            in_=d_vh[:, :, 2 * p0:2 * p0 + 2, :]),
            inc=A_LD, inck=16)

    with nc.psum_tensor("psST", [128, 2, 1024], F32) as psST, \
         nc.psum_tensor("psOT", [128, 2, 1024], F32) as psOT:
        nj = 0
        kidx = 0
        pending_av = None
        for (p, qc, j, c0, last, diag) in sched3:
            g = p % 2
            jg = nj % 2
            w = [(A_LD, 48 * (p + 1))]
            if nj >= 2:
                w.append((A_EXP, nj - 1))

            def stm(h, p0=p, q0=qc, j0=j, cc=c0, g0=g, jg0=jg):
                for hh in range(2):
                    ins = h.matmul(
                        psST[:, jg0, 512 * hh + cc:512 * hh + 512],
                        kbuf[64 * hh:64 * hh + 64, g0, 128 * j0:128 * j0 + 128],
                        qbuf[64 * hh:64 * hh + 64, g0, 512 * q0 + cc:512 * q0 + 512],
                        start=True, stop=True)
                return ins
            sc.emit("tensor", stm, waits=w, inc=A_ST)
            st_done = sc.v(A_ST)

            if pending_av is not None:
                sc.emit("tensor", pending_av[0], waits=pending_av[1], inc=A_AV)
                pending_av = None

            w = [(A_ST, st_done)]
            if nj >= 2:
                w.append((A_AV, nj - 1))

            def expf(h, cc=c0, jg0=jg):
                iv = psST[:, jg0, :].rearrange("p (h q) -> p h q", h=2)[:, :, cc:512]
                ov = p2[:, jg0, :].rearrange("p (h q) -> p h q", h=2)[:, :, cc:512]
                return h.activation(ov, iv, AF.Exp)
            sc.emit("scalar", expf, waits=w, inc=A_EXP)
            exp_done = sc.v(A_EXP)

            if diag:
                def mskf(h, cc=c0, jg0=jg):
                    ov = p2[:, jg0, :].rearrange("p (h q) -> p h q", h=2)[:, :, cc:cc + 128]
                    tv = tri2[:, :].rearrange("p (h q) -> p h q", h=2)
                    return h.tensor_tensor(ov, ov, tv, ALU.mult)
                sc.emit("vector", mskf, waits=[(A_EXP, exp_done)], inc=A_MSK)
            else:
                sc.emit("vector", lambda h: h.engine_nop(),
                        waits=[(A_EXP, exp_done)], inc=A_MSK)
            msk_done = sc.v(A_MSK)

            def avm(h, p0=p, j0=j, cc=c0, last0=last, first=(j == 0), g0=g, jg0=jg,
                    ot=kidx % 2):
                for hh in range(2):
                    ins = h.matmul(
                        psOT[0:65, ot, 512 * hh + cc:512 * hh + 512],
                        vbuf[:, g0, j0, hh, :],
                        p2[:, jg0, 512 * hh + cc:512 * hh + 512],
                        start=first, stop=last0)
                return ins
            avw = [(A_MSK, msk_done)]
            if j == 0 and kidx >= 2:
                avw.append((A_EV, 3 * (kidx - 1)))
            if last:
                sc.emit("tensor", avm, waits=avw, inc=A_AV)
                av_done = sc.v(A_AV)
                row = 2 * kidx
                eg = kidx % 2
                w = [(A_AV, av_done)]
                if kidx >= 2:
                    w.append((A_YD, 48 * (kidx - 1)))
                sc.emit("vector",
                        lambda h, e=eg, ot=kidx % 2: h.tensor_copy(
                            lstg[64:65, e, :], psOT[64:65, ot, :]),
                        waits=w, inc=A_EV)
                sc.emit("vector",
                        lambda h, e=eg, ot=kidx % 2: h.tensor_copy(
                            ystg[0:64, e, 0:512], psOT[0:64, ot, 0:512]),
                        inc=A_EV)
                sc.emit("vector",
                        lambda h, e=eg, ot=kidx % 2: h.tensor_copy(
                            ystg[0:64, e, 512:1024], psOT[0:64, ot, 512:1024]),
                        inc=A_EV)
                ev_here = sc.v(A_EV)
                sc.emit("gpsimd",
                        lambda h, p0=p, q0=qc, e=eg: h.dma_start(
                            out=d_yh[p0, 0:64, 512 * q0:512 * q0 + 512], in_=ystg[0:64, e, 0:512]),
                        waits=[(A_EV, ev_here)], inc=A_YD, inck=16)
                sc.emit("gpsimd",
                        lambda h, p0=p, q0=qc, e=eg: h.dma_start(
                            out=d_yh[p0, 64:128, 512 * q0:512 * q0 + 512], in_=ystg[0:64, e, 512:1024]),
                        inc=A_YD, inck=16)
                sc.emit("gpsimd",
                        lambda h, r=row, e=eg: h.dma_start(
                            out=l_all[r:r + 2, :],
                            in_=lstg[64:65, e, :].rearrange("p (a b) -> p a b", a=2)),
                        inc=A_YD, inck=16)
                kidx += 1
            else:
                pending_av = (avm, avw)
            nj += 1
        sc.emit("vector", lambda h: h.reciprocal(rl_all[:, :], l_all[:, :]),
                waits=[(A_YD, cnt[A_YD])], inc=A_EV)
        sc.run(nc)
    A_YD_ALL = cnt[A_YD]
    G_ALL3 = cnt[A_EV]
    st3.close()

    st_acc = ExitStack()
    acc2 = sb(st_acc, "acc2", [128, 8 * TOWN + 512], F32)
    w_own = sb(st_acc, "w_own", [8, TOWN], F32)
    st_sc = ExitStack()
    scoresT = sb(st_sc, "scoresT", [8, T], F32)
    st4 = ExitStack()
    wc_sb = sb(st4, "wc_sb", [128, NDC, C], BF16)
    rw_sb = sb(st4, "rw_sb", [128, NDC, 8], F32)
    ybuf = sb(st4, "ybuf", [128, 2, 8, 512], BF16)
    ystr = sb(st4, "ystr", [128, 2, 8, 512], BF16)
    xfb = sb(st4, "xfb", [128, 2, 512], F32)
    stag2 = sb(st4, "stag2", [2, 2, 512], F32)
    rbuf = sb(st4, "rbuf", [128, NDC, 512], F32)
    smls = sb(st4, "smls", [1, 3072], F32)
    xpbf = sb(st4, "xpbf", [128, NDC, 512], BF16)
    xpf = sb(st4, "xpf", [128, NDC, 512], F32)

    sc = Sched(sems, cnt)
    L4, M4, V4, S4, D4, G4, L4b, D4s = S(), S(), S(), S(), S(), S(), S(), S()
    with nc.psum_tensor("psRB", [128, 2, 512], F32) as psRB, \
         nc.psum_tensor("psAT", [128, 2, 512], F32) as psAT, \
         nc.psum_tensor("psMS", [1, 1024], F32) as psMS, \
         nc.psum_tensor("psSC", [8, 512], F32) as psSC:
        sc.emit("gpsimd", lambda h: h.dma_start(out=wc_sb[:, :, :], in_=d_wc[:, :, :]),
                inc=L4, inck=16)
        sc.emit("sync", lambda h: h.dma_start(out=rw_sb[:, :, :], in_=d_rwT[:, :, :]),
                inc=D4, inck=16)
        sc.emit("vector", lambda h: h.tensor_scalar_max(t_sb[:, :], t_sb[:, :], 0.1),
                waits=[(SL, LD_ALL)], inc=V4)
        sc.emit("vector", lambda h: h.reciprocal(t_sb[:, :], t_sb[:, :]), inc=V4)
        sc.emit("gpsimd", lambda h: h.partition_broadcast(rtemp[:, :], t_sb[:, :]),
                waits=[(V4, 2)], inc=L4b)
        ld4 = sc.v(L4)

        vnorm_done, resid_done, tc_end_v4, resid_v4, rlb_mm, statm_done = {}, {}, {}, {}, {}, {}
        dbl_prev = [0]
        g4_prev = [0]
        for tc in range(NTC):
            g = tc % 2
            w = [(A_YD, A_YD_ALL)] if tc == 0 else []
            if tc >= 2:
                w.append((V4, vnorm_done[tc - 2]))
            sc.emit("sync", lambda h, t0=tc, g0=g: h.dma_start(
                out=ystr[:, g0, :, :],
                in_=d_yh[:, :, 512 * t0:512 * t0 + 512].transpose([1, 0, 2])),
                waits=w, inc=D4, inck=16)
            ld_here = sc.v(D4)

            for p in range(NPAIR):
                k0 = 2 * (NPAIR * p + tc)
                wg_ = []
                if NPAIR * tc + p >= 2:
                    wg_ = [(M4, rlb_mm.get(NPAIR * tc + p - 2, 0))]
                sc.emit("scalar", lambda h, k=k0, g0=p % 2: h.dma_start(
                    out=stag2[:, g0, :], in_=rl_all[k:k + 2, :]),
                    waits=wg_, inc=D4s, inck=16)
                gat_here = sc.v(D4s)
                w = [(D4s, gat_here), (L4, ld4)]
                if p >= 2:
                    w.append((V4, sc.v(V4) - 1))
                elif tc >= 1:
                    w.append((V4, vnorm_done[tc - 1]))
                sc.emit("tensor", lambda h, g0=p % 2: h.matmul(
                    psRB[:, g0, :], sel2[:, :], stag2[:, g0, :],
                    start=True, stop=True),
                    waits=w, inc=M4)
                rlb_mm[NPAIR * tc + p] = sc.v(M4)
                sc.emit("vector", lambda h, p0=p, g0=p % 2, gg=g: h.tensor_tensor(
                    ybuf[:, gg, p0, :], ystr[:, gg, p0, :], psRB[:, g0, :], ALU.mult),
                    waits=[(M4, sc.v(M4)), (D4, ld_here)], inc=V4)
            vnorm_done[tc] = sc.v(V4)

            for dout in range(NDC):
                gi = 8 * tc + dout
                w = [(V4, resid_v4[gi - 2])] if gi >= 2 else []
                sc.emit("sync", lambda h, t0=tc, d0=dout, g2=gi % 2: h.dma_start(
                    out=xfb[:, g2, :], in_=d_xT[:, d0, 512 * t0:512 * t0 + 512]),
                    waits=w, inc=D4, inck=16)
                xf_here = sc.v(D4)
                w = [(V4, vnorm_done[tc])]
                if dout >= 2:
                    w.append((V4, sc.v(V4)))

                def wcm(h, d0=dout, g0=dout % 2, gg=g):
                    for p in range(NPAIR):
                        ins = h.matmul(psAT[:, g0, :],
                                       wc_sb[:, p, 128 * d0:128 * d0 + 128],
                                       ybuf[:, gg, p, :], start=(p == 0), stop=(p == NPAIR - 1))
                    return ins
                if dout < 2 and tc >= 1:
                    w.append((V4, dbl_prev[0]))
                sc.emit("tensor", wcm, waits=w, inc=M4)
                wrs = [(M4, sc.v(M4)), (D4, xf_here)]
                if dout == 0 and tc >= 1:
                    wrs.append((G4, g4_prev[0]))
                sc.emit("vector", lambda h, d0=dout, g0=dout % 2, g2=gi % 2: h.tensor_tensor(
                    rbuf[:, d0, :], psAT[:, g0, :], xfb[:, g2, :], ALU.add),
                    waits=wrs, inc=V4)
                resid_v4[gi] = sc.v(V4)
            resid_done[tc] = sc.v(V4)

            sc.emit("scalar", lambda h: h.activation(xpbf[:, :, :], rbuf[:, :, :], AF.Square),
                    waits=[(V4, resid_done[tc])], inc=S4)
            sc.emit("vector", lambda h, gg=g: h.tensor_copy(ybuf[:, gg, :, :], rbuf[:, :, :]),
                    waits=[(V4, vnorm_done[tc])], inc=V4)
            rdy = (sc.v(S4), sc.v(V4))

            def statm(h, gg=g):
                for d0 in range(NDC):
                    h.matmul(psMS[0:1, 0:512], ones_b[:, :], ybuf[:, gg, d0, :],
                             start=(d0 == 0), stop=(d0 == NDC - 1))
                for d0 in range(NDC):
                    ins = h.matmul(psMS[0:1, 512:1024], ones_b[:, :], xpbf[:, d0, :],
                                   start=(d0 == 0), stop=(d0 == NDC - 1))
                return ins
            sc.emit("tensor", statm, waits=[(S4, rdy[0]), (V4, rdy[1])], inc=M4)
            stat_done = sc.v(M4)
            statm_done[tc] = stat_done

            def small1(h):
                h.tensor_scalar_mul(smls[0:1, 0:512], psMS[0:1, 0:512], 1.0 / C)
                h.tensor_scalar_mul(smls[0:1, 512:1024], psMS[0:1, 512:1024], 1.0 / C)
                h.tensor_tensor(smls[0:1, 1024:1536], smls[0:1, 0:512], smls[0:1, 0:512], ALU.mult)
                h.tensor_tensor(smls[0:1, 1536:2048], smls[0:1, 512:1024], smls[0:1, 1024:1536], ALU.subtract)
                return h.tensor_scalar_add(smls[0:1, 1536:2048], smls[0:1, 1536:2048], 1e-5)
            sc.emit("vector", small1, waits=[(M4, stat_done)], inc=V4)
            sc.emit("scalar", lambda h: h.activation(smls[0:1, 2048:2560], smls[0:1, 1536:2048],
                                                     AF.Ln),
                    waits=[(V4, sc.v(V4))], inc=S4)
            sc.emit("scalar", lambda h: h.activation(smls[0:1, 2560:3072], smls[0:1, 2048:2560],
                                                     AF.Exp, scale=-0.5),
                    inc=S4)
            sc.emit("vector", lambda h: h.tensor_tensor(
                smls[0:1, 1024:1536], smls[0:1, 0:512], smls[0:1, 2560:3072], ALU.mult),
                waits=[(S4, sc.v(S4))], inc=V4)
            ms_done = sc.v(V4)

            def bcm(h):
                h.matmul(psAT[:, 0, :], ones1f[:, :], smls[0:1, 2560:3072], start=True, stop=True)
                return h.matmul(psAT[:, 1, :], ones1f[:, :], smls[0:1, 1024:1536], start=True, stop=True)
            sc.emit("tensor", bcm, waits=[(V4, ms_done)], inc=M4)
            bc_done = sc.v(M4)

            for dout in range(NDC):
                sc.emit("vector", lambda h, d0=dout: h.tensor_tensor(
                    xpf[:, d0, :], rbuf[:, d0, :], psAT[:, 0, :], ALU.mult),
                    waits=[(M4, bc_done)], inc=V4)
                sc.emit("vector", lambda h, d0=dout: h.tensor_tensor(
                    xpf[:, d0, :], xpf[:, d0, :], psAT[:, 1, :], ALU.subtract),
                    inc=V4)
            xpf_done = sc.v(V4)
            sc.emit("vector", lambda h: h.tensor_scalar_mul(
                rbuf[:, :, :], xpf[:, :, :], 2.0), waits=[], inc=V4)
            dbl_done = sc.v(V4)
            dbl_prev[0] = dbl_done

            def rtm(h):
                for d0 in range(NDC):
                    ins = h.matmul(psSC[:, :], rw_sb[:, d0, :], xpf[:, d0, :],
                                   start=(d0 == 0), stop=(d0 == NDC - 1))
                return ins
            sc.emit("tensor", rtm, waits=[(V4, xpf_done)], inc=M4)
            sc.emit("vector", lambda h, t0=tc: h.tensor_scalar(
                scoresT[:, 512 * t0:512 * t0 + 512], psSC[:, :],
                rb_sb[:, 0:1], rtemp[:, 0:1], ALU.add, ALU.mult),
                waits=[(M4, sc.v(M4)), (L4b, 1)], inc=V4)

            for dout in range(NDC):
                def dyn(h, t0=tc, d0=dout):
                    off = h.value_load(offs_sb[t0:t0 + 1, d0:d0 + 1])
                    return h.dma_start(out=acc2[:, bass.ds(off, 512)], in_=rbuf[:, d0, :])
                sc.emit("gpsimd", dyn, waits=[(V4, dbl_done)], inc=G4, inck=16)
            g4_prev[0] = sc.v(G4)
            tc_end_v4[tc] = sc.v(V4)
        sc.run(nc)
    G4_ALL = cnt[G4]
    st4.close()

    st5 = ExitStack()
    sc_tm = sb(st5, "sc_tm", [128, 256], F32)
    mx32 = sb(st5, "mx32", [128, 32], F32)
    ssum = sb(st5, "ssum", [128, 32], F32)
    ebuf = sb(st5, "ebuf", [128, 256], F32)
    ebf = sb(st5, "ebf", [128, 256], BF16)
    tsb5 = sb(st5, "tsb5", [1, 256], F32)
    lnt8 = sb(st5, "lnt8", [1, 8], F32)
    lnt_b = sb(st5, "lnt_b", [128, 8], F32)
    gbuf = sb(st5, "gbuf", [128, 256], F32)
    eqb = sb(st5, "eqb", [128, 256], F32)
    g2b = sb(st5, "g2b", [128, 256], F32)
    v1b = sb(st5, "v1b", [128, 32], F32)
    v2b = sb(st5, "v2b", [128, 32], F32)
    w_tm = sb(st5, "w_tm", [128, 256], F32)
    w_em = sb(st5, "w_em", [8, T], F32)

    sc = Sched(sems, cnt)
    M5, V5, S5, G5, G5d = S(), S(), S(), S(), S()
    v3 = lambda x: x[:, :].rearrange("p (j e) -> p j e", e=8)

    with nc.psum_tensor("psT5", [128, 256], F32) as psT5, \
         nc.psum_tensor("psS5", [1, 256], F32) as psS5, \
         nc.psum_tensor("psW5", [8, 512], F32) as psW5:
        def trs(h):
            for j in range(NKT):
                ins = h.transpose(psT5[:, 8 * j:8 * j + 8],
                                  scoresT[0:8, 128 * j:128 * j + 128],
                                  ident_f[0:8, 0:8])
            return ins
        sc.emit("tensor", trs, inc=M5)
        sc.emit("vector", lambda h: h.tensor_copy(sc_tm[:, :], psT5[:, :]),
                waits=[(M5, sc.v(M5))], inc=V5)

        tsb_done = {}
        for it in range(3):
            sc.emit("vector", lambda h: h.reduce_max(mx32[:, :], v3(sc_tm), axis=AX.X),
                    inc=V5)
            for e0 in range(8):
                sc.emit("vector", lambda h, e=e0: h.tensor_tensor(
                    v3(sc_tm)[:, :, e], v3(sc_tm)[:, :, e], mx32[:, :], ALU.subtract),
                    inc=V5)
            sc.emit("scalar", lambda h: h.activation(ebuf[:, :], sc_tm[:, :], AF.Exp),
                    waits=[(V5, sc.v(V5))], inc=S5)
            sc.emit("vector", lambda h: h.reduce_sum(ssum[:, :], v3(ebuf), axis=AX.X),
                    waits=[(S5, sc.v(S5))], inc=V5)
            sc.emit("scalar", lambda h: h.activation(ssum[:, :], ssum[:, :], AF.Ln),
                    waits=[(V5, sc.v(V5))], inc=S5)
            for e0 in range(8):
                sc.emit("vector", lambda h, e=e0: h.tensor_tensor(
                    v3(sc_tm)[:, :, e], v3(sc_tm)[:, :, e], ssum[:, :], ALU.subtract),
                    waits=[(S5, sc.v(S5))] if e0 == 0 else [], inc=V5)
            sc.emit("scalar", lambda h: h.activation(ebf[:, :], sc_tm[:, :], AF.Exp),
                    waits=[(V5, sc.v(V5))], inc=S5)
            sc.emit("tensor", lambda h: h.matmul(psS5[:, :], ones_b[:, :], ebf[:, :],
                                                 start=True, stop=True),
                    waits=[(S5, sc.v(S5))] + ([(V5, tsb_done[it - 1])] if it > 0 else []),
                    inc=M5)
            sc.emit("vector", lambda h: h.tensor_copy(tsb5[:, :], psS5[:, :]),
                    waits=[(M5, sc.v(M5))], inc=V5)
            tsb_done[it] = sc.v(V5)
            sc.emit("vector", lambda h: h.reduce_sum(
                lnt8[:, :], tsb5[:, :].rearrange("p (j e) -> p e j", e=8), axis=AX.X),
                inc=V5)
            sc.emit("scalar", lambda h: h.activation(lnt8[:, :], lnt8[:, :], AF.Ln),
                    waits=[(V5, sc.v(V5))], inc=S5)
            sc.emit("gpsimd", lambda h: h.partition_broadcast(lnt_b[:, :], lnt8[:, :]),
                    waits=[(S5, sc.v(S5))], inc=G5)
            for e0 in range(8):
                sc.emit("vector", lambda h, e=e0: h.tensor_scalar_sub(
                    v3(sc_tm)[:, :, e], v3(sc_tm)[:, :, e], lnt_b[:, e:e + 1]),
                    waits=[(G5, sc.v(G5))] if e0 == 0 else [], inc=V5)

        sc.emit("scalar", lambda h: h.activation(gbuf[:, :], sc_tm[:, :], AF.Exp),
                waits=[(V5, sc.v(V5))], inc=S5)
        sc.emit("vector", lambda h: h.reduce_sum(ssum[:, :], v3(gbuf), axis=AX.X),
                waits=[(S5, sc.v(S5))], inc=V5)
        sc.emit("vector", lambda h: h.tensor_scalar_add(ssum[:, :], ssum[:, :], 1e-8), inc=V5)
        sc.emit("vector", lambda h: h.reciprocal(ssum[:, :], ssum[:, :]), inc=V5)
        rec_done = sc.v(V5)
        for e0 in range(8):
            sc.emit("vector", lambda h, e=e0: h.tensor_tensor(
                v3(gbuf)[:, :, e], v3(gbuf)[:, :, e], ssum[:, :], ALU.mult),
                waits=[(V5, rec_done)] if e0 == 0 else [], inc=V5)
        sc.emit("vector", lambda h: h.reduce_max(v1b[:, :], v3(gbuf), axis=AX.X), inc=V5)
        for e0 in range(8):
            sc.emit("vector", lambda h, e=e0: h.tensor_tensor(
                v3(eqb)[:, :, e], v3(gbuf)[:, :, e], v1b[:, :], ALU.is_equal), inc=V5)
        for e0 in range(8):
            sc.emit("vector", lambda h, e=e0: h.scalar_tensor_tensor(
                v3(g2b)[:, :, e], v3(eqb)[:, :, e], -1e30, v3(gbuf)[:, :, e],
                ALU.mult, ALU.add), inc=V5)
        sc.emit("vector", lambda h: h.reduce_max(v2b[:, :], v3(g2b), axis=AX.X), inc=V5)
        sc.emit("vector", lambda h: h.tensor_tensor(v1b[:, :], v1b[:, :], v2b[:, :], ALU.add),
                inc=V5)
        sc.emit("vector", lambda h: h.tensor_scalar_add(v1b[:, :], v1b[:, :], 1e-8), inc=V5)
        sc.emit("vector", lambda h: h.reciprocal(v1b[:, :], v1b[:, :]), inc=V5)
        rec2_done = sc.v(V5)
        for e0 in range(8):
            sc.emit("vector", lambda h, e=e0: h.tensor_tensor(
                v3(eqb)[:, :, e], v3(gbuf)[:, :, e], v2b[:, :], ALU.is_ge),
                waits=[(V5, rec2_done)] if e0 == 0 else [], inc=V5)
        for e0 in range(8):
            sc.emit("vector", lambda h, e=e0: h.tensor_tensor(
                v3(w_tm)[:, :, e], v3(gbuf)[:, :, e], v3(eqb)[:, :, e], ALU.mult), inc=V5)
        for e0 in range(8):
            sc.emit("vector", lambda h, e=e0: h.tensor_tensor(
                v3(w_tm)[:, :, e], v3(w_tm)[:, :, e], v1b[:, :], ALU.mult), inc=V5)
        wtm_done = sc.v(V5)

        for c in range(8):
            w = [(V5, wtm_done if c == 0 else sc.v(V5))]

            def trb(h, c0=c):
                for jj in range(4):
                    j = 4 * c0 + jj
                    ins = h.transpose(psW5[:, 128 * jj:128 * jj + 128],
                                      w_tm[:, 8 * j:8 * j + 8], ident_f[:, :])
                return ins
            sc.emit("tensor", trb, waits=w, inc=M5)
            sc.emit("vector", lambda h, c0=c: h.tensor_copy(
                w_em[:, 512 * c0:512 * c0 + 512], psW5[:, :]),
                waits=[(M5, sc.v(M5))], inc=V5)

        def wslice(h):
            off = h.value_load(soff_sb[0:1, 0:1])
            return h.dma_start(out=w_own[:, :], in_=w_em[:, bass.ds(off, TOWN)])
        sc.emit("gpsimd", wslice, waits=[(V5, sc.v(V5))], inc=G5d, inck=16)
        if _DBG_SCRATCH:
            sc.emit("sync", lambda h: h.dma_start(out=d_dbg5[:, :], in_=w_tm[:, :]),
                    waits=[(G5d, sc.v(G5d))], inc=G5d, inck=16)
            sc.emit("sync", lambda h: h.dma_start(out=d_dbg6[:, :], in_=w_em[:, :]),
                    inc=G5d, inck=16)
            sc.emit("sync", lambda h: h.dma_start(out=d_dbg7[:, :], in_=sc_tm[:, :]),
                    inc=G5d, inck=16)
            sc.emit("sync", lambda h: h.dma_start(out=d_dbg8[:, :], in_=gbuf[:, :]),
                    inc=G5d, inck=16)
            sc.emit("sync", lambda h: h.dma_start(out=d_dbg1[:, :], in_=l_all[:, :]),
                    inc=G5d, inck=16)
            sc.emit("sync", lambda h: h.dma_start(out=d_dbg2[:, :], in_=scoresT[:, :]),
                    inc=G5d, inck=16)
            sc.emit("sync", lambda h: h.dma_start(out=d_dbg3[:, :], in_=w_own[:, :]),
                    inc=G5d, inck=16)
            sc.emit("sync", lambda h: None, waits=[(G5d, sc.v(G5d))])
        sc.run(nc)
    G5_ALL = cnt[G5d]
    st5.close()
    st_sc.close()

    st6 = ExitStack()
    xpo = sb(st6, "xpo", [128, NDC, TOWN], BF16)
    gpw = sb(st6, "gpw", [128, 8, TOWN], BF16)
    grelu = sb(st6, "grelu", [128, TOWN], BF16)
    wgp_f = sb(st6, "wgp_f", [128, 2, NDC, 128], F32)
    wgp_r = sb(st6, "wgp_r", [128, 2, 2, NDC, 128], BF16)
    wo_f = sb(st6, "wo_f", [128, 2, 8, 128], F32)
    wo_r = sb(st6, "wo_r", [128, 2, 8, 128], BF16)
    wbc = sb(st6, "wbc", [128, TOWN], F32)
    wstag = sb(st6, "wstag", [1, TOWN], F32)
    wbcb = sb(st6, "wbcb", [128, 2, TOWN], BF16)

    sc = Sched(sems, cnt)
    L6, M6, V6, S6, G6, G6d, C6 = S(), S(), S(), S(), S(), S(), S()

    with nc.psum_tensor("psG", [128, 1024], F32) as psG, \
         nc.psum_tensor("psP6", [128, 1024], F32) as psP6, \
         nc.psum_tensor("psO6", [128, 2, 1024], F32) as psO6:
        if _DBG_SCRATCH:
            DBG9 = S()
            sc.emit("sync", lambda h: h.dma_start(out=d_dbg9[:, :], in_=acc2[:, :]),
                    waits=[(G4, G4_ALL)], inc=DBG9, inck=16)
        sc.emit("vector", lambda h: h.tensor_scalar_mul(
            xpo[:, :, :], acc2[:, 0:8 * TOWN].rearrange("p (d t) -> p d t", d=NDC), 0.5),
            waits=[(G4, G4_ALL)], inc=V6)
        xpo_done = sc.v(V6)
        if _DBG_SCRATCH:
            sc.emit("sync", lambda h: h.dma_start(out=d_dbg10[:, :, :], in_=xpo[:, :, :]),
                    waits=[(V6, xpo_done)], inc=DBG9, inck=16)
            sc.emit("sync", lambda h: None, waits=[(DBG9, 32)])

        om_last = [0]
        acc_stt = {}
        cast_cnt = {}
        kcast = {}
        wocast = {}
        nfc = 0
        nth = 0
        ndo = 0
        wgld, wold, wbcd, gpend, grelu_rd = {}, {}, {}, {}, {}
        relu_cnt = {}
        stt_cnt = {}
        for e0 in range(E):
            w = [(G5d, G5_ALL)]
            if e0 >= 2:
                w.append((V6, gpend[e0 - 2]))
            if e0 >= 1:
                w.append((V6, wbcd[e0 - 1]))
            sc.emit("gpsimd", lambda h, e=e0: h.dma_start(
                out=wstag[:, :], in_=w_own[e:e + 1, :]), waits=w, inc=G6d, inck=16)
            sc.emit("gpsimd", lambda h: h.partition_broadcast(
                wbc[:, :], wstag[:, :]), waits=[(G6d, sc.v(G6d))], inc=G6)
            sc.emit("vector", lambda h, e=e0: h.tensor_copy(wbcb[:, e % 2, :], wbc[:, :]),
                    waits=[(G6, sc.v(G6))], inc=V6)
            wbcd[e0] = sc.v(V6)

            for half in range(2):
                for f8 in range(8):
                    fc = 8 * half + f8
                    for kd in range(2):
                        nk = 2 * nfc + kd
                        w = [(C6, kcast.get(nk - 2, 0))] if nk >= 2 else []
                        sc.emit("sync", lambda h, e=e0, f=fc, k=kd, g=nk % 2: h.dma_start(
                            out=wgp_f[:, g, :, :], in_=d_wgp[e, f, :, k, :, :]),
                            waits=w, inc=L6, inck=16)
                        wc_ = [(L6, sc.v(L6))]
                        if nfc >= 2:
                            wc_.append((M6, wgld[nfc - 2]))
                        sc.emit("scalar", lambda h, k=kd, g=nk % 2, gr=nfc % 2: h.copy(
                            wgp_r[:, gr, k, :, :], wgp_f[:, g, :, :]), waits=wc_, inc=C6)
                        kcast[nk] = sc.v(C6)
                    cast_cnt[nfc] = sc.v(C6)
                    ldv = None

                    for th in range(2):
                        w = [(C6, cast_cnt[nfc]), (V6, xpo_done)]
                        if nth >= 1:
                            w.append((S6, relu_cnt[nth - 1]))

                        def gm(h, g=nfc % 2, t1=th):
                            for d0 in range(NDC):
                                for t0 in range(2):
                                    ins = h.matmul(
                                        psG[:, 512 * t0:512 * t0 + 512],
                                        wgp_r[:, g, 0, d0, :],
                                        xpo[:, d0, 1024 * t1 + 512 * t0:1024 * t1 + 512 * t0 + 512],
                                        start=(d0 == 0), stop=(d0 == NDC - 1))
                            return ins
                        sc.emit("tensor", gm, waits=w, inc=M6)
                        wr = [(M6, sc.v(M6))]
                        if nth >= 1:
                            wr.append((V6, grelu_rd[nth - 1]))
                        sc.emit("scalar", lambda h, t1=th: h.activation(
                            grelu[:, 1024 * t1:1024 * t1 + 1024], psG[:, :], AF.Relu),
                            waits=wr, inc=S6)
                        relu_cnt[nth] = sc.v(S6)

                        w = []
                        if nth >= 1:
                            w.append((V6, stt_cnt[nth - 1]))

                        def pm(h, g=nfc % 2, t1=th):
                            for d0 in range(NDC):
                                for t0 in range(2):
                                    ins = h.matmul(
                                        psP6[:, 512 * t0:512 * t0 + 512],
                                        wgp_r[:, g, 1, d0, :],
                                        xpo[:, d0, 1024 * t1 + 512 * t0:1024 * t1 + 512 * t0 + 512],
                                        start=(d0 == 0), stop=(d0 == NDC - 1))
                            return ins
                        sc.emit("tensor", pm, waits=w, inc=M6)
                        pm_done = sc.v(M6)
                        if th == 1:
                            wgld[nfc] = pm_done

                        w = [(M6, pm_done), (S6, relu_cnt[nth]), (V6, wbcd[e0])]
                        if half == 1 or e0 > 0:
                            w.append((M6, om_last[0]))
                        sc.emit("vector", lambda h, f=f8, t1=th: h.scalar_tensor_tensor(
                            gpw[:, f, 1024 * t1:1024 * t1 + 1024], psP6[:, :], 1.0,
                            grelu[:, 1024 * t1:1024 * t1 + 1024], ALU.mult, ALU.mult),
                            waits=w, inc=V6)
                        sc.emit("vector", lambda h, f=f8, e=e0, t1=th: h.tensor_tensor(
                            gpw[:, f, 1024 * t1:1024 * t1 + 1024],
                            gpw[:, f, 1024 * t1:1024 * t1 + 1024],
                            wbcb[:, e % 2, 1024 * t1:1024 * t1 + 1024], ALU.mult),
                            inc=V6)
                        stt_cnt[nth] = sc.v(V6)
                        grelu_rd[nth] = sc.v(V6)
                        nth += 1
                    nfc += 1
                gpw_done = sc.v(V6)

                for dout in range(NDC):
                    w = [(C6, wocast.get(ndo // 2 - 2, 0))] if ndo >= 4 else []
                    sc.emit("sync", lambda h, e=e0, hf=half, d0=dout, g=(ndo // 2) % 2: h.dma_start(
                        out=wo_f[:, g, :, :], in_=d_wo[e, hf, d0, :, :, :]),
                        waits=w, inc=L6, inck=16)
                    wc_ = [(L6, sc.v(L6))]
                    if ndo >= 4:
                        wc_.append((M6, wold[ndo // 2 - 2]))
                    sc.emit("scalar", lambda h, g=(ndo // 2) % 2: h.copy(
                        wo_r[:, g, :, :], wo_f[:, g, :, :]), waits=wc_, inc=C6)
                    wocast[ndo // 2] = sc.v(C6)
                    for th in range(2):
                        w = [(C6, wocast[ndo // 2]), (V6, gpw_done)]
                        if ndo >= 2:
                            w.append((V6, acc_stt[ndo - 2]))

                        def om(h, g=(ndo // 2) % 2, og=ndo % 2, t1=th):
                            for f in range(8):
                                for t0 in range(2):
                                    ins = h.matmul(
                                        psO6[:, og, 512 * t0:512 * t0 + 512],
                                        wo_r[:, g, f, :],
                                        gpw[:, f, 1024 * t1 + 512 * t0:1024 * t1 + 512 * t0 + 512],
                                        start=(f == 0), stop=(f == 7))
                            return ins
                        sc.emit("tensor", om, waits=w, inc=M6)
                        om_last[0] = sc.v(M6)
                        if th == 1:
                            wold[ndo // 2] = sc.v(M6)
                        sc.emit("vector", lambda h, d0=dout, og=ndo % 2, t1=th: h.scalar_tensor_tensor(
                            acc2[:, TOWN * d0 + 1024 * t1:TOWN * d0 + 1024 * t1 + 1024],
                            psO6[:, og, :], 1.0,
                            acc2[:, TOWN * d0 + 1024 * t1:TOWN * d0 + 1024 * t1 + 1024],
                            ALU.mult, ALU.add),
                            waits=[(M6, sc.v(M6))], inc=V6)
                        acc_stt[ndo] = sc.v(V6)
                        ndo += 1
            gpend[e0] = sc.v(V6)
        sc.run(nc)
    st6.close()

    st7 = ExitStack()
    rbf7 = sb(st7, "rbf7", [128, NDC, 512], BF16)
    sqb7 = sb(st7, "sqb7", [128, NDC, 512], BF16)
    sml7 = sb(st7, "sml7", [1, 3072], F32)
    ost = sb(st7, "ost", [128, 2, NDC, 512], F32)

    sc = Sched(sems, cnt)
    M7, V7, S7, D7 = S(), S(), S(), S()
    statm7_done = {}
    if _DBG_SCRATCH:
        scd = Sched(sems, cnt)
        DD7 = S()
        scd.emit("sync", lambda h: h.dma_start(out=d_dbg4[:, :], in_=acc2[:, :]),
                 inc=DD7, inck=16)
        scd.emit("sync", lambda h: None, waits=[(DD7, 16)])
        scd.run(nc)
    with nc.psum_tensor("psM7", [1, 1024], F32) as psM7, \
         nc.psum_tensor("psB7", [128, 2, 512], F32) as psB7:
        accv = acc2[:, 0:8 * TOWN].rearrange("p (d t) -> p d t", d=NDC)
        for tc in range(4):
            av = accv[:, :, 512 * tc:512 * tc + 512]
            sc.emit("scalar", lambda h, a=av: h.activation(sqb7[:, :, :], a, AF.Square),
                    inc=S7)
            sc.emit("vector", lambda h, a=av: h.tensor_copy(rbf7[:, :, :], a),
                    waits=[(M7, statm7_done.get(tc - 1, 0))], inc=V7)
            rdy = (sc.v(S7), sc.v(V7))

            def statm7(h):
                for d0 in range(NDC):
                    h.matmul(psM7[0:1, 0:512], ones_b[:, :], rbf7[:, d0, :],
                             start=(d0 == 0), stop=(d0 == NDC - 1))
                for d0 in range(NDC):
                    ins = h.matmul(psM7[0:1, 512:1024], ones_b[:, :], sqb7[:, d0, :],
                                   start=(d0 == 0), stop=(d0 == NDC - 1))
                return ins
            sc.emit("tensor", statm7, waits=[(S7, rdy[0]), (V7, rdy[1])], inc=M7)
            statm7_done[tc] = sc.v(M7)

            def small7(h):
                h.tensor_scalar_mul(sml7[0:1, 0:512], psM7[0:1, 0:512], 1.0 / C)
                h.tensor_scalar_mul(sml7[0:1, 512:1024], psM7[0:1, 512:1024], 1.0 / C)
                h.tensor_tensor(sml7[0:1, 1024:1536], sml7[0:1, 0:512], sml7[0:1, 0:512], ALU.mult)
                h.tensor_tensor(sml7[0:1, 1536:2048], sml7[0:1, 512:1024], sml7[0:1, 1024:1536], ALU.subtract)
                h.tensor_scalar_add(sml7[0:1, 1536:2048], sml7[0:1, 1536:2048], 1e-5)
                return h.reciprocal(sml7[0:1, 2048:2560], sml7[0:1, 1536:2048])
            sc.emit("vector", small7, waits=[(M7, sc.v(M7))], inc=V7)
            sc.emit("scalar", lambda h: h.activation(sml7[0:1, 2560:3072], sml7[0:1, 2048:2560],
                                                     AF.Sqrt),
                    waits=[(V7, sc.v(V7))], inc=S7)
            sc.emit("vector", lambda h: h.tensor_tensor(
                sml7[0:1, 1024:1536], sml7[0:1, 0:512], sml7[0:1, 2560:3072], ALU.mult),
                waits=[(S7, sc.v(S7))], inc=V7)

            def bcm7(h):
                h.matmul(psB7[:, 0, :], ones1f[:, :], sml7[0:1, 2560:3072], start=True, stop=True)
                return h.matmul(psB7[:, 1, :], ones1f[:, :], sml7[0:1, 1024:1536], start=True, stop=True)
            sc.emit("tensor", bcm7, waits=[(V7, sc.v(V7))], inc=M7)
            bc7 = sc.v(M7)

            g = tc % 2
            for dout in range(NDC):
                w = [(M7, bc7)]
                if tc >= 2 and dout == 0:
                    w.append((D7, 16 * (tc - 1)))
                sc.emit("vector", lambda h, a=av, d0=dout, g0=g: h.tensor_tensor(
                    ost[:, g0, d0, :], a[:, d0, :], psB7[:, 0, :], ALU.mult),
                    waits=w, inc=V7)
                sc.emit("vector", lambda h, d0=dout, g0=g: h.tensor_tensor(
                    ost[:, g0, d0, :], ost[:, g0, d0, :], psB7[:, 1, :], ALU.subtract),
                    inc=V7)
            sc.emit("sync", lambda h, t0=tc, g0=g: h.dma_start(
                out=d_yout[:, :, 512 * t0:512 * t0 + 512], in_=ost[:, g0, :, :]),
                waits=[(V7, sc.v(V7))], inc=D7, inck=16)
        sc.emit("sync", lambda h: None, waits=[(D7, sc.v(D7))])
        sc.run(nc)
    st7.close()
    st_acc.close()

    nc.finalize()
    _CACHE["nc"] = nc
    return nc


def _host_prep(inputs):
    x = np.asarray(inputs["L0_attn_x"], np.float32)
    ac = np.ascontiguousarray
    r3 = lambda a: ac(a.reshape(NDC, 128, -1).transpose(1, 0, 2))

    cp, sp = _rope_tables()
    tri = np.triu(np.ones((128, 128), np.float32))
    sel = np.zeros((2, 128), np.float32)
    sel[0, 0:64] = 1.0
    sel[1, 64:128] = 1.0

    wg = np.asarray(inputs["L0_ffn_wg"], np.float32)
    wp = np.asarray(inputs["L0_ffn_wp"], np.float32)
    wo = np.asarray(inputs["L0_ffn_wo"], np.float32)
    wgs = wg.reshape(E, NDC, 128, 16, 128)
    wps = wp.reshape(E, NDC, 128, 16, 128)
    wgp = ac(np.stack([wgs, wps], axis=1).transpose(0, 4, 3, 1, 2, 5))
    wo_h = ac(wo.reshape(E, 2, 8, 128, 8, 128).transpose(0, 1, 4, 3, 2, 5))

    common = {
        "wqdT": r3(np.asarray(inputs["L0_attn_Wq_down"], np.float32).T),
        "wkdT": r3(np.asarray(inputs["L0_attn_Wk_down"], np.float32).T),
        "wvdT": r3(np.asarray(inputs["L0_attn_Wv_down"], np.float32).T),
        "upqc": np.asarray(inputs["L0_attn_Wq_up_c"], np.float32),
        "upqe": np.asarray(inputs["L0_attn_Wq_up_e"], np.float32),
        "upqeT": ac(np.asarray(inputs["L0_attn_Wq_up_e"], np.float32).T),
        "upkc": np.asarray(inputs["L0_attn_Wk_up_c"], np.float32),
        "upke": np.asarray(inputs["L0_attn_Wk_up_e"], np.float32),
        "upkeT": ac(np.asarray(inputs["L0_attn_Wk_up_e"], np.float32).T),
        "upv": np.asarray(inputs["L0_attn_Wv_up"], np.float32),
        "Pm": _perm_matrix(),
        "wc": r3(np.asarray(inputs["L0_attn_Wc"], np.float32)),
        "rwT": r3(np.asarray(inputs["L0_ffn_router_w"], np.float32).T),
        "rb": ac(np.asarray(inputs["L0_ffn_router_b"], np.float32).reshape(8, 1)),
        "temp": ac(np.asarray(inputs["L0_ffn_temperature"], np.float32).reshape(1, 1)),
        "wgp": wgp,
        "wo": wo_h,
        "cosp": cp,
        "sinp": sp,
        "tri2": ac(np.concatenate([tri, tri], axis=1)),
        "ident": np.eye(128, dtype=np.float32),
        "sel2": sel,
    }
    xts = [r3(ac(x[b].T)) for b in range(B)]
    offs, soffs = [], []
    for s in range(2):
        o = np.full((NTC, 8), GARB, np.int32)
        for tc in range(NTC):
            if 4 * s <= tc < 4 * (s + 1):
                for dout in range(NDC):
                    o[tc, dout] = TOWN * dout + 512 * (tc - 4 * s)
        offs.append(o)
        soffs.append(np.array([[TOWN * s]], np.int32))
    return (common, xts, offs, soffs)


def kernel(**inputs):
    common, xts, offs, soffs = _host_prep(inputs)
    nc = build_nc()
    in_maps = []
    for c in range(8):
        b, s = c // 2, c % 2
        m = dict(common)
        m["xT"] = xts[b]
        m["offs"] = offs[s]
        m["soff"] = soffs[s]
        in_maps.append(m)
    res = run_bass_kernel_spmd(nc, in_maps, core_ids=list(range(8)))
    out = np.empty((B, T, C), np.float32)
    for c in range(8):
        b, s = c // 2, c % 2
        y = np.asarray(res.results[c]["yout"])
        out[b, TOWN * s:TOWN * (s + 1), :] = y.transpose(1, 0, 2).reshape(C, TOWN).T
    return out


# version 10
# speedup vs baseline: 1.3626x; 1.0519x over previous
import numpy as np
from contextlib import ExitStack

import concourse.bacc as bacc
import concourse.bass as bass
import concourse.mybir as mybir
from concourse.bass_utils import run_bass_kernel_spmd

F32 = mybir.dt.float32
BF16 = mybir.dt.bfloat16
I32 = mybir.dt.int32
AF = mybir.ActivationFunctionType
ALU = mybir.AluOpType
AX = mybir.AxisListType

B, T, C = 4, 4096, 1024
H, DH, DHE = 16, 64, 32
NPAIR = 8
E, FF = 8, 2048
TOWN = 2048
NDC = C // 128
NTC = T // 512
NOW = TOWN // 512
NKT = T // 128
RG = [[0, 1], [2, 3], [4, 5], [6, 7]]

ENGINES = ("tensor", "vector", "scalar", "gpsimd", "sync")


class Sched:

    def __init__(self, sems, cnt):
        self.sems = sems
        self.cnt = cnt
        self.ops = {e: [] for e in ENGINES}

    def emit(self, eng, fn, waits=(), inc=None, inck=1):
        self.ops[eng].append((fn, tuple(waits), inc, inck))
        if inc is not None:
            self.cnt[inc] += inck

    def v(self, si):
        return self.cnt[si]

    def run(self, nc):
        sems = self.sems
        with nc.Block() as blk:
            for eng in ENGINES:
                lst = self.ops[eng]
                if not lst:
                    continue

                def make(lst):
                    def body(h):
                        for (fn, waits, inc, inck) in lst:
                            for (si, val) in waits:
                                if val > 0:
                                    h.wait_ge(sems[si], val)
                            ins = fn(h)
                            if inc is not None:
                                ins.then_inc(sems[inc], inck)
                    return body
                getattr(blk, eng)(make(lst))


def _rope_tables():
    inv = 1.0 / (10000.0 ** (np.arange(0, DHE, 2, dtype=np.float64) / DHE))
    freqs = np.arange(T, dtype=np.float64)[:, None] * inv[None, :]
    emb = np.concatenate([freqs, freqs], axis=1)
    cos = np.cos(emb).astype(np.float32).T
    sin = np.sin(emb).astype(np.float32).T
    cp = np.ones((128, T), np.float32)
    sp = np.zeros((128, T), np.float32)
    cp[32:64] = cos
    cp[96:128] = cos
    sp[32:64] = sin
    sp[96:128] = sin
    return cp, sp


def _perm_matrix():
    P = np.zeros((DHE, DHE), np.float32)
    for i in range(DHE // 2):
        P[2 * i + 1, 2 * i] = -1.0
        P[2 * i, 2 * i + 1] = 1.0
    return P


_CACHE = {}
_DBG_SCRATCH = False


def build_nc():
    if "nc" in _CACHE:
        return _CACHE["nc"]
    nc = bacc.Bacc("TRN2", target_bir_lowering=False, detect_race_conditions=False)

    dt = lambda n, s, d, k: nc.dram_tensor(n, s, d, kind=k).ap()
    d_xT = dt("xT", [128, NDC, T], F32, "ExternalInput")
    d_xTo = dt("xTo", [128, NDC, TOWN], F32, "ExternalInput")
    d_wqdT = dt("wqdT", [128, NDC, C], F32, "ExternalInput")
    d_wkdT = dt("wkdT", [128, NDC, C], F32, "ExternalInput")
    d_wvdT = dt("wvdT", [128, NDC, C], F32, "ExternalInput")
    d_upqc = dt("upqc", [64, 32], F32, "ExternalInput")
    d_upqe = dt("upqe", [64, 32], F32, "ExternalInput")
    d_upqeT = dt("upqeT", [32, 64], F32, "ExternalInput")
    d_upkc = dt("upkc", [64, 32], F32, "ExternalInput")
    d_upke = dt("upke", [64, 32], F32, "ExternalInput")
    d_upkeT = dt("upkeT", [32, 64], F32, "ExternalInput")
    d_upv = dt("upv", [64, 64], F32, "ExternalInput")
    d_P = dt("Pm", [32, 32], F32, "ExternalInput")
    d_wc = dt("wc", [128, NDC, C], F32, "ExternalInput")
    d_rwT = dt("rwT", [128, NDC, 8], F32, "ExternalInput")
    d_rb = dt("rb", [8, 1], F32, "ExternalInput")
    d_temp = dt("temp", [1, 1], F32, "ExternalInput")
    d_wgp = dt("wgp", [E, 16, 128, 2, NDC, 128], F32, "ExternalInput")
    d_wo = dt("wo", [E, 2, 8, 128, 8, 128], F32, "ExternalInput")
    d_cos = dt("cosp", [128, T], F32, "ExternalInput")
    d_sin = dt("sinp", [128, T], F32, "ExternalInput")
    d_coso = dt("coso", [128, TOWN], F32, "ExternalInput")
    d_sino = dt("sino", [128, TOWN], F32, "ExternalInput")
    d_msk = dt("msk", [128, 8, 512], F32, "ExternalInput")
    d_ident = dt("ident", [128, 128], F32, "ExternalInput")
    d_sel2 = dt("sel2", [2, 128], F32, "ExternalInput")
    d_soff = dt("soff", [1, 1], I32, "ExternalInput")

    d_yout = dt("yout", [128, 8, TOWN], F32, "ExternalOutput")

    _knd = "ExternalOutput" if _DBG_SCRATCH else "Internal"
    d_qh = dt("qh", [NPAIR, 128, TOWN], BF16, _knd)
    d_kh = dt("kh", [NPAIR, 128, T], BF16, _knd)
    d_vh = dt("vh", [128, NKT, H, 65], BF16, _knd)
    d_yh = dt("yh", [NPAIR, 128, TOWN], BF16, _knd)
    d_scin = dt("scin", [8, TOWN], F32, "Internal")
    d_scout = dt("scout", [2, 8, TOWN], F32, "Internal")
    if _DBG_SCRATCH:
        d_dbg1 = dt("dbg1", [128, 512], F32, "ExternalOutput")
        d_dbg2 = dt("dbg2", [8, 2 * TOWN], F32, "ExternalOutput")
        d_dbg3 = dt("dbg3", [8, TOWN], F32, "ExternalOutput")
        d_dbg9 = dt("dbg9", [128, 8 * TOWN], F32, "ExternalOutput")

    es = ExitStack()
    nc._keepalive = es
    sems = [es.enter_context(nc.semaphore(f"sm{i}")) for i in range(64)]
    cnt = {i: 0 for i in range(64)}
    _ns = [0]

    def S():
        i = _ns[0]
        _ns[0] += 1
        return i

    sb = lambda st, name, shape, d: st.enter_context(nc.sbuf_tensor(name, shape, d))

    ident_f = sb(es, "ident_f", [128, 128], F32)
    ones_b = sb(es, "ones_b", [128, 1], BF16)
    ones1f = sb(es, "ones1f", [1, 128], F32)
    sel2 = sb(es, "sel2s", [2, 128], F32)
    l_all = sb(es, "l_all", [128, 512], F32)
    rl_all = sb(es, "rl_all", [128, 512], F32)
    rb_sb = sb(es, "rb_sb", [8, 1], F32)
    rtemp = sb(es, "rtemp", [8, 1], F32)
    t_sb = sb(es, "t_sb", [1, 1], F32)
    soff_sb = sb(es, "soff_sb", [1, 1], I32)
    st_msk = ExitStack()
    msk_sb = sb(st_msk, "msk_sb", [128, 8, 512], BF16)

    st_x = ExitStack()
    xT = sb(st_x, "xT_sb", [128, NDC, T], BF16)
    cosp = sb(st_x, "cosp_sb", [128, T], BF16)
    sinp = sb(st_x, "sinp_sb", [128, T], BF16)
    st_w = ExitStack()
    wq_eff = sb(st_w, "wq_eff", [128, NDC, C], BF16)
    wq_rot = sb(st_w, "wq_rot", [128, NDC, C], BF16)
    wk_eff = sb(st_w, "wk_eff", [128, NDC, C], BF16)
    wk_rot = sb(st_w, "wk_rot", [128, NDC, C], BF16)
    st_fold = ExitStack()
    wdT2 = sb(st_fold, "wdT2", [128, 2 * NDC, C], BF16)
    upfq = sb(st_fold, "upfq", [128, 128], BF16)
    upfk = sb(st_fold, "upfk", [128, 128], BF16)
    upeT2 = sb(st_fold, "upeT2", [32, 2, 64], BF16)
    P_sb = sb(st_fold, "P_sb", [32, 32], BF16)

    sc = Sched(sems, cnt)
    SL, SLg, SLm, SLX = S(), S(), S(), S()
    for dst, src in [(ident_f, d_ident), (sel2, d_sel2), (rb_sb, d_rb), (t_sb, d_temp),
                     (soff_sb, d_soff)]:
        sc.emit("sync", lambda h, a=dst, b=src: h.dma_start(out=a[:, :], in_=b[:, :]),
                inc=SL, inck=16)
    sc.emit("gpsimd", lambda h: h.dma_start(out=xT[:, 0:4, :], in_=d_xT[:, 0:4, :]),
            inc=SLX, inck=16)
    sc.emit("gpsimd", lambda h: h.dma_start(out=xT[:, 4:8, :], in_=d_xT[:, 4:8, :]),
            inc=SLX, inck=16)
    for dst, src in [(cosp, d_cos), (sinp, d_sin), (P_sb, d_P)]:
        sc.emit("gpsimd", lambda h, a=dst, b=src: h.dma_start(out=a[:, :], in_=b[:, :]),
                inc=SLg, inck=16)
    sc.emit("gpsimd", lambda h: h.dma_start(out=msk_sb[:, :, :], in_=d_msk[:, :, :]),
            inc=SLg, inck=16)
    for (dst, c0, c1, src) in [(upfq, 0, 32, d_upqc), (upfq, 32, 64, d_upqe),
                               (upfk, 0, 32, d_upkc), (upfk, 32, 64, d_upke)]:
        sc.emit("gpsimd", lambda h, a=dst, x0=c0, x1=c1, b=src: h.dma_start(
            out=a[0:64, x0:x1], in_=b[:, :]), inc=SLg, inck=16)
    sc.emit("gpsimd", lambda h: h.dma_start(out=upeT2[:, 0, :], in_=d_upqeT[:, :]),
            inc=SLg, inck=16)
    sc.emit("gpsimd", lambda h: h.dma_start(out=upeT2[:, 1, :], in_=d_upkeT[:, :]),
            inc=SLg, inck=16)
    sc.emit("gpsimd", lambda h: h.dma_start(out=wdT2[:, 0:NDC, :], in_=d_wqdT[:, :, :]),
            inc=SLg, inck=16)
    sc.emit("gpsimd", lambda h: h.dma_start(out=wdT2[:, NDC:2 * NDC, :], in_=d_wkdT[:, :, :]),
            inc=SLg, inck=16)
    sc.emit("gpsimd", lambda h: h.memset(ones_b[:, :], 1.0))
    sc.emit("gpsimd", lambda h: h.memset(ones1f[:, :], 1.0))
    sc.emit("gpsimd", lambda h: h.memset(upfq[:, 64:96], 0.0))
    sc.emit("gpsimd", lambda h: h.memset(upfk[:, 64:96], 0.0), inc=SLm, inck=1)
    sc.run(nc)
    LD_ALL = sc.v(SL)
    LDG_ALL = sc.v(SLg)

    sc = Sched(sems, cnt)
    SM0, SV0, SD0 = S(), S(), S()
    with nc.psum_tensor("psR", [64, 2, 512], F32) as psR:
        for kk in range(2):
            sc.emit("tensor",
                    lambda h, k=kk: h.matmul(psR[:, k, 0:32], upeT2[:, k, :], P_sb[:, :],
                                             start=True, stop=True),
                    waits=[(SL, LD_ALL), (SLg, LDG_ALL), (SLm, 1)], inc=SM0)
        sc.emit("vector", lambda h: h.tensor_copy(upfq[0:64, 96:128], psR[:, 0, 0:32]),
                waits=[(SM0, 1)], inc=SV0)
        sc.emit("vector", lambda h: h.tensor_copy(upfk[0:64, 96:128], psR[:, 1, 0:32]),
                waits=[(SM0, 2)], inc=SV0)
        sc.emit("vector", lambda h: h.tensor_scalar_mul(upfq[0:64, :], upfq[0:64, :], 0.125),
                inc=SV0)
        sc.emit("gpsimd", lambda h: h.dma_start(out=upfq[64:128, :], in_=upfq[0:64, :]),
                waits=[(SV0, sc.v(SV0))], inc=SD0, inck=16)
        sc.emit("gpsimd", lambda h: h.dma_start(out=upfk[64:128, :], in_=upfk[0:64, :]),
                inc=SD0, inck=16)
        sc.run(nc)

    sc = Sched(sems, cnt)
    SM2, SV2 = S(), S()
    with nc.psum_tensor("psF", [128, 2, 512], F32) as psF:
        mmn = 0
        for kind in range(2):
            upf = upfq if kind == 0 else upfk
            eff = wq_eff if kind == 0 else wk_eff
            rot = wq_rot if kind == 0 else wk_rot
            for h0 in range(H):
                for dc0 in range(NDC):
                    w = [(SL, LD_ALL), (SLg, LDG_ALL), (SLm, 1), (SD0, 32)] if mmn == 0 else []
                    if mmn >= 2:
                        w.append((SV2, 2 * (mmn - 1)))
                    lhs = wdT2[64 * (h0 % 2):64 * (h0 % 2) + 64,
                               kind * NDC + h0 // 2, 128 * dc0:128 * dc0 + 128]
                    r0 = 64 * (h0 % 2)
                    sc.emit("tensor",
                            lambda h, l=lhs, u=upf, g=mmn % 2, r=r0: h.matmul(
                                psF[:, g, 0:128], l, u[r:r + 64, :], start=True, stop=True),
                            waits=w, inc=SM2)
                    mm_done = sc.v(SM2)
                    sc.emit("vector",
                            lambda h, e=eff, d=dc0, hh=h0, g=mmn % 2: h.tensor_copy(
                                e[:, d, 64 * hh:64 * hh + 64], psF[:, g, 0:64]),
                            waits=[(SM2, mm_done)], inc=SV2)
                    sc.emit("vector",
                            lambda h, r=rot, d=dc0, hh=h0, g=mmn % 2: h.tensor_copy(
                                r[:, d, 64 * hh:64 * hh + 64], psF[:, g, 64:128]),
                            inc=SV2)
                    mmn += 1
        sc.run(nc)
    st_fold.close()

    st1 = ExitStack()
    t1b = sb(st1, "t1b", [128, 512], F32)
    t2b = sb(st1, "t2b", [128, 512], F32)
    qstage = sb(st1, "qstage", [128, 2, 512], BF16)
    xTo = sb(st1, "xTo_sb", [128, NDC, TOWN], BF16)
    cospo = sb(st1, "cospo_sb", [128, TOWN], BF16)
    sinpo = sb(st1, "sinpo_sb", [128, TOWN], BF16)

    sc = Sched(sems, cnt)
    P1M, P1V, P1D, XO = S(), S(), S(), S()
    sc.emit("gpsimd", lambda h: h.dma_start(out=xTo[:, :, :], in_=d_xTo[:, :, :]),
            inc=XO, inck=16)
    sc.emit("gpsimd", lambda h: h.dma_start(out=cospo[:, :], in_=d_coso[:, :]),
            inc=XO, inck=16)
    sc.emit("gpsimd", lambda h: h.dma_start(out=sinpo[:, :], in_=d_sino[:, :]),
            inc=XO, inck=16)

    plan1 = [("k", p, tc) for p in range(NPAIR) for tc in range(NTC)]
    nq0 = len(plan1)
    plan1 += [("q", p, tc) for p in range(NPAIR) for tc in range(NOW)]

    with nc.psum_tensor("psP", [128, 2, 512], F32) as psP, \
         nc.psum_tensor("psO", [128, 2, 512], F32) as psO:
        for it, (kind, p, tc) in enumerate(plan1):
            isq = kind == "q"
            eff = wq_eff if isq else wk_eff
            rot = wq_rot if isq else wk_rot
            xs = xTo if isq else xT
            cc_ = cospo if isq else cosp
            ss_ = sinpo if isq else sinp
            dh = d_qh if isq else d_kh
            g = it % 2
            w = [(SL, LD_ALL), (SLg, LDG_ALL), (SLX, 32)] if it == 0 else []
            if it == nq0:
                w.append((XO, 48))
            if it >= 2:
                w.append((P1V, 3 * (it - 1)))

            def mms(h, e=eff, r=rot, x0=xs, p0=p, t0=tc, g0=g):
                for dc0 in range(NDC):
                    h.matmul(psP[:, g0, :], e[:, dc0, 128 * p0:128 * p0 + 128],
                             x0[:, dc0, 512 * t0:512 * t0 + 512],
                             start=(dc0 == 0), stop=(dc0 == NDC - 1))
                for dc0 in range(NDC):
                    ins = h.matmul(psO[:, g0, :], r[:, dc0, 128 * p0:128 * p0 + 128],
                                   x0[:, dc0, 512 * t0:512 * t0 + 512],
                                   start=(dc0 == 0), stop=(dc0 == NDC - 1))
                return ins
            sc.emit("tensor", mms, waits=w, inc=P1M)
            mmv = sc.v(P1M)
            wv = [(P1M, mmv)]
            if it >= 2:
                wv.append((P1D, 16 * (it - 1)))
            sc.emit("vector",
                    lambda h, c=cc_, t0=tc, g0=g: h.tensor_tensor(
                        t1b[:, :], psP[:, g0, :],
                        c[:, 512 * t0:512 * t0 + 512], ALU.mult),
                    waits=wv, inc=P1V)
            sc.emit("vector",
                    lambda h, s=ss_, t0=tc, g0=g: h.tensor_tensor(
                        t2b[:, :], psO[:, g0, :],
                        s[:, 512 * t0:512 * t0 + 512], ALU.mult),
                    inc=P1V)
            sc.emit("vector",
                    lambda h, g0=g: h.tensor_add(qstage[:, g0, :], t1b[:, :], t2b[:, :]),
                    inc=P1V)
            vv = sc.v(P1V)
            sc.emit("sync",
                    lambda h, d=dh, p0=p, t0=tc, g0=g: h.dma_start(
                        out=d[p0, :, 512 * t0:512 * t0 + 512], in_=qstage[:, g0, :]),
                    waits=[(P1V, vv)], inc=P1D, inck=16)
        sc.run(nc)
    P1D_ALL = cnt[P1D]
    st1.close()
    st_w.close()

    st2 = ExitStack()
    wvdT = sb(st2, "wvdT_sb", [128, NDC, C], BF16)
    wv_eff = sb(st2, "wv_eff", [128, NDC, C], BF16)
    upv_sb = sb(st2, "upv_sb", [128, 64], BF16)
    vstage = sb(st2, "vstage", [128, 2, H, 65], BF16)

    sc = Sched(sems, cnt)
    P2L, P2M, P2V, P2D, P2Lm = S(), S(), S(), S(), S()
    with nc.psum_tensor("psVF", [128, 2, 512], F32) as psVF, \
         nc.psum_tensor("psV", [128, 2, 1024], F32) as psV:
        sc.emit("gpsimd", lambda h: h.dma_start(out=wvdT[:, :, :], in_=d_wvdT[:, :, :]),
                inc=P2L, inck=16)
        sc.emit("gpsimd", lambda h: h.dma_start(out=upv_sb[0:64, :], in_=d_upv[:, :]),
                inc=P2L, inck=16)
        sc.emit("gpsimd", lambda h: h.dma_start(out=upv_sb[64:128, :], in_=upv_sb[0:64, :]),
                waits=[(P2L, 32)], inc=P2L, inck=16)
        sc.emit("gpsimd", lambda h: h.memset(vstage[:, :, :, :], 1.0), inc=P2Lm)
        ld = sc.v(P2L)
        mmn = 0
        for h0 in range(H):
            for dc0 in range(NDC):
                w = [(P2L, ld), (P2Lm, 1)] if mmn == 0 else []
                if mmn >= 2:
                    w.append((P2V, mmn - 1))
                lhs = wvdT[64 * (h0 % 2):64 * (h0 % 2) + 64, h0 // 2, 128 * dc0:128 * dc0 + 128]
                r0 = 64 * (h0 % 2)
                sc.emit("tensor",
                        lambda h, l=lhs, g=mmn % 2, r=r0: h.matmul(
                            psVF[:, g, 0:64], l, upv_sb[r:r + 64, :], start=True, stop=True),
                        waits=w, inc=P2M)
                sc.emit("vector",
                        lambda h, d=dc0, hh=h0, g=mmn % 2: h.tensor_copy(
                            wv_eff[:, d, 64 * hh:64 * hh + 64], psVF[:, g, 0:64]),
                        waits=[(P2M, sc.v(P2M))], inc=P2V)
                mmn += 1
        vf_done = sc.v(P2V)
        for kt in range(NKT):
            g = kt % 2
            w = [(P2V, vf_done)] if kt < 2 else [(P2V, vf_done + kt - 1)]

            def vmms(h, k0=kt, g0=g):
                for dc0 in range(NDC):
                    for hh in range(2):
                        ins = h.matmul(psV[:, g0, 512 * hh:512 * hh + 512],
                                       xT[:, dc0, 128 * k0:128 * k0 + 128],
                                       wv_eff[:, dc0, 512 * hh:512 * hh + 512],
                                       start=(dc0 == 0), stop=(dc0 == NDC - 1))
                return ins
            sc.emit("tensor", vmms, waits=w, inc=P2M)
            wv = [(P2M, sc.v(P2M))]
            if kt >= 2:
                wv.append((P2D, 16 * (kt - 1)))
            sc.emit("vector",
                    lambda h, g0=g: h.tensor_copy(
                        vstage[:, g0, :, 0:64],
                        psV[:, g0, :].rearrange("p (h d) -> p h d", h=H)),
                    waits=wv, inc=P2V)
            sc.emit("sync",
                    lambda h, k0=kt, g0=g: h.dma_start(out=d_vh[:, k0, :, :],
                                                       in_=vstage[:, g0, :, :]),
                    waits=[(P2V, sc.v(P2V))], inc=P2D, inck=16)
        sc.run(nc)
    P2D_ALL = cnt[P2D]
    st2.close()
    st_x.close()

    st3 = ExitStack()
    qbuf = sb(st3, "qbuf", [128, 2, TOWN], BF16)
    kbuf = sb(st3, "kbuf", [128, 2, T], BF16)
    vbuf = sb(st3, "vbuf", [128, 2, NKT, 2, 65], BF16)
    p2 = sb(st3, "p2", [128, 2, 1024], BF16)
    ystg = sb(st3, "ystg", [128, 2, 1024], BF16)
    lstg = sb(st3, "lstg", [128, 2, 1024], F32)

    sc = Sched(sems, cnt)
    A_LD, A_ST, A_EXP, A_MSK, A_AV, A_EV, A_YD = S(), S(), S(), S(), S(), S(), S()

    sched3 = []
    for p in range(NPAIR):
        for qc in range(NOW):
            for j in range(8 * qc + 8):
                r = j - 8 * qc
                c0 = 64 * r if r > 0 else 0
                sched3.append((p, qc, j, c0, j == 8 * qc + 7, r if r >= 0 else None))
    NJP = len(sched3) // NPAIR

    for p in range(NPAIR):
        w = []
        if p == 0:
            w = [(P1D, P1D_ALL), (P2D, P2D_ALL)]
        if p >= 2:
            w = [(A_AV, NJP * (p - 1))]
        g = p % 2
        sc.emit("sync", lambda h, p0=p, g0=g: h.dma_start(out=qbuf[:, g0, :], in_=d_qh[p0, :, :]),
                waits=w, inc=A_LD, inck=16)
        sc.emit("sync", lambda h, p0=p, g0=g: h.dma_start(out=kbuf[:, g0, :], in_=d_kh[p0, :, :]),
                inc=A_LD, inck=16)
        sc.emit("sync", lambda h, p0=p, g0=g: h.dma_start(
            out=vbuf[:, g0, :, :, :],
            in_=d_vh[:, :, 2 * p0:2 * p0 + 2, :]),
            inc=A_LD, inck=16)

    with nc.psum_tensor("psST", [128, 2, 1024], F32) as psST, \
         nc.psum_tensor("psOT", [128, 2, 1024], F32) as psOT:
        nj = 0
        kidx = 0
        pending_av = None
        for (p, qc, j, c0, last, r) in sched3:
            g = p % 2
            jg = nj % 2
            w = [(A_LD, 48 * (p + 1))]
            if nj >= 2:
                w.append((A_EXP, nj - 1))

            def stm(h, p0=p, q0=qc, j0=j, cc=c0, g0=g, jg0=jg):
                for hh in range(2):
                    ins = h.matmul(
                        psST[:, jg0, 512 * hh + cc:512 * hh + 512],
                        kbuf[64 * hh:64 * hh + 64, g0, 128 * j0:128 * j0 + 128],
                        qbuf[64 * hh:64 * hh + 64, g0, 512 * q0 + cc:512 * q0 + 512],
                        start=True, stop=True)
                return ins
            sc.emit("tensor", stm, waits=w, inc=A_ST)
            st_done = sc.v(A_ST)

            if pending_av is not None:
                sc.emit("tensor", pending_av[0], waits=pending_av[1], inc=A_AV)
                pending_av = None

            w = [(A_ST, st_done)]
            if nj >= 2:
                w.append((A_AV, nj - 1))

            def expf(h, cc=c0, jg0=jg):
                iv = psST[:, jg0, :].rearrange("p (h q) -> p h q", h=2)[:, :, cc:512]
                ov = p2[:, jg0, :].rearrange("p (h q) -> p h q", h=2)[:, :, cc:512]
                return h.activation(ov, iv, AF.Exp)
            sc.emit("scalar", expf, waits=w, inc=A_EXP)
            exp_done = sc.v(A_EXP)

            if r is not None:
                def mskf0(h, cc=c0, jg0=jg, r0=r):
                    ov = p2[:, jg0, :].rearrange("p (h q) -> p h q", h=2)[:, 0, cc:512]
                    return h.tensor_tensor(ov, ov, msk_sb[:, r0, cc:512], ALU.mult)

                def mskf1(h, cc=c0, jg0=jg, r0=r):
                    ov = p2[:, jg0, :].rearrange("p (h q) -> p h q", h=2)[:, 1, cc:512]
                    return h.tensor_tensor(ov, ov, msk_sb[:, r0, cc:512], ALU.mult)
                sc.emit("vector", mskf0, waits=[(A_EXP, exp_done)])
                sc.emit("vector", mskf1, inc=A_MSK)
            else:
                sc.emit("vector", lambda h: h.engine_nop(),
                        waits=[(A_EXP, exp_done)], inc=A_MSK)
            msk_done = sc.v(A_MSK)

            def avm(h, p0=p, j0=j, cc=c0, last0=last, first=(j == 0), g0=g, jg0=jg,
                    ot=kidx % 2):
                for hh in range(2):
                    ins = h.matmul(
                        psOT[0:65, ot, 512 * hh + cc:512 * hh + 512],
                        vbuf[:, g0, j0, hh, :],
                        p2[:, jg0, 512 * hh + cc:512 * hh + 512],
                        start=first, stop=last0)
                return ins
            avw = [(A_MSK, msk_done)]
            if j == 0 and kidx >= 2:
                avw.append((A_EV, 3 * (kidx - 1)))
            if last:
                sc.emit("tensor", avm, waits=avw, inc=A_AV)
                av_done = sc.v(A_AV)
                row = 2 * kidx
                eg = kidx % 2
                w = [(A_AV, av_done)]
                if kidx >= 2:
                    w.append((A_YD, 48 * (kidx - 1)))
                sc.emit("vector",
                        lambda h, e=eg, ot=kidx % 2: h.tensor_copy(
                            lstg[64:65, e, :], psOT[64:65, ot, :]),
                        waits=w, inc=A_EV)
                sc.emit("vector",
                        lambda h, e=eg, ot=kidx % 2: h.tensor_copy(
                            ystg[0:64, e, 0:512], psOT[0:64, ot, 0:512]),
                        inc=A_EV)
                sc.emit("vector",
                        lambda h, e=eg, ot=kidx % 2: h.tensor_copy(
                            ystg[0:64, e, 512:1024], psOT[0:64, ot, 512:1024]),
                        inc=A_EV)
                ev_here = sc.v(A_EV)
                sc.emit("gpsimd",
                        lambda h, p0=p, q0=qc, e=eg: h.dma_start(
                            out=d_yh[p0, 0:64, 512 * q0:512 * q0 + 512], in_=ystg[0:64, e, 0:512]),
                        waits=[(A_EV, ev_here)], inc=A_YD, inck=16)
                sc.emit("gpsimd",
                        lambda h, p0=p, q0=qc, e=eg: h.dma_start(
                            out=d_yh[p0, 64:128, 512 * q0:512 * q0 + 512], in_=ystg[0:64, e, 512:1024]),
                        inc=A_YD, inck=16)
                sc.emit("gpsimd",
                        lambda h, r0=row, e=eg: h.dma_start(
                            out=l_all[r0:r0 + 2, :],
                            in_=lstg[64:65, e, :].rearrange("p (a b) -> p a b", a=2)),
                        inc=A_YD, inck=16)
                kidx += 1
            else:
                pending_av = (avm, avw)
            nj += 1
        sc.emit("vector", lambda h: h.reciprocal(rl_all[:, :], l_all[:, :]),
                waits=[(A_YD, cnt[A_YD])], inc=A_EV)
        sc.run(nc)
    A_YD_ALL = cnt[A_YD]
    st3.close()
    st_msk.close()

    st_acc = ExitStack()
    acc2 = sb(st_acc, "acc2", [128, 8 * TOWN], F32)
    w_own = sb(st_acc, "w_own", [8, TOWN], F32)
    st_sc = ExitStack()
    scoresT = sb(st_sc, "scoresT", [8, TOWN], F32)
    st4 = ExitStack()
    wc_sb = sb(st4, "wc_sb", [128, NDC, C], BF16)
    rw_sb = sb(st4, "rw_sb", [128, NDC, 8], F32)
    ybuf = sb(st4, "ybuf", [128, 2, 8, 512], BF16)
    ystr = sb(st4, "ystr", [128, 2, 8, 512], BF16)
    xfb = sb(st4, "xfb", [128, 2, 512], F32)
    stag2 = sb(st4, "stag2", [2, 2, 512], F32)
    rbuf = sb(st4, "rbuf", [128, NDC, 512], F32)
    smls = sb(st4, "smls", [1, 3072], F32)
    xpbf = sb(st4, "xpbf", [128, NDC, 512], BF16)
    xpf = sb(st4, "xpf", [128, NDC, 512], F32)

    sc = Sched(sems, cnt)
    L4, M4, V4, S4, D4, G4, L4b, D4s = S(), S(), S(), S(), S(), S(), S(), S()
    with nc.psum_tensor("psRB", [128, 2, 512], F32) as psRB, \
         nc.psum_tensor("psAT", [128, 2, 512], F32) as psAT, \
         nc.psum_tensor("psMS", [1, 1024], F32) as psMS, \
         nc.psum_tensor("psSC", [8, 512], F32) as psSC:
        sc.emit("gpsimd", lambda h: h.dma_start(out=wc_sb[:, :, :], in_=d_wc[:, :, :]),
                inc=L4, inck=16)
        sc.emit("sync", lambda h: h.dma_start(out=rw_sb[:, :, :], in_=d_rwT[:, :, :]),
                inc=D4, inck=16)
        sc.emit("vector", lambda h: h.tensor_scalar_max(t_sb[:, :], t_sb[:, :], 0.1),
                waits=[(SL, LD_ALL)], inc=V4)
        sc.emit("vector", lambda h: h.reciprocal(t_sb[:, :], t_sb[:, :]), inc=V4)
        sc.emit("gpsimd", lambda h: h.partition_broadcast(rtemp[:, :], t_sb[:, :]),
                waits=[(V4, 2)], inc=L4b)
        ld4 = sc.v(L4)

        vnorm_done, resid_done, resid_v4, rlb_mm, statm_done = {}, {}, {}, {}, {}
        dbl_prev = [0]
        g4_prev = [0]
        for tc in range(NOW):
            g = tc % 2
            w = [(A_YD, A_YD_ALL)] if tc == 0 else []
            if tc >= 2:
                w.append((V4, vnorm_done[tc - 2]))
            sc.emit("sync", lambda h, t0=tc, g0=g: h.dma_start(
                out=ystr[:, g0, :, :],
                in_=d_yh[:, :, 512 * t0:512 * t0 + 512].transpose([1, 0, 2])),
                waits=w, inc=D4, inck=16)
            ld_here = sc.v(D4)

            for p in range(NPAIR):
                k0 = 2 * (NOW * p + tc)
                wg_ = []
                if NPAIR * tc + p >= 2:
                    wg_ = [(M4, rlb_mm.get(NPAIR * tc + p - 2, 0))]
                sc.emit("scalar", lambda h, k=k0, g0=p % 2: h.dma_start(
                    out=stag2[:, g0, :], in_=rl_all[k:k + 2, :]),
                    waits=wg_, inc=D4s, inck=16)
                gat_here = sc.v(D4s)
                w = [(D4s, gat_here), (L4, ld4)]
                if p >= 2:
                    w.append((V4, sc.v(V4) - 1))
                elif tc >= 1:
                    w.append((V4, vnorm_done[tc - 1]))
                sc.emit("tensor", lambda h, g0=p % 2: h.matmul(
                    psRB[:, g0, :], sel2[:, :], stag2[:, g0, :],
                    start=True, stop=True),
                    waits=w, inc=M4)
                rlb_mm[NPAIR * tc + p] = sc.v(M4)
                sc.emit("vector", lambda h, p0=p, g0=p % 2, gg=g: h.tensor_tensor(
                    ybuf[:, gg, p0, :], ystr[:, gg, p0, :], psRB[:, g0, :], ALU.mult),
                    waits=[(M4, sc.v(M4)), (D4, ld_here)], inc=V4)
            vnorm_done[tc] = sc.v(V4)

            for dout in range(NDC):
                gi = NDC * tc + dout
                w = [(V4, resid_v4[gi - 2])] if gi >= 2 else []
                sc.emit("sync", lambda h, t0=tc, d0=dout, g2=gi % 2: h.dma_start(
                    out=xfb[:, g2, :], in_=d_xTo[:, d0, 512 * t0:512 * t0 + 512]),
                    waits=w, inc=D4, inck=16)
                xf_here = sc.v(D4)
                w = [(V4, vnorm_done[tc])]
                if dout >= 2:
                    w.append((V4, sc.v(V4)))

                def wcm(h, d0=dout, g0=dout % 2, gg=g):
                    for p in range(NPAIR):
                        ins = h.matmul(psAT[:, g0, :],
                                       wc_sb[:, p, 128 * d0:128 * d0 + 128],
                                       ybuf[:, gg, p, :], start=(p == 0), stop=(p == NPAIR - 1))
                    return ins
                if dout < 2 and tc >= 1:
                    w.append((V4, dbl_prev[0]))
                sc.emit("tensor", wcm, waits=w, inc=M4)
                wrs = [(M4, sc.v(M4)), (D4, xf_here)]
                if dout == 0 and tc >= 1:
                    wrs.append((G4, g4_prev[0]))
                sc.emit("vector", lambda h, d0=dout, g0=dout % 2, g2=gi % 2: h.tensor_tensor(
                    rbuf[:, d0, :], psAT[:, g0, :], xfb[:, g2, :], ALU.add),
                    waits=wrs, inc=V4)
                resid_v4[gi] = sc.v(V4)
            resid_done[tc] = sc.v(V4)

            sc.emit("scalar", lambda h: h.activation(xpbf[:, :, :], rbuf[:, :, :], AF.Square),
                    waits=[(V4, resid_done[tc])], inc=S4)
            sc.emit("vector", lambda h, gg=g: h.tensor_copy(ybuf[:, gg, :, :], rbuf[:, :, :]),
                    waits=[(V4, vnorm_done[tc])], inc=V4)
            rdy = (sc.v(S4), sc.v(V4))

            def statm(h, gg=g):
                for d0 in range(NDC):
                    h.matmul(psMS[0:1, 0:512], ones_b[:, :], ybuf[:, gg, d0, :],
                             start=(d0 == 0), stop=(d0 == NDC - 1))
                for d0 in range(NDC):
                    ins = h.matmul(psMS[0:1, 512:1024], ones_b[:, :], xpbf[:, d0, :],
                                   start=(d0 == 0), stop=(d0 == NDC - 1))
                return ins
            sc.emit("tensor", statm, waits=[(S4, rdy[0]), (V4, rdy[1])], inc=M4)
            stat_done = sc.v(M4)
            statm_done[tc] = stat_done

            def small1(h):
                h.tensor_scalar_mul(smls[0:1, 0:512], psMS[0:1, 0:512], 1.0 / C)
                h.tensor_scalar_mul(smls[0:1, 512:1024], psMS[0:1, 512:1024], 1.0 / C)
                h.tensor_tensor(smls[0:1, 1024:1536], smls[0:1, 0:512], smls[0:1, 0:512], ALU.mult)
                h.tensor_tensor(smls[0:1, 1536:2048], smls[0:1, 512:1024], smls[0:1, 1024:1536], ALU.subtract)
                return h.tensor_scalar_add(smls[0:1, 1536:2048], smls[0:1, 1536:2048], 1e-5)
            sc.emit("vector", small1, waits=[(M4, stat_done)], inc=V4)
            sc.emit("scalar", lambda h: h.activation(smls[0:1, 2048:2560], smls[0:1, 1536:2048],
                                                     AF.Ln),
                    waits=[(V4, sc.v(V4))], inc=S4)
            sc.emit("scalar", lambda h: h.activation(smls[0:1, 2560:3072], smls[0:1, 2048:2560],
                                                     AF.Exp, scale=-0.5),
                    inc=S4)
            sc.emit("vector", lambda h: h.tensor_tensor(
                smls[0:1, 1024:1536], smls[0:1, 0:512], smls[0:1, 2560:3072], ALU.mult),
                waits=[(S4, sc.v(S4))], inc=V4)
            ms_done = sc.v(V4)

            def bcm(h):
                h.matmul(psAT[:, 0, :], ones1f[:, :], smls[0:1, 2560:3072], start=True, stop=True)
                return h.matmul(psAT[:, 1, :], ones1f[:, :], smls[0:1, 1024:1536], start=True, stop=True)
            sc.emit("tensor", bcm, waits=[(V4, ms_done)], inc=M4)
            bc_done = sc.v(M4)

            for dout in range(NDC):
                sc.emit("vector", lambda h, d0=dout: h.tensor_tensor(
                    xpf[:, d0, :], rbuf[:, d0, :], psAT[:, 0, :], ALU.mult),
                    waits=[(M4, bc_done)], inc=V4)
                sc.emit("vector", lambda h, d0=dout: h.tensor_tensor(
                    xpf[:, d0, :], xpf[:, d0, :], psAT[:, 1, :], ALU.subtract),
                    inc=V4)
            xpf_done = sc.v(V4)
            sc.emit("vector", lambda h: h.tensor_scalar_mul(
                rbuf[:, :, :], xpf[:, :, :], 2.0), waits=[], inc=V4)
            dbl_done = sc.v(V4)
            dbl_prev[0] = dbl_done

            def rtm(h):
                for d0 in range(NDC):
                    ins = h.matmul(psSC[:, :], rw_sb[:, d0, :], xpf[:, d0, :],
                                   start=(d0 == 0), stop=(d0 == NDC - 1))
                return ins
            sc.emit("tensor", rtm, waits=[(V4, xpf_done)], inc=M4)
            sc.emit("vector", lambda h, t0=tc: h.tensor_scalar(
                scoresT[:, 512 * t0:512 * t0 + 512], psSC[:, :],
                rb_sb[:, 0:1], rtemp[:, 0:1], ALU.add, ALU.mult),
                waits=[(M4, sc.v(M4)), (L4b, 1)], inc=V4)

            for dout in range(NDC):
                sc.emit("gpsimd", lambda h, t0=tc, d0=dout: h.dma_start(
                    out=acc2[:, TOWN * d0 + 512 * t0:TOWN * d0 + 512 * t0 + 512],
                    in_=rbuf[:, d0, :]),
                    waits=[(V4, dbl_done)] if dout == 0 else [], inc=G4, inck=16)
            g4_prev[0] = sc.v(G4)
        sc.run(nc)
    G4_ALL = cnt[G4]
    V4_ALL = cnt[V4]
    st4.close()

    st5 = ExitStack()
    scAll = sb(st5, "scAll", [8, 2 * TOWN], F32)
    sc_tm = sb(st5, "sc_tm", [128, 256], F32)
    mx32 = sb(st5, "mx32", [128, 32], F32)
    ssum = sb(st5, "ssum", [128, 32], F32)
    ebuf = sb(st5, "ebuf", [128, 256], F32)
    ebf = sb(st5, "ebf", [128, 256], BF16)
    tsb5 = sb(st5, "tsb5", [1, 256], F32)
    lnt8 = sb(st5, "lnt8", [1, 8], F32)
    lnt_b = sb(st5, "lnt_b", [128, 8], F32)
    gbuf = sb(st5, "gbuf", [128, 256], F32)
    eqb = sb(st5, "eqb", [128, 256], F32)
    g2b = sb(st5, "g2b", [128, 256], F32)
    v1b = sb(st5, "v1b", [128, 32], F32)
    v2b = sb(st5, "v2b", [128, 32], F32)
    w_tm = sb(st5, "w_tm", [128, 256], F32)
    w_em = sb(st5, "w_em", [8, T], F32)

    sc = Sched(sems, cnt)
    M5, V5, S5, G5, G5d, CCD, CCS, CCL = S(), S(), S(), S(), S(), S(), S(), S()
    v3 = lambda x: x[:, :].rearrange("p (j e) -> p j e", e=8)

    with nc.psum_tensor("psT5", [128, 256], F32) as psT5, \
         nc.psum_tensor("psS5", [1, 256], F32) as psS5, \
         nc.psum_tensor("psW5", [8, 512], F32) as psW5:
        sc.emit("gpsimd", lambda h: h.dma_start(out=d_scin[:, :], in_=scoresT[:, :]),
                waits=[(V4, V4_ALL)], inc=CCD, inck=16)
        sc.emit("gpsimd", lambda h: h.collective_compute(
            "AllGather", ALU.bypass, replica_groups=RG,
            ins=[d_scin.opt()], outs=[d_scout.opt()]),
            waits=[(CCD, 16)], inc=CCS, inck=1)
        sc.emit("gpsimd", lambda h: h.dma_start(
            out=scAll[:, :].rearrange("p (r t) -> p r t", r=2),
            in_=d_scout[:, :, :].transpose([1, 0, 2])),
            waits=[(CCS, 1)], inc=CCL, inck=16)

        def trs(h):
            for j in range(NKT):
                ins = h.transpose(psT5[:, 8 * j:8 * j + 8],
                                  scAll[0:8, 128 * j:128 * j + 128],
                                  ident_f[0:8, 0:8])
            return ins
        sc.emit("tensor", trs, waits=[(CCL, 16)], inc=M5)
        sc.emit("vector", lambda h: h.tensor_copy(sc_tm[:, :], psT5[:, :]),
                waits=[(M5, sc.v(M5))], inc=V5)

        tsb_done = {}
        for it in range(3):
            sc.emit("vector", lambda h: h.reduce_max(mx32[:, :], v3(sc_tm), axis=AX.X),
                    inc=V5)
            for e0 in range(8):
                sc.emit("vector", lambda h, e=e0: h.tensor_tensor(
                    v3(sc_tm)[:, :, e], v3(sc_tm)[:, :, e], mx32[:, :], ALU.subtract),
                    inc=V5)
            sc.emit("scalar", lambda h: h.activation(ebuf[:, :], sc_tm[:, :], AF.Exp),
                    waits=[(V5, sc.v(V5))], inc=S5)
            sc.emit("vector", lambda h: h.reduce_sum(ssum[:, :], v3(ebuf), axis=AX.X),
                    waits=[(S5, sc.v(S5))], inc=V5)
            sc.emit("scalar", lambda h: h.activation(ssum[:, :], ssum[:, :], AF.Ln),
                    waits=[(V5, sc.v(V5))], inc=S5)
            for e0 in range(8):
                sc.emit("vector", lambda h, e=e0: h.tensor_tensor(
                    v3(sc_tm)[:, :, e], v3(sc_tm)[:, :, e], ssum[:, :], ALU.subtract),
                    waits=[(S5, sc.v(S5))] if e0 == 0 else [], inc=V5)
            sc.emit("scalar", lambda h: h.activation(ebf[:, :], sc_tm[:, :], AF.Exp),
                    waits=[(V5, sc.v(V5))], inc=S5)
            sc.emit("tensor", lambda h: h.matmul(psS5[:, :], ones_b[:, :], ebf[:, :],
                                                 start=True, stop=True),
                    waits=[(S5, sc.v(S5))] + ([(V5, tsb_done[it - 1])] if it > 0 else []),
                    inc=M5)
            sc.emit("vector", lambda h: h.tensor_copy(tsb5[:, :], psS5[:, :]),
                    waits=[(M5, sc.v(M5))], inc=V5)
            tsb_done[it] = sc.v(V5)
            sc.emit("vector", lambda h: h.reduce_sum(
                lnt8[:, :], tsb5[:, :].rearrange("p (j e) -> p e j", e=8), axis=AX.X),
                inc=V5)
            sc.emit("scalar", lambda h: h.activation(lnt8[:, :], lnt8[:, :], AF.Ln),
                    waits=[(V5, sc.v(V5))], inc=S5)
            sc.emit("gpsimd", lambda h: h.partition_broadcast(lnt_b[:, :], lnt8[:, :]),
                    waits=[(S5, sc.v(S5))], inc=G5)
            for e0 in range(8):
                sc.emit("vector", lambda h, e=e0: h.tensor_scalar_sub(
                    v3(sc_tm)[:, :, e], v3(sc_tm)[:, :, e], lnt_b[:, e:e + 1]),
                    waits=[(G5, sc.v(G5))] if e0 == 0 else [], inc=V5)

        sc.emit("scalar", lambda h: h.activation(gbuf[:, :], sc_tm[:, :], AF.Exp),
                waits=[(V5, sc.v(V5))], inc=S5)
        sc.emit("vector", lambda h: h.reduce_sum(ssum[:, :], v3(gbuf), axis=AX.X),
                waits=[(S5, sc.v(S5))], inc=V5)
        sc.emit("vector", lambda h: h.tensor_scalar_add(ssum[:, :], ssum[:, :], 1e-8), inc=V5)
        sc.emit("vector", lambda h: h.reciprocal(ssum[:, :], ssum[:, :]), inc=V5)
        rec_done = sc.v(V5)
        for e0 in range(8):
            sc.emit("vector", lambda h, e=e0: h.tensor_tensor(
                v3(gbuf)[:, :, e], v3(gbuf)[:, :, e], ssum[:, :], ALU.mult),
                waits=[(V5, rec_done)] if e0 == 0 else [], inc=V5)
        sc.emit("vector", lambda h: h.reduce_max(v1b[:, :], v3(gbuf), axis=AX.X), inc=V5)
        for e0 in range(8):
            sc.emit("vector", lambda h, e=e0: h.tensor_tensor(
                v3(eqb)[:, :, e], v3(gbuf)[:, :, e], v1b[:, :], ALU.is_equal), inc=V5)
        for e0 in range(8):
            sc.emit("vector", lambda h, e=e0: h.scalar_tensor_tensor(
                v3(g2b)[:, :, e], v3(eqb)[:, :, e], -1e30, v3(gbuf)[:, :, e],
                ALU.mult, ALU.add), inc=V5)
        sc.emit("vector", lambda h: h.reduce_max(v2b[:, :], v3(g2b), axis=AX.X), inc=V5)
        sc.emit("vector", lambda h: h.tensor_tensor(v1b[:, :], v1b[:, :], v2b[:, :], ALU.add),
                inc=V5)
        sc.emit("vector", lambda h: h.tensor_scalar_add(v1b[:, :], v1b[:, :], 1e-8), inc=V5)
        sc.emit("vector", lambda h: h.reciprocal(v1b[:, :], v1b[:, :]), inc=V5)
        rec2_done = sc.v(V5)
        for e0 in range(8):
            sc.emit("vector", lambda h, e=e0: h.tensor_tensor(
                v3(eqb)[:, :, e], v3(gbuf)[:, :, e], v2b[:, :], ALU.is_ge),
                waits=[(V5, rec2_done)] if e0 == 0 else [], inc=V5)
        for e0 in range(8):
            sc.emit("vector", lambda h, e=e0: h.tensor_tensor(
                v3(w_tm)[:, :, e], v3(gbuf)[:, :, e], v3(eqb)[:, :, e], ALU.mult), inc=V5)
        for e0 in range(8):
            sc.emit("vector", lambda h, e=e0: h.tensor_tensor(
                v3(w_tm)[:, :, e], v3(w_tm)[:, :, e], v1b[:, :], ALU.mult), inc=V5)
        wtm_done = sc.v(V5)

        for c in range(8):
            w = [(V5, wtm_done if c == 0 else sc.v(V5))]

            def trb(h, c0=c):
                for jj in range(4):
                    j = 4 * c0 + jj
                    ins = h.transpose(psW5[:, 128 * jj:128 * jj + 128],
                                      w_tm[:, 8 * j:8 * j + 8], ident_f[:, :])
                return ins
            sc.emit("tensor", trb, waits=w, inc=M5)
            sc.emit("vector", lambda h, c0=c: h.tensor_copy(
                w_em[:, 512 * c0:512 * c0 + 512], psW5[:, :]),
                waits=[(M5, sc.v(M5))], inc=V5)

        def wslice(h):
            off = h.value_load(soff_sb[0:1, 0:1])
            return h.dma_start(out=w_own[:, :], in_=w_em[:, bass.ds(off, TOWN)])
        sc.emit("gpsimd", wslice, waits=[(V5, sc.v(V5))], inc=G5d, inck=16)
        if _DBG_SCRATCH:
            sc.emit("sync", lambda h: h.dma_start(out=d_dbg1[:, :], in_=l_all[:, :]),
                    waits=[(G5d, sc.v(G5d))], inc=G5d, inck=16)
            sc.emit("sync", lambda h: h.dma_start(out=d_dbg2[:, :], in_=scAll[:, :]),
                    inc=G5d, inck=16)
            sc.emit("sync", lambda h: h.dma_start(out=d_dbg3[:, :], in_=w_own[:, :]),
                    inc=G5d, inck=16)
            sc.emit("sync", lambda h: None, waits=[(G5d, sc.v(G5d))])
        sc.run(nc)
    G5_ALL = cnt[G5d]
    st5.close()
    st_sc.close()

    st6 = ExitStack()
    xpo = sb(st6, "xpo", [128, NDC, TOWN], BF16)
    gpw = sb(st6, "gpw", [128, 8, TOWN], BF16)
    grelu = sb(st6, "grelu", [128, TOWN], BF16)
    wgp_f = sb(st6, "wgp_f", [128, 2, NDC, 128], F32)
    wgp_r = sb(st6, "wgp_r", [128, 2, 2, NDC, 128], BF16)
    wo_f = sb(st6, "wo_f", [128, 2, 8, 128], F32)
    wo_r = sb(st6, "wo_r", [128, 2, 8, 128], BF16)
    wbc = sb(st6, "wbc", [128, TOWN], F32)
    wstag = sb(st6, "wstag", [1, TOWN], F32)
    wbcb = sb(st6, "wbcb", [128, 2, TOWN], BF16)

    sc = Sched(sems, cnt)
    L6, M6, V6, S6, G6, G6d, C6 = S(), S(), S(), S(), S(), S(), S()

    with nc.psum_tensor("psG", [128, 1024], F32) as psG, \
         nc.psum_tensor("psP6", [128, 1024], F32) as psP6, \
         nc.psum_tensor("psO6", [128, 2, 1024], F32) as psO6:
        if _DBG_SCRATCH:
            DBG9 = S()
            sc.emit("sync", lambda h: h.dma_start(out=d_dbg9[:, :], in_=acc2[:, :]),
                    waits=[(G4, G4_ALL)], inc=DBG9, inck=16)
            sc.emit("sync", lambda h: None, waits=[(DBG9, 16)])
        sc.emit("vector", lambda h: h.tensor_scalar_mul(
            xpo[:, :, :], acc2[:, :].rearrange("p (d t) -> p d t", d=NDC), 0.5),
            waits=[(G4, G4_ALL)], inc=V6)
        xpo_done = sc.v(V6)

        om_last = [0]
        acc_stt = {}
        cast_cnt = {}
        kcast = {}
        wocast = {}
        nfc = 0
        nth = 0
        ndo = 0
        wgld, wold, wbcd, gpend, grelu_rd = {}, {}, {}, {}, {}
        relu_cnt = {}
        stt_cnt = {}
        for e0 in range(E):
            w = [(G5d, G5_ALL)]
            if e0 >= 2:
                w.append((V6, gpend[e0 - 2]))
            if e0 >= 1:
                w.append((V6, wbcd[e0 - 1]))
            sc.emit("gpsimd", lambda h, e=e0: h.dma_start(
                out=wstag[:, :], in_=w_own[e:e + 1, :]), waits=w, inc=G6d, inck=16)
            sc.emit("gpsimd", lambda h: h.partition_broadcast(
                wbc[:, :], wstag[:, :]), waits=[(G6d, sc.v(G6d))], inc=G6)
            sc.emit("vector", lambda h, e=e0: h.tensor_copy(wbcb[:, e % 2, :], wbc[:, :]),
                    waits=[(G6, sc.v(G6))], inc=V6)
            wbcd[e0] = sc.v(V6)

            for half in range(2):
                for f8 in range(8):
                    fc = 8 * half + f8
                    for kd in range(2):
                        nk = 2 * nfc + kd
                        w = [(C6, kcast.get(nk - 2, 0))] if nk >= 2 else []
                        sc.emit("sync", lambda h, e=e0, f=fc, k=kd, g=nk % 2: h.dma_start(
                            out=wgp_f[:, g, :, :], in_=d_wgp[e, f, :, k, :, :]),
                            waits=w, inc=L6, inck=16)
                        wc_ = [(L6, sc.v(L6))]
                        if nfc >= 2:
                            wc_.append((M6, wgld[nfc - 2]))
                        sc.emit("scalar", lambda h, k=kd, g=nk % 2, gr=nfc % 2: h.copy(
                            wgp_r[:, gr, k, :, :], wgp_f[:, g, :, :]), waits=wc_, inc=C6)
                        kcast[nk] = sc.v(C6)
                    cast_cnt[nfc] = sc.v(C6)

                    for th in range(2):
                        w = [(C6, cast_cnt[nfc]), (V6, xpo_done)]
                        if nth >= 1:
                            w.append((S6, relu_cnt[nth - 1]))

                        def gm(h, g=nfc % 2, t1=th):
                            for d0 in range(NDC):
                                for t0 in range(2):
                                    ins = h.matmul(
                                        psG[:, 512 * t0:512 * t0 + 512],
                                        wgp_r[:, g, 0, d0, :],
                                        xpo[:, d0, 1024 * t1 + 512 * t0:1024 * t1 + 512 * t0 + 512],
                                        start=(d0 == 0), stop=(d0 == NDC - 1))
                            return ins
                        sc.emit("tensor", gm, waits=w, inc=M6)
                        wr = [(M6, sc.v(M6))]
                        if nth >= 1:
                            wr.append((V6, grelu_rd[nth - 1]))
                        sc.emit("scalar", lambda h, t1=th: h.activation(
                            grelu[:, 1024 * t1:1024 * t1 + 1024], psG[:, :], AF.Relu),
                            waits=wr, inc=S6)
                        relu_cnt[nth] = sc.v(S6)

                        w = []
                        if nth >= 1:
                            w.append((V6, stt_cnt[nth - 1]))

                        def pm(h, g=nfc % 2, t1=th):
                            for d0 in range(NDC):
                                for t0 in range(2):
                                    ins = h.matmul(
                                        psP6[:, 512 * t0:512 * t0 + 512],
                                        wgp_r[:, g, 1, d0, :],
                                        xpo[:, d0, 1024 * t1 + 512 * t0:1024 * t1 + 512 * t0 + 512],
                                        start=(d0 == 0), stop=(d0 == NDC - 1))
                            return ins
                        sc.emit("tensor", pm, waits=w, inc=M6)
                        pm_done = sc.v(M6)
                        if th == 1:
                            wgld[nfc] = pm_done

                        w = [(M6, pm_done), (S6, relu_cnt[nth]), (V6, wbcd[e0])]
                        if half == 1 or e0 > 0:
                            w.append((M6, om_last[0]))
                        sc.emit("vector", lambda h, f=f8, t1=th: h.scalar_tensor_tensor(
                            gpw[:, f, 1024 * t1:1024 * t1 + 1024], psP6[:, :], 1.0,
                            grelu[:, 1024 * t1:1024 * t1 + 1024], ALU.mult, ALU.mult),
                            waits=w, inc=V6)
                        sc.emit("vector", lambda h, f=f8, e=e0, t1=th: h.tensor_tensor(
                            gpw[:, f, 1024 * t1:1024 * t1 + 1024],
                            gpw[:, f, 1024 * t1:1024 * t1 + 1024],
                            wbcb[:, e % 2, 1024 * t1:1024 * t1 + 1024], ALU.mult),
                            inc=V6)
                        stt_cnt[nth] = sc.v(V6)
                        grelu_rd[nth] = sc.v(V6)
                        nth += 1
                    nfc += 1
                gpw_done = sc.v(V6)

                for dout in range(NDC):
                    w = [(C6, wocast.get(ndo // 2 - 2, 0))] if ndo >= 4 else []
                    sc.emit("sync", lambda h, e=e0, hf=half, d0=dout, g=(ndo // 2) % 2: h.dma_start(
                        out=wo_f[:, g, :, :], in_=d_wo[e, hf, d0, :, :, :]),
                        waits=w, inc=L6, inck=16)
                    wc_ = [(L6, sc.v(L6))]
                    if ndo >= 4:
                        wc_.append((M6, wold[ndo // 2 - 2]))
                    sc.emit("scalar", lambda h, g=(ndo // 2) % 2: h.copy(
                        wo_r[:, g, :, :], wo_f[:, g, :, :]), waits=wc_, inc=C6)
                    wocast[ndo // 2] = sc.v(C6)
                    for th in range(2):
                        w = [(C6, wocast[ndo // 2]), (V6, gpw_done)]
                        if ndo >= 2:
                            w.append((V6, acc_stt[ndo - 2]))

                        def om(h, g=(ndo // 2) % 2, og=ndo % 2, t1=th):
                            for f in range(8):
                                for t0 in range(2):
                                    ins = h.matmul(
                                        psO6[:, og, 512 * t0:512 * t0 + 512],
                                        wo_r[:, g, f, :],
                                        gpw[:, f, 1024 * t1 + 512 * t0:1024 * t1 + 512 * t0 + 512],
                                        start=(f == 0), stop=(f == 7))
                            return ins
                        sc.emit("tensor", om, waits=w, inc=M6)
                        om_last[0] = sc.v(M6)
                        if th == 1:
                            wold[ndo // 2] = sc.v(M6)
                        sc.emit("vector", lambda h, d0=dout, og=ndo % 2, t1=th: h.scalar_tensor_tensor(
                            acc2[:, TOWN * d0 + 1024 * t1:TOWN * d0 + 1024 * t1 + 1024],
                            psO6[:, og, :], 1.0,
                            acc2[:, TOWN * d0 + 1024 * t1:TOWN * d0 + 1024 * t1 + 1024],
                            ALU.mult, ALU.add),
                            waits=[(M6, sc.v(M6))], inc=V6)
                        acc_stt[ndo] = sc.v(V6)
                        ndo += 1
            gpend[e0] = sc.v(V6)
        sc.run(nc)
    st6.close()

    st7 = ExitStack()
    rbf7 = sb(st7, "rbf7", [128, NDC, 512], BF16)
    sqb7 = sb(st7, "sqb7", [128, NDC, 512], BF16)
    sml7 = sb(st7, "sml7", [1, 3072], F32)
    ost = sb(st7, "ost", [128, 2, NDC, 512], F32)

    sc = Sched(sems, cnt)
    M7, V7, S7, D7 = S(), S(), S(), S()
    statm7_done = {}
    with nc.psum_tensor("psM7", [1, 1024], F32) as psM7, \
         nc.psum_tensor("psB7", [128, 2, 512], F32) as psB7:
        accv = acc2[:, :].rearrange("p (d t) -> p d t", d=NDC)
        for tc in range(4):
            av = accv[:, :, 512 * tc:512 * tc + 512]
            sc.emit("scalar", lambda h, a=av: h.activation(sqb7[:, :, :], a, AF.Square),
                    inc=S7)
            sc.emit("vector", lambda h, a=av: h.tensor_copy(rbf7[:, :, :], a),
                    waits=[(M7, statm7_done.get(tc - 1, 0))], inc=V7)
            rdy = (sc.v(S7), sc.v(V7))

            def statm7(h):
                for d0 in range(NDC):
                    h.matmul(psM7[0:1, 0:512], ones_b[:, :], rbf7[:, d0, :],
                             start=(d0 == 0), stop=(d0 == NDC - 1))
                for d0 in range(NDC):
                    ins = h.matmul(psM7[0:1, 512:1024], ones_b[:, :], sqb7[:, d0, :],
                                   start=(d0 == 0), stop=(d0 == NDC - 1))
                return ins
            sc.emit("tensor", statm7, waits=[(S7, rdy[0]), (V7, rdy[1])], inc=M7)
            statm7_done[tc] = sc.v(M7)

            def small7(h):
                h.tensor_scalar_mul(sml7[0:1, 0:512], psM7[0:1, 0:512], 1.0 / C)
                h.tensor_scalar_mul(sml7[0:1, 512:1024], psM7[0:1, 512:1024], 1.0 / C)
                h.tensor_tensor(sml7[0:1, 1024:1536], sml7[0:1, 0:512], sml7[0:1, 0:512], ALU.mult)
                h.tensor_tensor(sml7[0:1, 1536:2048], sml7[0:1, 512:1024], sml7[0:1, 1024:1536], ALU.subtract)
                h.tensor_scalar_add(sml7[0:1, 1536:2048], sml7[0:1, 1536:2048], 1e-5)
                return h.reciprocal(sml7[0:1, 2048:2560], sml7[0:1, 1536:2048])
            sc.emit("vector", small7, waits=[(M7, sc.v(M7))], inc=V7)
            sc.emit("scalar", lambda h: h.activation(sml7[0:1, 2560:3072], sml7[0:1, 2048:2560],
                                                     AF.Sqrt),
                    waits=[(V7, sc.v(V7))], inc=S7)
            sc.emit("vector", lambda h: h.tensor_tensor(
                sml7[0:1, 1024:1536], sml7[0:1, 0:512], sml7[0:1, 2560:3072], ALU.mult),
                waits=[(S7, sc.v(S7))], inc=V7)

            def bcm7(h):
                h.matmul(psB7[:, 0, :], ones1f[:, :], sml7[0:1, 2560:3072], start=True, stop=True)
                return h.matmul(psB7[:, 1, :], ones1f[:, :], sml7[0:1, 1024:1536], start=True, stop=True)
            sc.emit("tensor", bcm7, waits=[(V7, sc.v(V7))], inc=M7)
            bc7 = sc.v(M7)

            g = tc % 2
            for dout in range(NDC):
                w = [(M7, bc7)]
                if tc >= 2 and dout == 0:
                    w.append((D7, 16 * (tc - 1)))
                sc.emit("vector", lambda h, a=av, d0=dout, g0=g: h.tensor_tensor(
                    ost[:, g0, d0, :], a[:, d0, :], psB7[:, 0, :], ALU.mult),
                    waits=w, inc=V7)
                sc.emit("vector", lambda h, d0=dout, g0=g: h.tensor_tensor(
                    ost[:, g0, d0, :], ost[:, g0, d0, :], psB7[:, 1, :], ALU.subtract),
                    inc=V7)
            sc.emit("sync", lambda h, t0=tc, g0=g: h.dma_start(
                out=d_yout[:, :, 512 * t0:512 * t0 + 512], in_=ost[:, g0, :, :]),
                waits=[(V7, sc.v(V7))], inc=D7, inck=16)
        sc.emit("sync", lambda h: None, waits=[(D7, sc.v(D7))])
        sc.run(nc)
    st7.close()
    st_acc.close()

    nc.finalize()
    _CACHE["nc"] = nc
    return nc


def _host_prep(inputs):
    x = np.asarray(inputs["L0_attn_x"], np.float32)
    ac = np.ascontiguousarray
    r3 = lambda a: ac(a.reshape(NDC, 128, -1).transpose(1, 0, 2))

    cp, sp = _rope_tables()
    sel = np.zeros((2, 128), np.float32)
    sel[0, 0:64] = 1.0
    sel[1, 64:128] = 1.0

    wg = np.asarray(inputs["L0_ffn_wg"], np.float32)
    wp = np.asarray(inputs["L0_ffn_wp"], np.float32)
    wo = np.asarray(inputs["L0_ffn_wo"], np.float32)
    wgs = wg.reshape(E, NDC, 128, 16, 128)
    wps = wp.reshape(E, NDC, 128, 16, 128)
    wgp = ac(np.stack([wgs, wps], axis=1).transpose(0, 4, 3, 1, 2, 5))
    wo_h = ac(wo.reshape(E, 2, 8, 128, 8, 128).transpose(0, 1, 4, 3, 2, 5))

    common = {
        "wqdT": r3(np.asarray(inputs["L0_attn_Wq_down"], np.float32).T),
        "wkdT": r3(np.asarray(inputs["L0_attn_Wk_down"], np.float32).T),
        "wvdT": r3(np.asarray(inputs["L0_attn_Wv_down"], np.float32).T),
        "upqc": np.asarray(inputs["L0_attn_Wq_up_c"], np.float32),
        "upqe": np.asarray(inputs["L0_attn_Wq_up_e"], np.float32),
        "upqeT": ac(np.asarray(inputs["L0_attn_Wq_up_e"], np.float32).T),
        "upkc": np.asarray(inputs["L0_attn_Wk_up_c"], np.float32),
        "upke": np.asarray(inputs["L0_attn_Wk_up_e"], np.float32),
        "upkeT": ac(np.asarray(inputs["L0_attn_Wk_up_e"], np.float32).T),
        "upv": np.asarray(inputs["L0_attn_Wv_up"], np.float32),
        "Pm": _perm_matrix(),
        "wc": r3(np.asarray(inputs["L0_attn_Wc"], np.float32)),
        "rwT": r3(np.asarray(inputs["L0_ffn_router_w"], np.float32).T),
        "rb": ac(np.asarray(inputs["L0_ffn_router_b"], np.float32).reshape(8, 1)),
        "temp": ac(np.asarray(inputs["L0_ffn_temperature"], np.float32).reshape(1, 1)),
        "wgp": wgp,
        "wo": wo_h,
        "cosp": cp,
        "sinp": sp,
        "ident": np.eye(128, dtype=np.float32),
        "sel2": sel,
    }
    xts = [r3(ac(x[b].T)) for b in range(B)]
    xos = [[r3(ac(x[b].T[:, par::2])) for par in range(2)] for b in range(B)]
    cosos = [ac(cp[:, par::2]) for par in range(2)]
    sinos = [ac(sp[:, par::2]) for par in range(2)]
    kk = np.arange(128)[:, None]
    qq = np.arange(512)[None, :]
    msks = []
    for par in range(2):
        m = np.zeros((128, 8, 512), np.float32)
        for r in range(8):
            m[:, r, :] = (128 * r + kk <= 2 * qq + par)
        msks.append(m)
    soffs = [np.array([[TOWN * par]], np.int32) for par in range(2)]
    return (common, xts, xos, cosos, sinos, msks, soffs)


def kernel(**inputs):
    common, xts, xos, cosos, sinos, msks, soffs = _host_prep(inputs)
    nc = build_nc()
    in_maps = []
    for c in range(8):
        b, par = c // 2, c % 2
        m = dict(common)
        m["xT"] = xts[b]
        m["xTo"] = xos[b][par]
        m["coso"] = cosos[par]
        m["sino"] = sinos[par]
        m["msk"] = msks[par]
        m["soff"] = soffs[par]
        in_maps.append(m)
    res = run_bass_kernel_spmd(nc, in_maps, core_ids=list(range(8)))
    out = np.empty((B, T, C), np.float32)
    for c in range(8):
        b, par = c // 2, c % 2
        y = np.asarray(res.results[c]["yout"])
        out[b, par::2, :] = y.transpose(1, 0, 2).reshape(C, TOWN).T
    return out


# version 11
# speedup vs baseline: 1.5566x; 1.1424x over previous
import numpy as np
from contextlib import ExitStack

import concourse.bacc as bacc
import concourse.bass as bass
import concourse.mybir as mybir
from concourse.bass_utils import run_bass_kernel_spmd

F32 = mybir.dt.float32
BF16 = mybir.dt.bfloat16
I32 = mybir.dt.int32
AF = mybir.ActivationFunctionType
ALU = mybir.AluOpType
AX = mybir.AxisListType

B, T, C = 4, 4096, 1024
H, DH, DHE = 16, 64, 32
NPAIR = 8
E, FF = 8, 2048
TOWN = 2048
NDC = C // 128
NTC = T // 512
NOW = TOWN // 512
NKT = T // 128
RG = [[0, 1], [2, 3], [4, 5], [6, 7]]

ENGINES = ("tensor", "vector", "scalar", "gpsimd", "sync")


class Sched:

    def __init__(self, sems, cnt):
        self.sems = sems
        self.cnt = cnt
        self.ops = {e: [] for e in ENGINES}

    def emit(self, eng, fn, waits=(), inc=None, inck=1):
        self.ops[eng].append((fn, tuple(waits), inc, inck))
        if inc is not None:
            self.cnt[inc] += inck

    def v(self, si):
        return self.cnt[si]

    def run(self, nc):
        sems = self.sems
        with nc.Block() as blk:
            for eng in ENGINES:
                lst = self.ops[eng]
                if not lst:
                    continue

                def make(lst):
                    def body(h):
                        for (fn, waits, inc, inck) in lst:
                            for (si, val) in waits:
                                if val > 0:
                                    h.wait_ge(sems[si], val)
                            ins = fn(h)
                            if inc is not None:
                                ins.then_inc(sems[inc], inck)
                    return body
                getattr(blk, eng)(make(lst))


def _rope_tables():
    inv = 1.0 / (10000.0 ** (np.arange(0, DHE, 2, dtype=np.float64) / DHE))
    freqs = np.arange(T, dtype=np.float64)[:, None] * inv[None, :]
    emb = np.concatenate([freqs, freqs], axis=1)
    cos = np.cos(emb).astype(np.float32).T
    sin = np.sin(emb).astype(np.float32).T
    cp = np.ones((128, T), np.float32)
    sp = np.zeros((128, T), np.float32)
    cp[32:64] = cos
    cp[96:128] = cos
    sp[32:64] = sin
    sp[96:128] = sin
    return cp, sp


def _perm_matrix():
    P = np.zeros((DHE, DHE), np.float32)
    for i in range(DHE // 2):
        P[2 * i + 1, 2 * i] = -1.0
        P[2 * i, 2 * i + 1] = 1.0
    return P


_CACHE = {}
_DBG_SCRATCH = False


def build_nc():
    if "nc" in _CACHE:
        return _CACHE["nc"]
    nc = bacc.Bacc("TRN2", target_bir_lowering=False, detect_race_conditions=False)

    dt = lambda n, s, d, k: nc.dram_tensor(n, s, d, kind=k).ap()
    d_xT = dt("xT", [128, NDC, T], F32, "ExternalInput")
    d_xTo = dt("xTo", [128, NDC, TOWN], F32, "ExternalInput")
    d_wqdT = dt("wqdT", [128, NDC, C], F32, "ExternalInput")
    d_wkdT = dt("wkdT", [128, NDC, C], F32, "ExternalInput")
    d_wvdT = dt("wvdT", [128, NDC, C], F32, "ExternalInput")
    d_upqc = dt("upqc", [64, 32], F32, "ExternalInput")
    d_upqe = dt("upqe", [64, 32], F32, "ExternalInput")
    d_upqeT = dt("upqeT", [32, 64], F32, "ExternalInput")
    d_upkc = dt("upkc", [64, 32], F32, "ExternalInput")
    d_upke = dt("upke", [64, 32], F32, "ExternalInput")
    d_upkeT = dt("upkeT", [32, 64], F32, "ExternalInput")
    d_upv = dt("upv", [64, 64], F32, "ExternalInput")
    d_P = dt("Pm", [32, 32], F32, "ExternalInput")
    d_wc = dt("wc", [128, NDC, C], F32, "ExternalInput")
    d_rwT = dt("rwT", [128, NDC, 8], F32, "ExternalInput")
    d_rb = dt("rb", [8, 1], F32, "ExternalInput")
    d_temp = dt("temp", [1, 1], F32, "ExternalInput")
    d_wgp = dt("wgp", [E, 16, 128, 2, NDC, 128], F32, "ExternalInput")
    d_wo = dt("wo", [E, 2, 8, 128, 8, 128], F32, "ExternalInput")
    d_cos = dt("cosp", [128, T], F32, "ExternalInput")
    d_sin = dt("sinp", [128, T], F32, "ExternalInput")
    d_coso = dt("coso", [128, TOWN], F32, "ExternalInput")
    d_sino = dt("sino", [128, TOWN], F32, "ExternalInput")
    d_msk = dt("msk", [128, 8, 512], F32, "ExternalInput")
    d_ident = dt("ident", [128, 128], F32, "ExternalInput")
    d_sel2 = dt("sel2", [2, 128], F32, "ExternalInput")
    d_soff = dt("soff", [1, 1], I32, "ExternalInput")

    d_yout = dt("yout", [128, 8, TOWN], F32, "ExternalOutput")

    _knd = "ExternalOutput" if _DBG_SCRATCH else "Internal"
    d_qh = dt("qh", [NPAIR, 128, TOWN], BF16, _knd)
    d_kh = dt("kh", [NPAIR, 128, T], BF16, _knd)
    d_vh = dt("vh", [128, NKT, H, 65], BF16, _knd)
    d_yh = dt("yh", [NPAIR, 128, TOWN], BF16, _knd)
    d_scin = dt("scin", [8, TOWN], F32, "Internal")
    d_scout = dt("scout", [2, 8, TOWN], F32, "Internal")
    if _DBG_SCRATCH:
        d_dbg1 = dt("dbg1", [128, 512], F32, "ExternalOutput")
        d_dbg2 = dt("dbg2", [8, 2 * TOWN], F32, "ExternalOutput")
        d_dbg3 = dt("dbg3", [8, TOWN], F32, "ExternalOutput")
        d_dbg9 = dt("dbg9", [128, 8 * TOWN], F32, "ExternalOutput")

    es = ExitStack()
    nc._keepalive = es
    sems = [es.enter_context(nc.semaphore(f"sm{i}")) for i in range(64)]
    cnt = {i: 0 for i in range(64)}
    _ns = [0]

    def S():
        i = _ns[0]
        _ns[0] += 1
        return i

    sb = lambda st, name, shape, d: st.enter_context(nc.sbuf_tensor(name, shape, d))

    ident_f = sb(es, "ident_f", [128, 128], F32)
    ones_b = sb(es, "ones_b", [128, 1], BF16)
    ones1f = sb(es, "ones1f", [1, 128], F32)
    sel2 = sb(es, "sel2s", [2, 128], F32)
    l_all = sb(es, "l_all", [128, 512], F32)
    rl_all = sb(es, "rl_all", [128, 512], F32)
    rb_sb = sb(es, "rb_sb", [8, 1], F32)
    rtemp = sb(es, "rtemp", [8, 1], F32)
    t_sb = sb(es, "t_sb", [1, 1], F32)
    soff_sb = sb(es, "soff_sb", [1, 1], I32)
    st_msk = ExitStack()
    msk_sb = sb(st_msk, "msk_sb", [128, 8, 512], BF16)

    st_x = ExitStack()
    xT = sb(st_x, "xT_sb", [128, NDC, T], BF16)
    cosp = sb(st_x, "cosp_sb", [128, T], BF16)
    sinp = sb(st_x, "sinp_sb", [128, T], BF16)
    st_w = ExitStack()
    wq_eff = sb(st_w, "wq_eff", [128, NDC, C], BF16)
    wq_rot = sb(st_w, "wq_rot", [128, NDC, C], BF16)
    wk_eff = sb(st_w, "wk_eff", [128, NDC, C], BF16)
    wk_rot = sb(st_w, "wk_rot", [128, NDC, C], BF16)
    st_fold = ExitStack()
    wdT2 = sb(st_fold, "wdT2", [128, 2 * NDC, C], BF16)
    upfq = sb(st_fold, "upfq", [128, 128], BF16)
    upfk = sb(st_fold, "upfk", [128, 128], BF16)
    upeT2 = sb(st_fold, "upeT2", [32, 2, 64], BF16)
    P_sb = sb(st_fold, "P_sb", [32, 32], BF16)

    sc = Sched(sems, cnt)
    SL, SLg, SLm, SLX = S(), S(), S(), S()
    for dst, src in [(ident_f, d_ident), (sel2, d_sel2), (rb_sb, d_rb), (t_sb, d_temp),
                     (soff_sb, d_soff)]:
        sc.emit("sync", lambda h, a=dst, b=src: h.dma_start(out=a[:, :], in_=b[:, :]),
                inc=SL, inck=16)
    sc.emit("gpsimd", lambda h: h.dma_start(out=xT[:, 0:4, :], in_=d_xT[:, 0:4, :]),
            inc=SLX, inck=16)
    sc.emit("gpsimd", lambda h: h.dma_start(out=xT[:, 4:8, :], in_=d_xT[:, 4:8, :]),
            inc=SLX, inck=16)
    for dst, src in [(cosp, d_cos), (sinp, d_sin), (P_sb, d_P)]:
        sc.emit("gpsimd", lambda h, a=dst, b=src: h.dma_start(out=a[:, :], in_=b[:, :]),
                inc=SLg, inck=16)
    sc.emit("gpsimd", lambda h: h.dma_start(out=msk_sb[:, :, :], in_=d_msk[:, :, :]),
            inc=SLg, inck=16)
    for (dst, c0, c1, src) in [(upfq, 0, 32, d_upqc), (upfq, 32, 64, d_upqe),
                               (upfk, 0, 32, d_upkc), (upfk, 32, 64, d_upke)]:
        sc.emit("gpsimd", lambda h, a=dst, x0=c0, x1=c1, b=src: h.dma_start(
            out=a[0:64, x0:x1], in_=b[:, :]), inc=SLg, inck=16)
    sc.emit("gpsimd", lambda h: h.dma_start(out=upeT2[:, 0, :], in_=d_upqeT[:, :]),
            inc=SLg, inck=16)
    sc.emit("gpsimd", lambda h: h.dma_start(out=upeT2[:, 1, :], in_=d_upkeT[:, :]),
            inc=SLg, inck=16)
    sc.emit("gpsimd", lambda h: h.dma_start(out=wdT2[:, 0:NDC, :], in_=d_wqdT[:, :, :]),
            inc=SLg, inck=16)
    sc.emit("gpsimd", lambda h: h.dma_start(out=wdT2[:, NDC:2 * NDC, :], in_=d_wkdT[:, :, :]),
            inc=SLg, inck=16)
    sc.emit("gpsimd", lambda h: h.memset(ones_b[:, :], 1.0))
    sc.emit("gpsimd", lambda h: h.memset(ones1f[:, :], 1.0))
    sc.emit("gpsimd", lambda h: h.memset(upfq[:, 64:96], 0.0))
    sc.emit("gpsimd", lambda h: h.memset(upfk[:, 64:96], 0.0), inc=SLm, inck=1)
    sc.run(nc)
    LD_ALL = sc.v(SL)
    LDG_ALL = sc.v(SLg)

    sc = Sched(sems, cnt)
    SM0, SV0, SD0 = S(), S(), S()
    with nc.psum_tensor("psR", [64, 2, 512], F32) as psR:
        for kk in range(2):
            sc.emit("tensor",
                    lambda h, k=kk: h.matmul(psR[:, k, 0:32], upeT2[:, k, :], P_sb[:, :],
                                             start=True, stop=True),
                    waits=[(SL, LD_ALL), (SLg, LDG_ALL), (SLm, 1)], inc=SM0)
        sc.emit("vector", lambda h: h.tensor_copy(upfq[0:64, 96:128], psR[:, 0, 0:32]),
                waits=[(SM0, 1)], inc=SV0)
        sc.emit("vector", lambda h: h.tensor_copy(upfk[0:64, 96:128], psR[:, 1, 0:32]),
                waits=[(SM0, 2)], inc=SV0)
        sc.emit("vector", lambda h: h.tensor_scalar_mul(upfq[0:64, :], upfq[0:64, :], 0.125),
                inc=SV0)
        sc.emit("gpsimd", lambda h: h.dma_start(out=upfq[64:128, :], in_=upfq[0:64, :]),
                waits=[(SV0, sc.v(SV0))], inc=SD0, inck=16)
        sc.emit("gpsimd", lambda h: h.dma_start(out=upfk[64:128, :], in_=upfk[0:64, :]),
                inc=SD0, inck=16)
        sc.run(nc)

    sc = Sched(sems, cnt)
    SM2, SV2 = S(), S()
    with nc.psum_tensor("psF", [128, 2, 512], F32) as psF:
        mmn = 0
        for kind in range(2):
            upf = upfq if kind == 0 else upfk
            eff = wq_eff if kind == 0 else wk_eff
            rot = wq_rot if kind == 0 else wk_rot
            for h0 in range(H):
                for dc0 in range(NDC):
                    w = [(SL, LD_ALL), (SLg, LDG_ALL), (SLm, 1), (SD0, 32)] if mmn == 0 else []
                    if mmn >= 2:
                        w.append((SV2, 2 * (mmn - 1)))
                    lhs = wdT2[64 * (h0 % 2):64 * (h0 % 2) + 64,
                               kind * NDC + h0 // 2, 128 * dc0:128 * dc0 + 128]
                    r0 = 64 * (h0 % 2)
                    sc.emit("tensor",
                            lambda h, l=lhs, u=upf, g=mmn % 2, r=r0: h.matmul(
                                psF[:, g, 0:128], l, u[r:r + 64, :], start=True, stop=True),
                            waits=w, inc=SM2)
                    mm_done = sc.v(SM2)
                    sc.emit("vector",
                            lambda h, e=eff, d=dc0, hh=h0, g=mmn % 2: h.tensor_copy(
                                e[:, d, 64 * hh:64 * hh + 64], psF[:, g, 0:64]),
                            waits=[(SM2, mm_done)], inc=SV2)
                    sc.emit("vector",
                            lambda h, r=rot, d=dc0, hh=h0, g=mmn % 2: h.tensor_copy(
                                r[:, d, 64 * hh:64 * hh + 64], psF[:, g, 64:128]),
                            inc=SV2)
                    mmn += 1
        sc.run(nc)
    st_fold.close()

    st1 = ExitStack()
    t1b = sb(st1, "t1b", [128, 512], F32)
    t2b = sb(st1, "t2b", [128, 512], F32)
    qstage = sb(st1, "qstage", [128, 2, 512], BF16)
    xTo = sb(st1, "xTo_sb", [128, NDC, TOWN], BF16)
    cospo = sb(st1, "cospo_sb", [128, TOWN], BF16)
    sinpo = sb(st1, "sinpo_sb", [128, TOWN], BF16)

    sc = Sched(sems, cnt)
    P1M, P1V, P1D, XO = S(), S(), S(), S()
    sc.emit("gpsimd", lambda h: h.dma_start(out=xTo[:, :, :], in_=d_xTo[:, :, :]),
            inc=XO, inck=16)
    sc.emit("gpsimd", lambda h: h.dma_start(out=cospo[:, :], in_=d_coso[:, :]),
            inc=XO, inck=16)
    sc.emit("gpsimd", lambda h: h.dma_start(out=sinpo[:, :], in_=d_sino[:, :]),
            inc=XO, inck=16)

    plan1 = [("k", p, tc) for p in range(NPAIR) for tc in range(NTC)]
    nq0 = len(plan1)
    plan1 += [("q", p, tc) for p in range(NPAIR) for tc in range(NOW)]

    with nc.psum_tensor("psP", [128, 2, 512], F32) as psP, \
         nc.psum_tensor("psO", [128, 2, 512], F32) as psO:
        for it, (kind, p, tc) in enumerate(plan1):
            isq = kind == "q"
            eff = wq_eff if isq else wk_eff
            rot = wq_rot if isq else wk_rot
            xs = xTo if isq else xT
            cc_ = cospo if isq else cosp
            ss_ = sinpo if isq else sinp
            dh = d_qh if isq else d_kh
            g = it % 2
            w = [(SL, LD_ALL), (SLg, LDG_ALL), (SLX, 32)] if it == 0 else []
            if it == nq0:
                w.append((XO, 48))
            if it >= 2:
                w.append((P1V, 3 * (it - 1)))

            def mms(h, e=eff, r=rot, x0=xs, p0=p, t0=tc, g0=g):
                for dc0 in range(NDC):
                    h.matmul(psP[:, g0, :], e[:, dc0, 128 * p0:128 * p0 + 128],
                             x0[:, dc0, 512 * t0:512 * t0 + 512],
                             start=(dc0 == 0), stop=(dc0 == NDC - 1))
                for dc0 in range(NDC):
                    ins = h.matmul(psO[:, g0, :], r[:, dc0, 128 * p0:128 * p0 + 128],
                                   x0[:, dc0, 512 * t0:512 * t0 + 512],
                                   start=(dc0 == 0), stop=(dc0 == NDC - 1))
                return ins
            sc.emit("tensor", mms, waits=w, inc=P1M)
            mmv = sc.v(P1M)
            wv = [(P1M, mmv)]
            if it >= 2:
                wv.append((P1D, 16 * (it - 1)))
            sc.emit("vector",
                    lambda h, c=cc_, t0=tc, g0=g: h.tensor_tensor(
                        t1b[:, :], psP[:, g0, :],
                        c[:, 512 * t0:512 * t0 + 512], ALU.mult),
                    waits=wv, inc=P1V)
            sc.emit("vector",
                    lambda h, s=ss_, t0=tc, g0=g: h.tensor_tensor(
                        t2b[:, :], psO[:, g0, :],
                        s[:, 512 * t0:512 * t0 + 512], ALU.mult),
                    inc=P1V)
            sc.emit("vector",
                    lambda h, g0=g: h.tensor_add(qstage[:, g0, :], t1b[:, :], t2b[:, :]),
                    inc=P1V)
            vv = sc.v(P1V)
            sc.emit("sync",
                    lambda h, d=dh, p0=p, t0=tc, g0=g: h.dma_start(
                        out=d[p0, :, 512 * t0:512 * t0 + 512], in_=qstage[:, g0, :]),
                    waits=[(P1V, vv)], inc=P1D, inck=16)
        sc.run(nc)
    P1D_ALL = cnt[P1D]
    st1.close()
    st_w.close()

    st2 = ExitStack()
    wvdT = sb(st2, "wvdT_sb", [128, NDC, C], BF16)
    wv_eff = sb(st2, "wv_eff", [128, NDC, C], BF16)
    upv_sb = sb(st2, "upv_sb", [128, 64], BF16)
    vstage = sb(st2, "vstage", [128, 2, H, 65], BF16)

    sc = Sched(sems, cnt)
    P2L, P2M, P2V, P2D, P2Lm = S(), S(), S(), S(), S()
    with nc.psum_tensor("psVF", [128, 2, 512], F32) as psVF, \
         nc.psum_tensor("psV", [128, 2, 1024], F32) as psV:
        sc.emit("gpsimd", lambda h: h.dma_start(out=wvdT[:, :, :], in_=d_wvdT[:, :, :]),
                inc=P2L, inck=16)
        sc.emit("gpsimd", lambda h: h.dma_start(out=upv_sb[0:64, :], in_=d_upv[:, :]),
                inc=P2L, inck=16)
        sc.emit("gpsimd", lambda h: h.dma_start(out=upv_sb[64:128, :], in_=upv_sb[0:64, :]),
                waits=[(P2L, 32)], inc=P2L, inck=16)
        sc.emit("gpsimd", lambda h: h.memset(vstage[:, :, :, :], 1.0), inc=P2Lm)
        ld = sc.v(P2L)
        mmn = 0
        for h0 in range(H):
            for dc0 in range(NDC):
                w = [(P2L, ld), (P2Lm, 1)] if mmn == 0 else []
                if mmn >= 2:
                    w.append((P2V, mmn - 1))
                lhs = wvdT[64 * (h0 % 2):64 * (h0 % 2) + 64, h0 // 2, 128 * dc0:128 * dc0 + 128]
                r0 = 64 * (h0 % 2)
                sc.emit("tensor",
                        lambda h, l=lhs, g=mmn % 2, r=r0: h.matmul(
                            psVF[:, g, 0:64], l, upv_sb[r:r + 64, :], start=True, stop=True),
                        waits=w, inc=P2M)
                sc.emit("vector",
                        lambda h, d=dc0, hh=h0, g=mmn % 2: h.tensor_copy(
                            wv_eff[:, d, 64 * hh:64 * hh + 64], psVF[:, g, 0:64]),
                        waits=[(P2M, sc.v(P2M))], inc=P2V)
                mmn += 1
        vf_done = sc.v(P2V)
        for kt in range(NKT):
            g = kt % 2
            w = [(P2V, vf_done)] if kt < 2 else [(P2V, vf_done + kt - 1)]

            def vmms(h, k0=kt, g0=g):
                for dc0 in range(NDC):
                    for hh in range(2):
                        ins = h.matmul(psV[:, g0, 512 * hh:512 * hh + 512],
                                       xT[:, dc0, 128 * k0:128 * k0 + 128],
                                       wv_eff[:, dc0, 512 * hh:512 * hh + 512],
                                       start=(dc0 == 0), stop=(dc0 == NDC - 1))
                return ins
            sc.emit("tensor", vmms, waits=w, inc=P2M)
            wv = [(P2M, sc.v(P2M))]
            if kt >= 2:
                wv.append((P2D, 16 * (kt - 1)))
            sc.emit("vector",
                    lambda h, g0=g: h.tensor_copy(
                        vstage[:, g0, :, 0:64],
                        psV[:, g0, :].rearrange("p (h d) -> p h d", h=H)),
                    waits=wv, inc=P2V)
            sc.emit("sync",
                    lambda h, k0=kt, g0=g: h.dma_start(out=d_vh[:, k0, :, :],
                                                       in_=vstage[:, g0, :, :]),
                    waits=[(P2V, sc.v(P2V))], inc=P2D, inck=16)
        sc.run(nc)
    P2D_ALL = cnt[P2D]
    st2.close()
    st_x.close()

    st3 = ExitStack()
    qbuf = sb(st3, "qbuf", [128, 2, TOWN], BF16)
    kbuf = sb(st3, "kbuf", [128, 2, T], BF16)
    vbuf = sb(st3, "vbuf", [128, 2, NKT, 2, 65], BF16)
    p2 = sb(st3, "p2", [128, 2, 1024], BF16)
    ystg = sb(st3, "ystg", [128, 2, 1024], BF16)
    lstg = sb(st3, "lstg", [128, 2, 1024], F32)

    sc = Sched(sems, cnt)
    A_LD, A_ST, A_EXP, A_MSK, A_AV, A_EV, A_YD = S(), S(), S(), S(), S(), S(), S()

    sched3 = []
    for p in range(NPAIR):
        for qc in range(NOW):
            for j in range(8 * qc + 8):
                r = j - 8 * qc
                c0 = 64 * r if r > 0 else 0
                sched3.append((p, qc, j, c0, j == 8 * qc + 7, r if r >= 0 else None))
    NJP = len(sched3) // NPAIR

    for p in range(NPAIR):
        w = []
        if p == 0:
            w = [(P1D, P1D_ALL), (P2D, P2D_ALL)]
        if p >= 2:
            w = [(A_AV, NJP * (p - 1))]
        g = p % 2
        sc.emit("sync", lambda h, p0=p, g0=g: h.dma_start(out=qbuf[:, g0, :], in_=d_qh[p0, :, :]),
                waits=w, inc=A_LD, inck=16)
        sc.emit("sync", lambda h, p0=p, g0=g: h.dma_start(out=kbuf[:, g0, :], in_=d_kh[p0, :, :]),
                inc=A_LD, inck=16)
        sc.emit("sync", lambda h, p0=p, g0=g: h.dma_start(
            out=vbuf[:, g0, :, :, :],
            in_=d_vh[:, :, 2 * p0:2 * p0 + 2, :]),
            inc=A_LD, inck=16)

    with nc.psum_tensor("psST", [128, 2, 1024], F32) as psST, \
         nc.psum_tensor("psOT", [128, 2, 1024], F32) as psOT:
        nj = 0
        kidx = 0
        pending_av = None
        for (p, qc, j, c0, last, r) in sched3:
            g = p % 2
            jg = nj % 2
            w = [(A_LD, 48 * (p + 1))]
            if nj >= 2:
                w.append((A_EXP, nj - 1))

            def stm(h, p0=p, q0=qc, j0=j, cc=c0, g0=g, jg0=jg):
                for hh in range(2):
                    ins = h.matmul(
                        psST[:, jg0, 512 * hh + cc:512 * hh + 512],
                        kbuf[64 * hh:64 * hh + 64, g0, 128 * j0:128 * j0 + 128],
                        qbuf[64 * hh:64 * hh + 64, g0, 512 * q0 + cc:512 * q0 + 512],
                        start=True, stop=True)
                return ins
            sc.emit("tensor", stm, waits=w, inc=A_ST)
            st_done = sc.v(A_ST)

            if pending_av is not None:
                sc.emit("tensor", pending_av[0], waits=pending_av[1], inc=A_AV)
                pending_av = None

            w = [(A_ST, st_done)]
            if nj >= 2:
                w.append((A_AV, nj - 1))

            def expf(h, cc=c0, jg0=jg):
                iv = psST[:, jg0, :].rearrange("p (h q) -> p h q", h=2)[:, :, cc:512]
                ov = p2[:, jg0, :].rearrange("p (h q) -> p h q", h=2)[:, :, cc:512]
                return h.activation(ov, iv, AF.Exp)
            sc.emit("scalar", expf, waits=w, inc=A_EXP)
            exp_done = sc.v(A_EXP)

            if r is not None:
                def mskf0(h, cc=c0, jg0=jg, r0=r):
                    ov = p2[:, jg0, :].rearrange("p (h q) -> p h q", h=2)[:, 0, cc:512]
                    return h.tensor_tensor(ov, ov, msk_sb[:, r0, cc:512], ALU.mult)

                def mskf1(h, cc=c0, jg0=jg, r0=r):
                    ov = p2[:, jg0, :].rearrange("p (h q) -> p h q", h=2)[:, 1, cc:512]
                    return h.tensor_tensor(ov, ov, msk_sb[:, r0, cc:512], ALU.mult)
                sc.emit("vector", mskf0, waits=[(A_EXP, exp_done)])
                sc.emit("vector", mskf1, inc=A_MSK)
            else:
                sc.emit("vector", lambda h: h.engine_nop(),
                        waits=[(A_EXP, exp_done)], inc=A_MSK)
            msk_done = sc.v(A_MSK)

            def avm(h, p0=p, j0=j, cc=c0, last0=last, first=(j == 0), g0=g, jg0=jg,
                    ot=kidx % 2):
                for hh in range(2):
                    ins = h.matmul(
                        psOT[0:65, ot, 512 * hh + cc:512 * hh + 512],
                        vbuf[:, g0, j0, hh, :],
                        p2[:, jg0, 512 * hh + cc:512 * hh + 512],
                        start=first, stop=last0)
                return ins
            avw = [(A_MSK, msk_done)]
            if j == 0 and kidx >= 2:
                avw.append((A_EV, 3 * (kidx - 1)))
            if last:
                sc.emit("tensor", avm, waits=avw, inc=A_AV)
                av_done = sc.v(A_AV)
                row = 2 * kidx
                eg = kidx % 2
                w = [(A_AV, av_done)]
                if kidx >= 2:
                    w.append((A_YD, 48 * (kidx - 1)))
                sc.emit("vector",
                        lambda h, e=eg, ot=kidx % 2: h.tensor_copy(
                            lstg[64:65, e, :], psOT[64:65, ot, :]),
                        waits=w, inc=A_EV)
                sc.emit("vector",
                        lambda h, e=eg, ot=kidx % 2: h.tensor_copy(
                            ystg[0:64, e, 0:512], psOT[0:64, ot, 0:512]),
                        inc=A_EV)
                sc.emit("vector",
                        lambda h, e=eg, ot=kidx % 2: h.tensor_copy(
                            ystg[0:64, e, 512:1024], psOT[0:64, ot, 512:1024]),
                        inc=A_EV)
                ev_here = sc.v(A_EV)
                sc.emit("gpsimd",
                        lambda h, p0=p, q0=qc, e=eg: h.dma_start(
                            out=d_yh[p0, 0:64, 512 * q0:512 * q0 + 512], in_=ystg[0:64, e, 0:512]),
                        waits=[(A_EV, ev_here)], inc=A_YD, inck=16)
                sc.emit("gpsimd",
                        lambda h, p0=p, q0=qc, e=eg: h.dma_start(
                            out=d_yh[p0, 64:128, 512 * q0:512 * q0 + 512], in_=ystg[0:64, e, 512:1024]),
                        inc=A_YD, inck=16)
                sc.emit("gpsimd",
                        lambda h, r0=row, e=eg: h.dma_start(
                            out=l_all[r0:r0 + 2, :],
                            in_=lstg[64:65, e, :].rearrange("p (a b) -> p a b", a=2)),
                        inc=A_YD, inck=16)
                kidx += 1
            else:
                pending_av = (avm, avw)
            nj += 1
        sc.emit("vector", lambda h: h.reciprocal(rl_all[:, :], l_all[:, :]),
                waits=[(A_YD, cnt[A_YD])], inc=A_EV)
        sc.run(nc)
    A_YD_ALL = cnt[A_YD]
    st3.close()
    st_msk.close()

    st_acc = ExitStack()
    acc2 = sb(st_acc, "acc2", [128, 8 * TOWN], F32)
    w_own = sb(st_acc, "w_own", [8, TOWN], F32)
    st_sc = ExitStack()
    scoresT = sb(st_sc, "scoresT", [8, TOWN], F32)
    st4 = ExitStack()
    wc_sb = sb(st4, "wc_sb", [128, NDC, C], BF16)
    rw_sb = sb(st4, "rw_sb", [128, NDC, 8], F32)
    ybuf = sb(st4, "ybuf", [128, 2, 8, 512], BF16)
    ystr = sb(st4, "ystr", [128, 2, 8, 512], BF16)
    xfb = sb(st4, "xfb", [128, 2, 512], F32)
    stag2 = sb(st4, "stag2", [2, 2, 512], F32)
    rbuf = sb(st4, "rbuf", [128, NDC, 512], F32)
    smls = sb(st4, "smls", [1, 3072], F32)
    xpbf = sb(st4, "xpbf", [128, NDC, 512], BF16)
    xpf = sb(st4, "xpf", [128, NDC, 512], F32)

    sc = Sched(sems, cnt)
    L4, M4, V4, S4, D4, G4, L4b, D4s = S(), S(), S(), S(), S(), S(), S(), S()
    with nc.psum_tensor("psRB", [128, 2, 512], F32) as psRB, \
         nc.psum_tensor("psAT", [128, 2, 512], F32) as psAT, \
         nc.psum_tensor("psMS", [1, 1024], F32) as psMS, \
         nc.psum_tensor("psSC", [8, 512], F32) as psSC:
        sc.emit("gpsimd", lambda h: h.dma_start(out=wc_sb[:, :, :], in_=d_wc[:, :, :]),
                inc=L4, inck=16)
        sc.emit("sync", lambda h: h.dma_start(out=rw_sb[:, :, :], in_=d_rwT[:, :, :]),
                inc=D4, inck=16)
        sc.emit("vector", lambda h: h.tensor_scalar_max(t_sb[:, :], t_sb[:, :], 0.1),
                waits=[(SL, LD_ALL)], inc=V4)
        sc.emit("vector", lambda h: h.reciprocal(t_sb[:, :], t_sb[:, :]), inc=V4)
        sc.emit("gpsimd", lambda h: h.partition_broadcast(rtemp[:, :], t_sb[:, :]),
                waits=[(V4, 2)], inc=L4b)
        ld4 = sc.v(L4)

        vnorm_done, resid_done, resid_v4, rlb_mm, statm_done = {}, {}, {}, {}, {}
        dbl_prev = [0]
        g4_prev = [0]
        for tc in range(NOW):
            g = tc % 2
            w = [(A_YD, A_YD_ALL)] if tc == 0 else []
            if tc >= 2:
                w.append((V4, vnorm_done[tc - 2]))
            sc.emit("sync", lambda h, t0=tc, g0=g: h.dma_start(
                out=ystr[:, g0, :, :],
                in_=d_yh[:, :, 512 * t0:512 * t0 + 512].transpose([1, 0, 2])),
                waits=w, inc=D4, inck=16)
            ld_here = sc.v(D4)

            for p in range(NPAIR):
                k0 = 2 * (NOW * p + tc)
                wg_ = []
                if NPAIR * tc + p >= 2:
                    wg_ = [(M4, rlb_mm.get(NPAIR * tc + p - 2, 0))]
                sc.emit("scalar", lambda h, k=k0, g0=p % 2: h.dma_start(
                    out=stag2[:, g0, :], in_=rl_all[k:k + 2, :]),
                    waits=wg_, inc=D4s, inck=16)
                gat_here = sc.v(D4s)
                w = [(D4s, gat_here), (L4, ld4)]
                if p >= 2:
                    w.append((V4, sc.v(V4) - 1))
                elif tc >= 1:
                    w.append((V4, vnorm_done[tc - 1]))
                sc.emit("tensor", lambda h, g0=p % 2: h.matmul(
                    psRB[:, g0, :], sel2[:, :], stag2[:, g0, :],
                    start=True, stop=True),
                    waits=w, inc=M4)
                rlb_mm[NPAIR * tc + p] = sc.v(M4)
                sc.emit("vector", lambda h, p0=p, g0=p % 2, gg=g: h.tensor_tensor(
                    ybuf[:, gg, p0, :], ystr[:, gg, p0, :], psRB[:, g0, :], ALU.mult),
                    waits=[(M4, sc.v(M4)), (D4, ld_here)], inc=V4)
            vnorm_done[tc] = sc.v(V4)

            for dout in range(NDC):
                gi = NDC * tc + dout
                w = [(V4, resid_v4[gi - 2])] if gi >= 2 else []
                sc.emit("sync", lambda h, t0=tc, d0=dout, g2=gi % 2: h.dma_start(
                    out=xfb[:, g2, :], in_=d_xTo[:, d0, 512 * t0:512 * t0 + 512]),
                    waits=w, inc=D4, inck=16)
                xf_here = sc.v(D4)
                w = [(V4, vnorm_done[tc])]
                if dout >= 2:
                    w.append((V4, sc.v(V4)))

                def wcm(h, d0=dout, g0=dout % 2, gg=g):
                    for p in range(NPAIR):
                        ins = h.matmul(psAT[:, g0, :],
                                       wc_sb[:, p, 128 * d0:128 * d0 + 128],
                                       ybuf[:, gg, p, :], start=(p == 0), stop=(p == NPAIR - 1))
                    return ins
                if dout < 2 and tc >= 1:
                    w.append((V4, dbl_prev[0]))
                sc.emit("tensor", wcm, waits=w, inc=M4)
                wrs = [(M4, sc.v(M4)), (D4, xf_here)]
                if dout == 0 and tc >= 1:
                    wrs.append((G4, g4_prev[0]))
                sc.emit("vector", lambda h, d0=dout, g0=dout % 2, g2=gi % 2: h.tensor_tensor(
                    rbuf[:, d0, :], psAT[:, g0, :], xfb[:, g2, :], ALU.add),
                    waits=wrs, inc=V4)
                resid_v4[gi] = sc.v(V4)
            resid_done[tc] = sc.v(V4)

            sc.emit("scalar", lambda h: h.activation(xpbf[:, :, :], rbuf[:, :, :], AF.Square),
                    waits=[(V4, resid_done[tc])], inc=S4)
            sc.emit("vector", lambda h, gg=g: h.tensor_copy(ybuf[:, gg, :, :], rbuf[:, :, :]),
                    waits=[(V4, vnorm_done[tc])], inc=V4)
            rdy = (sc.v(S4), sc.v(V4))

            def statm(h, gg=g):
                for d0 in range(NDC):
                    h.matmul(psMS[0:1, 0:512], ones_b[:, :], ybuf[:, gg, d0, :],
                             start=(d0 == 0), stop=(d0 == NDC - 1))
                for d0 in range(NDC):
                    ins = h.matmul(psMS[0:1, 512:1024], ones_b[:, :], xpbf[:, d0, :],
                                   start=(d0 == 0), stop=(d0 == NDC - 1))
                return ins
            sc.emit("tensor", statm, waits=[(S4, rdy[0]), (V4, rdy[1])], inc=M4)
            stat_done = sc.v(M4)
            statm_done[tc] = stat_done

            def small1(h):
                h.tensor_scalar_mul(smls[0:1, 0:512], psMS[0:1, 0:512], 1.0 / C)
                h.tensor_scalar_mul(smls[0:1, 512:1024], psMS[0:1, 512:1024], 1.0 / C)
                h.tensor_tensor(smls[0:1, 1024:1536], smls[0:1, 0:512], smls[0:1, 0:512], ALU.mult)
                h.tensor_tensor(smls[0:1, 1536:2048], smls[0:1, 512:1024], smls[0:1, 1024:1536], ALU.subtract)
                return h.tensor_scalar_add(smls[0:1, 1536:2048], smls[0:1, 1536:2048], 1e-5)
            sc.emit("vector", small1, waits=[(M4, stat_done)], inc=V4)
            sc.emit("scalar", lambda h: h.activation(smls[0:1, 2048:2560], smls[0:1, 1536:2048],
                                                     AF.Ln),
                    waits=[(V4, sc.v(V4))], inc=S4)
            sc.emit("scalar", lambda h: h.activation(smls[0:1, 2560:3072], smls[0:1, 2048:2560],
                                                     AF.Exp, scale=-0.5),
                    inc=S4)
            sc.emit("vector", lambda h: h.tensor_tensor(
                smls[0:1, 1024:1536], smls[0:1, 0:512], smls[0:1, 2560:3072], ALU.mult),
                waits=[(S4, sc.v(S4))], inc=V4)
            ms_done = sc.v(V4)

            def bcm(h):
                h.matmul(psAT[:, 0, :], ones1f[:, :], smls[0:1, 2560:3072], start=True, stop=True)
                return h.matmul(psAT[:, 1, :], ones1f[:, :], smls[0:1, 1024:1536], start=True, stop=True)
            sc.emit("tensor", bcm, waits=[(V4, ms_done)], inc=M4)
            bc_done = sc.v(M4)

            for dout in range(NDC):
                sc.emit("vector", lambda h, d0=dout: h.tensor_tensor(
                    xpf[:, d0, :], rbuf[:, d0, :], psAT[:, 0, :], ALU.mult),
                    waits=[(M4, bc_done)], inc=V4)
                sc.emit("vector", lambda h, d0=dout: h.tensor_tensor(
                    xpf[:, d0, :], xpf[:, d0, :], psAT[:, 1, :], ALU.subtract),
                    inc=V4)
            xpf_done = sc.v(V4)
            sc.emit("vector", lambda h: h.tensor_scalar_mul(
                rbuf[:, :, :], xpf[:, :, :], 2.0), waits=[], inc=V4)
            dbl_done = sc.v(V4)
            dbl_prev[0] = dbl_done

            def rtm(h):
                for d0 in range(NDC):
                    ins = h.matmul(psSC[:, :], rw_sb[:, d0, :], xpf[:, d0, :],
                                   start=(d0 == 0), stop=(d0 == NDC - 1))
                return ins
            sc.emit("tensor", rtm, waits=[(V4, xpf_done)], inc=M4)
            sc.emit("vector", lambda h, t0=tc: h.tensor_scalar(
                scoresT[:, 512 * t0:512 * t0 + 512], psSC[:, :],
                rb_sb[:, 0:1], rtemp[:, 0:1], ALU.add, ALU.mult),
                waits=[(M4, sc.v(M4)), (L4b, 1)], inc=V4)

            for dout in range(NDC):
                sc.emit("gpsimd", lambda h, t0=tc, d0=dout: h.dma_start(
                    out=acc2[:, TOWN * d0 + 512 * t0:TOWN * d0 + 512 * t0 + 512],
                    in_=rbuf[:, d0, :]),
                    waits=[(V4, dbl_done)] if dout == 0 else [], inc=G4, inck=16)
            g4_prev[0] = sc.v(G4)
        sc.run(nc)
    G4_ALL = cnt[G4]
    V4_ALL = cnt[V4]
    st4.close()

    st5 = ExitStack()
    scAll = sb(st5, "scAll", [8, 2 * TOWN], F32)
    sc_tm = sb(st5, "sc_tm", [128, 256], F32)
    mx32 = sb(st5, "mx32", [128, 32], F32)
    ssum = sb(st5, "ssum", [128, 32], F32)
    ebuf = sb(st5, "ebuf", [128, 256], F32)
    ebf = sb(st5, "ebf", [128, 256], BF16)
    tsb5 = sb(st5, "tsb5", [1, 256], F32)
    lnt8 = sb(st5, "lnt8", [1, 8], F32)
    lnt_b = sb(st5, "lnt_b", [128, 8], F32)
    gbuf = sb(st5, "gbuf", [128, 256], F32)
    eqb = sb(st5, "eqb", [128, 256], F32)
    g2b = sb(st5, "g2b", [128, 256], F32)
    v1b = sb(st5, "v1b", [128, 32], F32)
    v2b = sb(st5, "v2b", [128, 32], F32)
    w_tm = sb(st5, "w_tm", [128, 256], F32)
    w_em = sb(st5, "w_em", [8, T], F32)

    sc = Sched(sems, cnt)
    M5, V5, S5, G5, G5d, CCD, CCS, CCL = S(), S(), S(), S(), S(), S(), S(), S()
    v3 = lambda x: x[:, :].rearrange("p (j e) -> p j e", e=8)

    with nc.psum_tensor("psT5", [128, 256], F32) as psT5, \
         nc.psum_tensor("psS5", [1, 256], F32) as psS5, \
         nc.psum_tensor("psW5", [8, 512], F32) as psW5:
        sc.emit("gpsimd", lambda h: h.dma_start(out=d_scin[:, :], in_=scoresT[:, :]),
                waits=[(V4, V4_ALL)], inc=CCD, inck=16)
        sc.emit("gpsimd", lambda h: h.collective_compute(
            "AllGather", ALU.bypass, replica_groups=RG,
            ins=[d_scin.opt()], outs=[d_scout.opt()]),
            waits=[(CCD, 16)], inc=CCS, inck=1)
        sc.emit("gpsimd", lambda h: h.dma_start(
            out=scAll[:, :].rearrange("p (r t) -> p r t", r=2),
            in_=d_scout[:, :, :].transpose([1, 0, 2])),
            waits=[(CCS, 1)], inc=CCL, inck=16)

        def trs(h):
            for j in range(NKT):
                ins = h.transpose(psT5[:, 8 * j:8 * j + 8],
                                  scAll[0:8, 128 * j:128 * j + 128],
                                  ident_f[0:8, 0:8])
            return ins
        sc.emit("tensor", trs, waits=[(CCL, 16)], inc=M5)
        sc.emit("vector", lambda h: h.tensor_copy(sc_tm[:, :], psT5[:, :]),
                waits=[(M5, sc.v(M5))], inc=V5)

        tsb_done = {}
        for it in range(3):
            sc.emit("vector", lambda h: h.reduce_max(mx32[:, :], v3(sc_tm), axis=AX.X),
                    inc=V5)
            sc.emit("vector", lambda h: h.tensor_tensor(
                v3(sc_tm), v3(sc_tm), mx32[:, :].broadcast_to([128, 32, 8]), ALU.subtract),
                inc=V5)
            sc.emit("scalar", lambda h: h.activation(ebuf[:, :], sc_tm[:, :], AF.Exp),
                    waits=[(V5, sc.v(V5))], inc=S5)
            sc.emit("vector", lambda h: h.reduce_sum(ssum[:, :], v3(ebuf), axis=AX.X),
                    waits=[(S5, sc.v(S5))], inc=V5)
            sc.emit("scalar", lambda h: h.activation(ssum[:, :], ssum[:, :], AF.Ln),
                    waits=[(V5, sc.v(V5))], inc=S5)
            sc.emit("vector", lambda h: h.tensor_tensor(
                v3(sc_tm), v3(sc_tm), ssum[:, :].broadcast_to([128, 32, 8]), ALU.subtract),
                waits=[(S5, sc.v(S5))], inc=V5)
            sc.emit("scalar", lambda h: h.activation(ebf[:, :], sc_tm[:, :], AF.Exp),
                    waits=[(V5, sc.v(V5))], inc=S5)
            sc.emit("tensor", lambda h: h.matmul(psS5[:, :], ones_b[:, :], ebf[:, :],
                                                 start=True, stop=True),
                    waits=[(S5, sc.v(S5))] + ([(V5, tsb_done[it - 1])] if it > 0 else []),
                    inc=M5)
            sc.emit("vector", lambda h: h.tensor_copy(tsb5[:, :], psS5[:, :]),
                    waits=[(M5, sc.v(M5))], inc=V5)
            tsb_done[it] = sc.v(V5)
            sc.emit("vector", lambda h: h.reduce_sum(
                lnt8[:, :], tsb5[:, :].rearrange("p (j e) -> p e j", e=8), axis=AX.X),
                inc=V5)
            sc.emit("scalar", lambda h: h.activation(lnt8[:, :], lnt8[:, :], AF.Ln),
                    waits=[(V5, sc.v(V5))], inc=S5)
            sc.emit("gpsimd", lambda h: h.partition_broadcast(lnt_b[:, :], lnt8[:, :]),
                    waits=[(S5, sc.v(S5))], inc=G5)
            sc.emit("vector", lambda h: h.tensor_tensor(
                v3(sc_tm), v3(sc_tm),
                lnt_b[:, :].rearrange("p (j e) -> p j e", j=1).broadcast_to([128, 32, 8]),
                ALU.subtract),
                waits=[(G5, sc.v(G5))], inc=V5)

        sc.emit("scalar", lambda h: h.activation(gbuf[:, :], sc_tm[:, :], AF.Exp),
                waits=[(V5, sc.v(V5))], inc=S5)
        sc.emit("vector", lambda h: h.reduce_sum(ssum[:, :], v3(gbuf), axis=AX.X),
                waits=[(S5, sc.v(S5))], inc=V5)
        sc.emit("vector", lambda h: h.tensor_scalar_add(ssum[:, :], ssum[:, :], 1e-8), inc=V5)
        sc.emit("vector", lambda h: h.reciprocal(ssum[:, :], ssum[:, :]), inc=V5)
        rec_done = sc.v(V5)
        sc.emit("vector", lambda h: h.tensor_tensor(
            v3(gbuf), v3(gbuf), ssum[:, :].broadcast_to([128, 32, 8]), ALU.mult),
            waits=[(V5, rec_done)], inc=V5)
        sc.emit("vector", lambda h: h.reduce_max(v1b[:, :], v3(gbuf), axis=AX.X), inc=V5)
        sc.emit("vector", lambda h: h.tensor_tensor(
            v3(eqb), v3(gbuf), v1b[:, :].broadcast_to([128, 32, 8]), ALU.is_equal), inc=V5)
        sc.emit("vector", lambda h: h.scalar_tensor_tensor(
            g2b[:, :], eqb[:, :], -1e30, gbuf[:, :], ALU.mult, ALU.add), inc=V5)
        sc.emit("vector", lambda h: h.reduce_max(v2b[:, :], v3(g2b), axis=AX.X), inc=V5)
        sc.emit("vector", lambda h: h.tensor_tensor(v1b[:, :], v1b[:, :], v2b[:, :], ALU.add),
                inc=V5)
        sc.emit("vector", lambda h: h.tensor_scalar_add(v1b[:, :], v1b[:, :], 1e-8), inc=V5)
        sc.emit("vector", lambda h: h.reciprocal(v1b[:, :], v1b[:, :]), inc=V5)
        rec2_done = sc.v(V5)
        sc.emit("vector", lambda h: h.tensor_tensor(
            v3(eqb), v3(gbuf), v2b[:, :].broadcast_to([128, 32, 8]), ALU.is_ge),
            waits=[(V5, rec2_done)], inc=V5)
        sc.emit("vector", lambda h: h.tensor_tensor(
            w_tm[:, :], gbuf[:, :], eqb[:, :], ALU.mult), inc=V5)
        sc.emit("vector", lambda h: h.tensor_tensor(
            v3(w_tm), v3(w_tm), v1b[:, :].broadcast_to([128, 32, 8]), ALU.mult), inc=V5)
        wtm_done = sc.v(V5)

        for c in range(8):
            w = [(V5, wtm_done if c == 0 else sc.v(V5))]

            def trb(h, c0=c):
                for jj in range(4):
                    j = 4 * c0 + jj
                    ins = h.transpose(psW5[:, 128 * jj:128 * jj + 128],
                                      w_tm[:, 8 * j:8 * j + 8], ident_f[:, :])
                return ins
            sc.emit("tensor", trb, waits=w, inc=M5)
            sc.emit("vector", lambda h, c0=c: h.tensor_copy(
                w_em[:, 512 * c0:512 * c0 + 512], psW5[:, :]),
                waits=[(M5, sc.v(M5))], inc=V5)

        def wslice(h):
            off = h.value_load(soff_sb[0:1, 0:1])
            return h.dma_start(out=w_own[:, :], in_=w_em[:, bass.ds(off, TOWN)])
        sc.emit("gpsimd", wslice, waits=[(V5, sc.v(V5))], inc=G5d, inck=16)
        if _DBG_SCRATCH:
            sc.emit("sync", lambda h: h.dma_start(out=d_dbg1[:, :], in_=l_all[:, :]),
                    waits=[(G5d, sc.v(G5d))], inc=G5d, inck=16)
            sc.emit("sync", lambda h: h.dma_start(out=d_dbg2[:, :], in_=scAll[:, :]),
                    inc=G5d, inck=16)
            sc.emit("sync", lambda h: h.dma_start(out=d_dbg3[:, :], in_=w_own[:, :]),
                    inc=G5d, inck=16)
            sc.emit("sync", lambda h: None, waits=[(G5d, sc.v(G5d))])
        sc.run(nc)
    G5_ALL = cnt[G5d]
    st5.close()
    st_sc.close()

    st6 = ExitStack()
    xpo = sb(st6, "xpo", [128, NDC, TOWN], BF16)
    gpw = sb(st6, "gpw", [128, 8, TOWN], BF16)
    grelu = sb(st6, "grelu", [128, TOWN], BF16)
    wgp_f = sb(st6, "wgp_f", [128, 2, NDC, 128], F32)
    wgp_r = sb(st6, "wgp_r", [128, 2, 2, NDC, 128], BF16)
    wo_f = sb(st6, "wo_f", [128, 2, 8, 128], F32)
    wo_r = sb(st6, "wo_r", [128, 2, 8, 128], BF16)
    wbc = sb(st6, "wbc", [128, TOWN], F32)
    wstag = sb(st6, "wstag", [1, TOWN], F32)
    wbcb = sb(st6, "wbcb", [128, 2, TOWN], BF16)

    sc = Sched(sems, cnt)
    L6, M6, V6, S6, G6, G6d, C6 = S(), S(), S(), S(), S(), S(), S()

    with nc.psum_tensor("psG", [128, 1024], F32) as psG, \
         nc.psum_tensor("psP6", [128, 1024], F32) as psP6, \
         nc.psum_tensor("psO6", [128, 2, 1024], F32) as psO6:
        if _DBG_SCRATCH:
            DBG9 = S()
            sc.emit("sync", lambda h: h.dma_start(out=d_dbg9[:, :], in_=acc2[:, :]),
                    waits=[(G4, G4_ALL)], inc=DBG9, inck=16)
            sc.emit("sync", lambda h: None, waits=[(DBG9, 16)])
        sc.emit("vector", lambda h: h.tensor_scalar_mul(
            xpo[:, :, :], acc2[:, :].rearrange("p (d t) -> p d t", d=NDC), 0.5),
            waits=[(G4, G4_ALL)], inc=V6)
        xpo_done = sc.v(V6)

        om_last = [0]
        acc_stt = {}
        cast_cnt = {}
        kcast = {}
        wocast = {}
        nfc = 0
        nth = 0
        ndo = 0
        nlr = 0
        v6_th0 = [0]
        wgld, wold, wbcd, gpend, grelu_rd = {}, {}, {}, {}, {}
        relu_cnt = {}
        stt_cnt = {}
        for e0 in range(E):
            w = [(G5d, G5_ALL)]
            if e0 >= 2:
                w.append((V6, gpend[e0 - 2]))
            if e0 >= 1:
                w.append((V6, wbcd[e0 - 1]))
            sc.emit("gpsimd", lambda h, e=e0: h.dma_start(
                out=wstag[:, :], in_=w_own[e:e + 1, :]), waits=w, inc=G6d, inck=16)
            sc.emit("gpsimd", lambda h: h.partition_broadcast(
                wbc[:, :], wstag[:, :]), waits=[(G6d, sc.v(G6d))], inc=G6)
            sc.emit("vector", lambda h, e=e0: h.tensor_copy(wbcb[:, e % 2, :], wbc[:, :]),
                    waits=[(G6, sc.v(G6))], inc=V6)
            wbcd[e0] = sc.v(V6)

            for half in range(2):
                for f8 in range(8):
                    fc = 8 * half + f8
                    for kd in range(2):
                        nk = 2 * nfc + kd
                        w = [(C6, kcast.get(nk - 2, 0))] if nk >= 2 else []
                        sc.emit("sync", lambda h, e=e0, f=fc, k=kd, g=nk % 2: h.dma_start(
                            out=wgp_f[:, g, :, :], in_=d_wgp[e, f, :, k, :, :]),
                            waits=w, inc=L6, inck=16)
                        wc_ = [(L6, sc.v(L6))]
                        if nfc >= 2:
                            wc_.append((M6, wgld[nfc - 2]))
                        sc.emit("scalar", lambda h, k=kd, g=nk % 2, gr=nfc % 2: h.copy(
                            wgp_r[:, gr, k, :, :], wgp_f[:, g, :, :]), waits=wc_, inc=C6)
                        kcast[nk] = sc.v(C6)
                    cast_cnt[nfc] = sc.v(C6)

                    for th in range(2):
                        w = [(C6, cast_cnt[nfc]), (V6, xpo_done)]
                        if nth >= 1:
                            w.append((S6, relu_cnt[nth - 1]))

                        def gm(h, g=nfc % 2, t1=th):
                            for d0 in range(NDC):
                                for t0 in range(2):
                                    ins = h.matmul(
                                        psG[:, 512 * t0:512 * t0 + 512],
                                        wgp_r[:, g, 0, d0, :],
                                        xpo[:, d0, 1024 * t1 + 512 * t0:1024 * t1 + 512 * t0 + 512],
                                        start=(d0 == 0), stop=(d0 == NDC - 1))
                            return ins
                        sc.emit("tensor", gm, waits=w, inc=M6)
                        wr = [(M6, sc.v(M6))]
                        if nth >= 1:
                            wr.append((V6, grelu_rd[nth - 1]))
                        sc.emit("scalar", lambda h, t1=th: h.activation(
                            grelu[:, 1024 * t1:1024 * t1 + 1024], psG[:, :], AF.Relu),
                            waits=wr, inc=S6)
                        relu_cnt[nth] = sc.v(S6)

                        w = []
                        if nth >= 1:
                            w.append((V6, stt_cnt[nth - 1]))

                        def pm(h, g=nfc % 2, t1=th):
                            for d0 in range(NDC):
                                for t0 in range(2):
                                    ins = h.matmul(
                                        psP6[:, 512 * t0:512 * t0 + 512],
                                        wgp_r[:, g, 1, d0, :],
                                        xpo[:, d0, 1024 * t1 + 512 * t0:1024 * t1 + 512 * t0 + 512],
                                        start=(d0 == 0), stop=(d0 == NDC - 1))
                            return ins
                        sc.emit("tensor", pm, waits=w, inc=M6)
                        pm_done = sc.v(M6)
                        if th == 1:
                            wgld[nfc] = pm_done

                        w = [(M6, pm_done), (S6, relu_cnt[nth]), (V6, wbcd[e0])]
                        if half == 1 or e0 > 0:
                            w.append((M6, om_last[0]))
                        sc.emit("vector", lambda h, f=f8, t1=th: h.scalar_tensor_tensor(
                            gpw[:, f, 1024 * t1:1024 * t1 + 1024], psP6[:, :], 1.0,
                            grelu[:, 1024 * t1:1024 * t1 + 1024], ALU.mult, ALU.mult),
                            waits=w, inc=V6)
                        sc.emit("vector", lambda h, f=f8, e=e0, t1=th: h.tensor_tensor(
                            gpw[:, f, 1024 * t1:1024 * t1 + 1024],
                            gpw[:, f, 1024 * t1:1024 * t1 + 1024],
                            wbcb[:, e % 2, 1024 * t1:1024 * t1 + 1024], ALU.mult),
                            inc=V6)
                        stt_cnt[nth] = sc.v(V6)
                        grelu_rd[nth] = sc.v(V6)
                        nth += 1
                    nfc += 1
                gpw_done = sc.v(V6)

                last_eh = (e0 == E - 1 and half == 1)
                if last_eh:
                    units = [(d0, t1, True) for t1 in range(2) for d0 in range(NDC)]
                else:
                    units = [(d0, t1, t1 == 0) for d0 in range(NDC) for t1 in range(2)]
                cur_load = 0
                for (dout, th, newload) in units:
                    if newload:
                        w = [(C6, wocast.get(nlr - 2, 0))] if nlr >= 2 else []
                        sc.emit("sync", lambda h, e=e0, hf=half, d0=dout, g=nlr % 2: h.dma_start(
                            out=wo_f[:, g, :, :], in_=d_wo[e, hf, d0, :, :, :]),
                            waits=w, inc=L6, inck=16)
                        wc_ = [(L6, sc.v(L6))]
                        if nlr >= 2:
                            wc_.append((M6, wold.get(nlr - 2, 0)))
                        sc.emit("scalar", lambda h, g=nlr % 2: h.copy(
                            wo_r[:, g, :, :], wo_f[:, g, :, :]), waits=wc_, inc=C6)
                        wocast[nlr] = sc.v(C6)
                        cur_load = nlr
                        nlr += 1
                    w = [(C6, wocast[cur_load]), (V6, gpw_done)]
                    if ndo >= 2:
                        w.append((V6, acc_stt[ndo - 2]))

                    def om(h, g=cur_load % 2, og=ndo % 2, t1=th):
                        for f in range(8):
                            for t0 in range(2):
                                ins = h.matmul(
                                    psO6[:, og, 512 * t0:512 * t0 + 512],
                                    wo_r[:, g, f, :],
                                    gpw[:, f, 1024 * t1 + 512 * t0:1024 * t1 + 512 * t0 + 512],
                                    start=(f == 0), stop=(f == 7))
                        return ins
                    sc.emit("tensor", om, waits=w, inc=M6)
                    om_last[0] = sc.v(M6)
                    if last_eh or th == 1:
                        wold[cur_load] = sc.v(M6)
                    sc.emit("vector", lambda h, d0=dout, og=ndo % 2, t1=th: h.scalar_tensor_tensor(
                        acc2[:, TOWN * d0 + 1024 * t1:TOWN * d0 + 1024 * t1 + 1024],
                        psO6[:, og, :], 1.0,
                        acc2[:, TOWN * d0 + 1024 * t1:TOWN * d0 + 1024 * t1 + 1024],
                        ALU.mult, ALU.add),
                        waits=[(M6, sc.v(M6))], inc=V6)
                    acc_stt[ndo] = sc.v(V6)
                    if last_eh and th == 0 and dout == NDC - 1:
                        v6_th0[0] = sc.v(V6)
                    ndo += 1
            gpend[e0] = sc.v(V6)
        sc.run(nc)
    st6.close()
    V6_ALL = cnt[V6]
    V6_TH0 = v6_th0[0]

    st7 = ExitStack()
    rbf7 = sb(st7, "rbf7", [128, NDC, 512], BF16)
    sqb7 = sb(st7, "sqb7", [128, NDC, 512], BF16)
    sml7 = sb(st7, "sml7", [1, 3072], F32)
    ost = sb(st7, "ost", [128, 2, NDC, 512], F32)

    sc = Sched(sems, cnt)
    M7, V7, S7, D7 = S(), S(), S(), S()
    statm7_done = {}
    with nc.psum_tensor("psM7", [1, 1024], F32) as psM7, \
         nc.psum_tensor("psB7", [128, 2, 512], F32) as psB7:
        accv = acc2[:, :].rearrange("p (d t) -> p d t", d=NDC)
        for tc in range(4):
            av = accv[:, :, 512 * tc:512 * tc + 512]
            vgate = V6_TH0 if tc < 2 else V6_ALL
            sc.emit("scalar", lambda h, a=av: h.activation(sqb7[:, :, :], a, AF.Square),
                    waits=[(V6, vgate)], inc=S7)
            sc.emit("vector", lambda h, a=av: h.tensor_copy(rbf7[:, :, :], a),
                    waits=[(M7, statm7_done.get(tc - 1, 0)), (V6, vgate)], inc=V7)
            rdy = (sc.v(S7), sc.v(V7))

            def statm7(h):
                for d0 in range(NDC):
                    h.matmul(psM7[0:1, 0:512], ones_b[:, :], rbf7[:, d0, :],
                             start=(d0 == 0), stop=(d0 == NDC - 1))
                for d0 in range(NDC):
                    ins = h.matmul(psM7[0:1, 512:1024], ones_b[:, :], sqb7[:, d0, :],
                                   start=(d0 == 0), stop=(d0 == NDC - 1))
                return ins
            sc.emit("tensor", statm7, waits=[(S7, rdy[0]), (V7, rdy[1])], inc=M7)
            statm7_done[tc] = sc.v(M7)

            def small7(h):
                h.tensor_scalar_mul(sml7[0:1, 0:512], psM7[0:1, 0:512], 1.0 / C)
                h.tensor_scalar_mul(sml7[0:1, 512:1024], psM7[0:1, 512:1024], 1.0 / C)
                h.tensor_tensor(sml7[0:1, 1024:1536], sml7[0:1, 0:512], sml7[0:1, 0:512], ALU.mult)
                h.tensor_tensor(sml7[0:1, 1536:2048], sml7[0:1, 512:1024], sml7[0:1, 1024:1536], ALU.subtract)
                h.tensor_scalar_add(sml7[0:1, 1536:2048], sml7[0:1, 1536:2048], 1e-5)
                return h.reciprocal(sml7[0:1, 2048:2560], sml7[0:1, 1536:2048])
            sc.emit("vector", small7, waits=[(M7, sc.v(M7))], inc=V7)
            sc.emit("scalar", lambda h: h.activation(sml7[0:1, 2560:3072], sml7[0:1, 2048:2560],
                                                     AF.Sqrt),
                    waits=[(V7, sc.v(V7))], inc=S7)
            sc.emit("vector", lambda h: h.tensor_tensor(
                sml7[0:1, 1024:1536], sml7[0:1, 0:512], sml7[0:1, 2560:3072], ALU.mult),
                waits=[(S7, sc.v(S7))], inc=V7)

            def bcm7(h):
                h.matmul(psB7[:, 0, :], ones1f[:, :], sml7[0:1, 2560:3072], start=True, stop=True)
                return h.matmul(psB7[:, 1, :], ones1f[:, :], sml7[0:1, 1024:1536], start=True, stop=True)
            sc.emit("tensor", bcm7, waits=[(V7, sc.v(V7))], inc=M7)
            bc7 = sc.v(M7)

            g = tc % 2
            for dout in range(NDC):
                w = [(M7, bc7)]
                if tc >= 2 and dout == 0:
                    w.append((D7, 16 * (tc - 1)))
                sc.emit("vector", lambda h, a=av, d0=dout, g0=g: h.tensor_tensor(
                    ost[:, g0, d0, :], a[:, d0, :], psB7[:, 0, :], ALU.mult),
                    waits=w, inc=V7)
                sc.emit("vector", lambda h, d0=dout, g0=g: h.tensor_tensor(
                    ost[:, g0, d0, :], ost[:, g0, d0, :], psB7[:, 1, :], ALU.subtract),
                    inc=V7)
            sc.emit("sync", lambda h, t0=tc, g0=g: h.dma_start(
                out=d_yout[:, :, 512 * t0:512 * t0 + 512], in_=ost[:, g0, :, :]),
                waits=[(V7, sc.v(V7))], inc=D7, inck=16)
        sc.emit("sync", lambda h: None, waits=[(D7, sc.v(D7))])
        sc.run(nc)
    st7.close()
    st_acc.close()

    nc.finalize()
    _CACHE["nc"] = nc
    return nc


def _host_prep(inputs):
    x = np.asarray(inputs["L0_attn_x"], np.float32)
    ac = np.ascontiguousarray
    r3 = lambda a: ac(a.reshape(NDC, 128, -1).transpose(1, 0, 2))

    cp, sp = _rope_tables()
    sel = np.zeros((2, 128), np.float32)
    sel[0, 0:64] = 1.0
    sel[1, 64:128] = 1.0

    wg = np.asarray(inputs["L0_ffn_wg"], np.float32)
    wp = np.asarray(inputs["L0_ffn_wp"], np.float32)
    wo = np.asarray(inputs["L0_ffn_wo"], np.float32)
    wgs = wg.reshape(E, NDC, 128, 16, 128)
    wps = wp.reshape(E, NDC, 128, 16, 128)
    wgp = ac(np.stack([wgs, wps], axis=1).transpose(0, 4, 3, 1, 2, 5))
    wo_h = ac(wo.reshape(E, 2, 8, 128, 8, 128).transpose(0, 1, 4, 3, 2, 5))

    common = {
        "wqdT": r3(np.asarray(inputs["L0_attn_Wq_down"], np.float32).T),
        "wkdT": r3(np.asarray(inputs["L0_attn_Wk_down"], np.float32).T),
        "wvdT": r3(np.asarray(inputs["L0_attn_Wv_down"], np.float32).T),
        "upqc": np.asarray(inputs["L0_attn_Wq_up_c"], np.float32),
        "upqe": np.asarray(inputs["L0_attn_Wq_up_e"], np.float32),
        "upqeT": ac(np.asarray(inputs["L0_attn_Wq_up_e"], np.float32).T),
        "upkc": np.asarray(inputs["L0_attn_Wk_up_c"], np.float32),
        "upke": np.asarray(inputs["L0_attn_Wk_up_e"], np.float32),
        "upkeT": ac(np.asarray(inputs["L0_attn_Wk_up_e"], np.float32).T),
        "upv": np.asarray(inputs["L0_attn_Wv_up"], np.float32),
        "Pm": _perm_matrix(),
        "wc": r3(np.asarray(inputs["L0_attn_Wc"], np.float32)),
        "rwT": r3(np.asarray(inputs["L0_ffn_router_w"], np.float32).T),
        "rb": ac(np.asarray(inputs["L0_ffn_router_b"], np.float32).reshape(8, 1)),
        "temp": ac(np.asarray(inputs["L0_ffn_temperature"], np.float32).reshape(1, 1)),
        "wgp": wgp,
        "wo": wo_h,
        "cosp": cp,
        "sinp": sp,
        "ident": np.eye(128, dtype=np.float32),
        "sel2": sel,
    }
    xts = [r3(ac(x[b].T)) for b in range(B)]
    xos = [[r3(ac(x[b].T[:, par::2])) for par in range(2)] for b in range(B)]
    cosos = [ac(cp[:, par::2]) for par in range(2)]
    sinos = [ac(sp[:, par::2]) for par in range(2)]
    kk = np.arange(128)[:, None]
    qq = np.arange(512)[None, :]
    msks = []
    for par in range(2):
        m = np.zeros((128, 8, 512), np.float32)
        for r in range(8):
            m[:, r, :] = (128 * r + kk <= 2 * qq + par)
        msks.append(m)
    soffs = [np.array([[TOWN * par]], np.int32) for par in range(2)]
    return (common, xts, xos, cosos, sinos, msks, soffs)


def kernel(**inputs):
    common, xts, xos, cosos, sinos, msks, soffs = _host_prep(inputs)
    nc = build_nc()
    in_maps = []
    for c in range(8):
        b, par = c // 2, c % 2
        m = dict(common)
        m["xT"] = xts[b]
        m["xTo"] = xos[b][par]
        m["coso"] = cosos[par]
        m["sino"] = sinos[par]
        m["msk"] = msks[par]
        m["soff"] = soffs[par]
        in_maps.append(m)
    res = run_bass_kernel_spmd(nc, in_maps, core_ids=list(range(8)))
    out = np.empty((B, T, C), np.float32)
    for c in range(8):
        b, par = c // 2, c % 2
        y = np.asarray(res.results[c]["yout"])
        out[b, par::2, :] = y.transpose(1, 0, 2).reshape(C, TOWN).T
    return out
